# version 04
# baseline (speedup 1.0000x reference)
import contextlib
import numpy as np
import ml_dtypes
import concourse.bass as bass
import concourse.mybir as mybir
from concourse.bass_utils import run_bass_kernel_spmd

F32 = mybir.dt.float32
BF16 = mybir.dt.bfloat16
AF = mybir.ActivationFunctionType
ALU = mybir.AluOpType
AX = mybir.AxisListType.X

NCORE = 8
C = 1024
NCC = 8
TOK = 4096
TA = 256
NTA = 16
TM = 512
NPASS = 4
PTOK = 1024
NE = 8
HH = 1024
CAPP1 = 5121.0
EPS = 1e-5

ENGS = ("tensor", "vector", "scalar", "sync", "gpsimd")


def build_nc():
    nc = bass.Bass("TRN2", target_bir_lowering=False, detect_race_conditions=False)

    d = {}
    d["x"] = nc.dram_tensor("x", [128, NTA, NCC, TA], F32, kind="ExternalInput").ap()
    for nm in ("wq", "wk", "wv", "wg", "wo", "w0"):
        d[nm] = nc.dram_tensor(nm, [128, NCC, C], F32, kind="ExternalInput").ap()
    d["aini"] = nc.dram_tensor("aini", [128, NCC, 8], F32, kind="ExternalInput").ap()
    d["bini"] = nc.dram_tensor("bini", [8, C], F32, kind="ExternalInput").ap()
    d["gw"] = nc.dram_tensor("gw", [128, NCC, 8], F32, kind="ExternalInput").ap()
    for nm in ("lng", "lnb", "lsg", "loglr"):
        d[nm] = nc.dram_tensor(nm, [128, NCC], F32, kind="ExternalInput").ap()
    d["gt"] = nc.dram_tensor("gt", [NE, 128, NCC, 2048], F32, kind="ExternalInput").ap()
    d["ut"] = nc.dram_tensor("ut", [NE, 128, NCC, 2048], F32, kind="ExternalInput").ap()
    d["dtw"] = nc.dram_tensor("dtw", [NE, 128, 16, C], F32, kind="ExternalInput").ap()
    d["ident"] = nc.dram_tensor("ident", [128, 128], F32, kind="ExternalInput").ap()
    d["ones512"] = nc.dram_tensor("ones512", [128, 256], F32, kind="ExternalInput").ap()
    d["onesrow"] = nc.dram_tensor("onesrow", [1, 128], F32, kind="ExternalInput").ap()
    d["onescolb"] = nc.dram_tensor("onescolb", [128, 1], BF16, kind="ExternalInput").ap()

    d["onehot"] = nc.dram_tensor("onehot", [1, 8], F32, kind="ExternalInput").ap()
    d["maskg"] = nc.dram_tensor("maskg", [8, 1], F32, kind="ExternalInput").ap()
    d["maskc"] = nc.dram_tensor("maskc", [8, 1], F32, kind="ExternalInput").ap()
    d["pos"] = nc.dram_tensor("pos", [1, TOK], F32, kind="ExternalInput").ap()

    out_d = nc.dram_tensor("out", [128, NCC, TOK], F32, kind="ExternalOutput").ap()

    KIND = "ExternalOutput" if globals().get("DBG") else None
    def scratch(nm, shp, dt):
        if KIND:
            return nc.dram_tensor(nm, shp, dt, kind=KIND).ap()
        return nc.dram_tensor(nm, shp, dt).ap()
    racc_d = scratch("racc_s", [128, NCC, TOK], F32)
    xnbf_d = scratch("xnbf_s", [128, NCC, TOK], BF16)
    ohs_d = scratch("ohs_s", [8, 2 * TOK], F32)
    w12_d = scratch("w12_s", [1, 3 * TOK], F32)
    comb_d = scratch("comb_s", [8, TOK], F32)
    rows_d = scratch("rows_s", [1, NTA * 1024], F32)
    grow_d = scratch("grow_s", [1, 1040], F32)
    prow_d = scratch("prow_s", [1, 1040], F32)
    ccin_d = nc.dram_tensor("ccin_s", [8, 1040], F32).ap()
    ccout_d = nc.dram_tensor("ccout_s", [8, 1040], F32).ap()

    ES = contextlib.ExitStack()
    with ES:
        def sbuf(name, shape, dt):
            return ES.enter_context(nc.sbuf_tensor(name, shape, dt)).ap()

        def psum(name, shape, dt=F32):
            return ES.enter_context(nc.psum_tensor(name, shape, dt)).ap()

        ABW = sbuf("ABW", [128, 5 * NCC * C], BF16)
        AFA = sbuf("AFA", [128, 7 * NCC * TA], F32)
        ABS = sbuf("ABSA", [128, 16384], BF16)
        GUA = sbuf("GUA", [128, 2 * NCC * TM], BF16)
        W0B = GUA[:, 0:NCC * C]
        FREE2 = ABS[:, 10240:16384].bitcast(F32)
        QCH = FREE2[:, 2816:3072]

        WSL = [ABW[:, j * NCC * C:(j + 1) * NCC * C] for j in range(5)]
        WQ, WK, WV, WG, WO = WSL

        def lhs_w(w, cc, dc):
            return w[:, cc * C + dc * 128: cc * C + (dc + 1) * 128]

        def afv(i):
            return AFA[:, i * NCC * TA:(i + 1) * NCC * TA]

        X0, X1, XN32, K32, V32, Y032, GRAD = (afv(i) for i in range(7))
        OUT32 = AFA[:, 0:NCC * PTOK]

        def ch(buf, cc):
            return buf[:, cc * TA:(cc + 1) * TA]

        XBF0 = ABS[:, 0:2048]
        XBF1 = ABS[:, 2048:4096]
        HB = ABS[:, 4096:6144]
        KB = ABS[:, 6144:8192]
        OG = ABS[:, 8192:10240]
        XSQ = OG
        XNQ = (ABS[:, 0:4096], ABS[:, 4096:8192])
        XNP = (ABS[:, 8192:12288], ABS[:, 12288:16384])
        GU = (GUA[:, 0:NCC * TM], GUA[:, NCC * TM:2 * NCC * TM])

        def mch(buf, cc):
            return buf[:, cc * TM:(cc + 1) * TM]

        ident = sbuf("identS", [128, 128], F32)
        ones512 = sbuf("ones512S", [128, 256], F32)
        onesrow = sbuf("onesrowS", [1, 128], F32)
        onescolb = sbuf("onescolbS", [128, 1], BF16)

        onehotS = sbuf("onehotS", [1, 8], F32)
        maskgS = sbuf("maskgS", [8, 1], F32)
        maskcS = sbuf("maskcS", [8, 1], F32)
        lngS = sbuf("lngS", [128, NCC], F32)
        lnbS = sbuf("lnbS", [128, NCC], F32)
        lsgS = sbuf("lsgS", [128, NCC], F32)
        lrS = sbuf("lrS", [128, NCC], F32)
        ainiB = sbuf("ainiB", [128, NCC * 8], BF16)
        biniB = sbuf("biniB", [8, C], BF16)
        gwB = sbuf("gwB", [128, NCC * 8], BF16)
        RT = sbuf("RT", [1, 2048], F32)
        RB = RT[0:1, 1024:2048]
        zcol = sbuf("zcol", [128, 1], F32)
        gtot = sbuf("gtot", [128, NCC], F32)
        carryg = sbuf("carryg", [128, NCC], F32)
        cnttot = sbuf("cnttot", [8, 2], F32)
        ccarry = sbuf("ccarry", [8, 2], F32)
        KAb = sbuf("KAb", [8, TA], BF16)
        lgt = sbuf("lgt", [128, 16], F32)
        pk = sbuf("pk", [128, 19], F32)
        rtmp = sbuf("rtmp", [128, 32], F32)
        rcol = sbuf("rcol", [128, 8], F32)
        ohst = FREE2[0:8, 0:512]
        ohst2 = ohst
        w12t = FREE2[0:1, 512:1280]
        keepS = FREE2[0:8, 1280:1792]
        lgsb = keepS[0:8, 0:TA]
        kwS = FREE2[0:1, 2048:2560]
        rdenT = FREE2[0:1, 2560:2816]
        combPF = sbuf("combPF", [8, 1040], F32)
        combt = combPF[0:8, 0:TA]
        combE = sbuf("combE", [1, TM], F32)
        contrib = combPF[0:8, 0:1040]
        myrow = RT[0:1, 0:1040]
        gathered = contrib
        prows = myrow
        gprefS = sbuf("gprefS", [128, NCC], F32)
        cprefS = sbuf("cprefS", [8, 2], F32)

        PB = [psum(f"pb{i}", [128, 512]) for i in range(8)]

        S = {}
        semnames = ["t", "v", "s", "cc"]
        semnames += [f"x{j}" for j in range(4)]
        semnames += [f"st{j}" for j in range(4)]
        semnames += [f"g{j}" for j in range(4)]
        for nm in semnames:
            S[nm] = ES.enter_context(nc.semaphore("sem_" + nm))
        cnt = {k: 0 for k in S}
        prog = {e: [] for e in ENGS}

        def ew(eng, sem, val=None):
            if val is None:
                if sem is None:
                    return
                sem, val = sem
            assert val <= cnt[sem], (eng, sem, val, cnt[sem])
            if val > 0:
                prog[eng].append(lambda e, h=S[sem], v=val: e.wait_ge(h, v))

        def ei(eng, fn, sem=None, n=1):
            if sem is None:
                prog[eng].append(fn)
            else:
                cnt[sem] += n
                prog[eng].append(lambda e, f=fn, h=S[sem], k=n: f(e).then_inc(h, k))

        def dma(eng, out, in_, sem, slow=False):
            ei(eng, lambda e, o=out, i=in_, sl=slow:
               e.dma_start(out=o, in_=i, allow_slow_non_contiguous=sl), sem, 16)

        def snap(sem):
            return cnt[sem]

        def snapw(sem):
            return (sem, cnt[sem])

        R_mu = RT[0:1, 0:256]
        R_e2 = RT[0:1, 256:512]; R_var = R_e2; R_bb = R_e2
        R_aa = RT[0:1, 512:768]
        R_ra = RT[0:1, 768:1024]
        R_r1 = RT[0:1, 1024:1280]; R_s2 = RT[0:1, 1280:1536]
        R_mus = RT[0:1, 1536:1792]; R_ps = RT[0:1, 1792:2048]
        RBs = (RT[0:1, 1024:2048], RT[0:1, 0:1024])
        def rbv(i, j):
            return RBs[i % 2][0:1, j * 256:(j + 1) * 256]

        dma("sync", ident[:, :], d["ident"], "x0")
        dma("sync", ones512[:, :], d["ones512"], "x0")
        dma("sync", onesrow[:, :], d["onesrow"], "x0")

        dma("sync", onehotS[:, :], d["onehot"], "x0")
        dma("sync", maskgS[:, :], d["maskg"], "x0")
        dma("sync", maskcS[:, :], d["maskc"], "x0")
        dma("sync", lngS[:, :], d["lng"], "x0")
        dma("sync", lnbS[:, :], d["lnb"], "x0")
        dma("sync", lsgS[:, :], d["lsg"], "x0")
        dma("sync", lrS[:, :], d["loglr"], "x0")
        consts_x = snapw("x0")

        dma("gpsimd", onescolb[:, :], d["onescolb"], "g0")
        for w, nm in ((WQ, "wq"), (WK, "wk"), (WV, "wv"), (WG, "wg"), (WO, "wo"), (W0B, "w0")):
            dma("gpsimd", w.rearrange("p (cc n) -> p cc n", n=C), d[nm], "g0")
        dma("gpsimd", ainiB.rearrange("p (cc n) -> p cc n", n=8), d["aini"], "g0")
        dma("gpsimd", biniB[:, :], d["bini"], "g0")
        dma("gpsimd", gwB.rearrange("p (cc n) -> p cc n", n=8), d["gw"], "g0")
        preload_g = snapw("g0")

        ew("scalar", consts_x)
        ei("scalar", lambda e: e.activation(lrS[:, :], lrS[:, :], AF.Exp), "s")
        ew("vector", "s", snap("s"))
        ew("vector", consts_x)
        ei("vector", lambda e: e.tensor_tensor(lrS[:, :], lrS[:, :], ones512[:, 0:NCC], ALU.min))
        ei("vector", lambda e: e.memset(zcol[:, :], 0.0))
        ei("vector", lambda e: e.memset(gtot[:, :], 0.0))
        ei("vector", lambda e: e.memset(cnttot[:, :], 0.0))
        ei("vector", lambda e: e.memset(ccarry[:, :], 0.0), "v")
        prelude_v = snap("v")

        def vd(fn):
            ei("vector", fn, "v")
            ew("vector", "v", snap("v"))

        def bcast_row(row, bank_ap, mark=None):
            ei("tensor",
               lambda e, r=row, o=bank_ap: e.matmul(o, onesrow[:, :], r, start=True, stop=True),
               mark)

        vfree = {}

        ew("tensor", preload_g)
        ew("tensor", consts_x)
        ew("vector", "v", prelude_v)
        ei("tensor", lambda e: e.nop(), "t")

        store_wm = {}
        for i in range(NTA):
            b = i % 2
            sx = f"x{i % 4}"
            sg = f"g{i % 4}"
            sst = f"st{i % 4}"
            X = X0 if b == 0 else X1
            XB = XBF0 if b == 0 else XBF1
            ew("sync", "v", vfree.get(("a1x", i - 2), 0))
            dma("sync", X.rearrange("p (cc n) -> p cc n", n=TA), d["x"][:, i, :, :], sx)
            dma("sync", RT[0:1, 1792:2048], d["pos"][0:1, i * TA:(i + 1) * TA], sx)
            x_x = snapw(sx)
            ew("gpsimd", "v", vfree.get(("a1x", i - 2), 0))
            ew("gpsimd", "t", vfree.get(("a1xbt", i - 2), 0))
            dma("gpsimd", XB.rearrange("p (cc n) -> p cc n", n=TA), d["x"][:, i, :, :], sg)
            xb_g = snapw(sg)

            ew("vector", store_wm.get(i - 1))
            ew("scalar", xb_g)
            ew("scalar", "v", vfree.get(("a1rows", i - 1), 0))
            ew("scalar", x_x)
            ew("scalar", store_wm.get(i - 1))
            ei("scalar", lambda e, xb=XB: e.activation(XSQ[:, :], xb[:, :], AF.Square), "s")
            ei("scalar", lambda e: e.activation(R_ps, R_ps, AF.Ln), "s")
            xsq_s = snap("s")
            ew("tensor", "s", xsq_s)
            ew("tensor", "v", vfree.get(("a1pb0", i - 1), 0))
            for src, c0 in ((XB, 0), (XSQ, TA)):
                for cc in range(NCC):
                    ei("tensor",
                       lambda e, r=src[:, cc * TA:(cc + 1) * TA], o=PB[0][0:1, c0:c0 + TA],
                       st=(cc == 0), sp=(cc == NCC - 1):
                       e.matmul(o, onescolb[:, :], r, start=st, stop=sp),
                       "t" if (cc == NCC - 1) else None)
            stats_t = snap("t")

            ew("vector", "t", stats_t)
            vd(lambda e: e.tensor_scalar_mul(R_mu, PB[0][0:1, 0:TA], 1.0 / C))
            vd(lambda e: e.tensor_scalar_mul(R_e2, PB[0][0:1, TA:2 * TA], 1.0 / C))
            vfree[("a1pb0", i)] = snap("v")
            vd(lambda e: e.tensor_mul(R_aa, R_mu, R_mu))
            vd(lambda e: e.tensor_sub(R_var, R_e2, R_aa))
            vd(lambda e: e.tensor_scalar_add(R_aa, R_var, EPS))
            vd(lambda e: e.reciprocal(R_ra, R_aa))
            vd(lambda e: e.tensor_scalar(R_ps, R_ps, 0.1, 1.0, ALU.mult, ALU.add))
            vd(lambda e: e.reciprocal(R_ps, R_ps))
            st_v1 = snap("v")
            ew("scalar", "v", st_v1)
            ei("scalar", lambda e: e.activation(R_r1, R_aa, AF.Ln), "s")
            ew("vector", "s", snap("s"))
            vd(lambda e: e.tensor_mul(R_bb, R_var, R_ra))
            vd(lambda e: e.tensor_scalar_add(R_bb, R_bb, EPS))
            ew("scalar", "v", snap("v"))
            ei("scalar", lambda e: e.activation(R_r1, R_r1, AF.Exp, scale=-0.5), "s")
            ei("scalar", lambda e: e.activation(R_s2, R_bb, AF.Ln), "s")
            ei("scalar", lambda e: e.activation(R_s2, R_s2, AF.Exp, scale=-0.5), "s")
            rows_s = snap("s")
            ew("vector", "s", rows_s)
            vd(lambda e: e.tensor_mul(R_mus, R_mu, R_r1))
            rows_v = snap("v")
            ew("sync", "v", rows_v)
            dma("sync", rows_d[0:1, i * 1024:(i + 1) * 1024], RT[0:1, 1024:2048], sst)

            ew("tensor", "v", rows_v)
            ew("tensor", "v", vfree.get(("a1pb", i - 1), 0))
            bcast_row(R_r1, PB[1][:, 0:TA])
            bcast_row(R_mus, PB[1][:, TA:2 * TA])
            bcast_row(R_s2, PB[2][:, 0:TA], "t")
            bc_t = snap("t")

            ew("vector", "t", bc_t)
            ew("vector", x_x)
            for cc in range(NCC):
                ei("vector", lambda e, c=cc, xx=X: e.tensor_mul(ch(XN32, c), ch(xx, c), PB[1][:, 0:TA]))
            for cc in range(NCC):
                ei("vector", lambda e, c=cc: e.tensor_sub(ch(XN32, c), ch(XN32, c), PB[1][:, TA:2 * TA]))
            ei("vector", lambda e, xb=XB: e.tensor_copy(xb[:, :], XN32[:, :]))
            for cc in range(NCC):
                ei("vector", lambda e, c=cc: e.tensor_mul(ch(GRAD, c), ch(XN32, c), PB[2][:, 0:TA]),
                   "v" if cc == NCC - 1 else None)
            h32_v = snap("v")
            vfree[("a1rows", i)] = h32_v
            vfree[("a1pb", i)] = h32_v
            ew("scalar", "v", h32_v)
            for cc in range(NCC):
                ei("scalar",
                   lambda e, c=cc: e.activation(HB[:, c * TA:(c + 1) * TA], ch(GRAD, c), AF.Identity,
                                                bias=lnbS[:, c:c + 1], scale=lngS[:, c:c + 1]),
                   "s" if cc == NCC - 1 else None)
            hb_s = snap("s")

            ew("tensor", "s", hb_s)
            bank_free = {3: ("t", 0), 4: ("t", 0)}
            for w, kind in ((WK, "K"), (WV, "V")):
                for dc in range(NCC):
                    bi = 3 + (dc % 2)
                    which, val = bank_free[bi]
                    ew("tensor", which, val)
                    for cc in range(NCC):
                        ei("tensor",
                           lambda e, l=lhs_w(w, cc, dc), r=HB[:, cc * TA:(cc + 1) * TA],
                           o=PB[bi][:, 0:TA], st=(cc == 0), sp=(cc == NCC - 1):
                           e.matmul(o, l, r, start=st, stop=sp),
                           "t" if cc == NCC - 1 else None)
                    mm_t = snap("t")
                    ew("vector", "t", mm_t)
                    if kind == "K":
                        ei("vector", lambda e, c=dc, b2=bi: e.tensor_copy(ch(K32, c), PB[b2][:, 0:TA]))
                        ei("vector", lambda e, c=dc, b2=bi: e.tensor_copy(KB[:, c * TA:(c + 1) * TA], PB[b2][:, 0:TA]), "v")
                    else:
                        ei("vector", lambda e, c=dc, b2=bi: e.tensor_copy(ch(V32, c), PB[b2][:, 0:TA]), "v")
                    bank_free[bi] = ("v", snap("v"))
            kv_v = snap("v")

            for cc in range(NCC):
                ei("tensor",
                   lambda e, l=gwB[:, cc * 8:(cc + 1) * 8], r=XB[:, cc * TA:(cc + 1) * TA],
                   o=PB[5][0:8, TA:2 * TA], st=(cc == 0), sp=(cc == NCC - 1):
                   e.matmul(o, l, r, start=st, stop=sp),
                   "t" if cc == NCC - 1 else None)
            lg_t = snap("t")

            ew("tensor", "v", kv_v)
            for cc in range(NCC):
                ei("tensor",
                   lambda e, l=ainiB[:, cc * 8:(cc + 1) * 8], r=KB[:, cc * TA:(cc + 1) * TA],
                   o=PB[5][0:8, 0:TA], st=(cc == 0), sp=(cc == NCC - 1):
                   e.matmul(o, l, r, start=st, stop=sp),
                   "t" if cc == NCC - 1 else None)
            ka_t = snap("t")
            ew("vector", "t", ka_t)
            ei("vector", lambda e: e.tensor_copy(KAb[:, :], PB[5][0:8, 0:TA]))
            ei("vector", lambda e: e.tensor_copy(lgsb[:, :], PB[5][0:8, TA:2 * TA]), "v")
            kab_v = snap("v")

            ew("tensor", "v", kab_v)
            for dc in range(NCC):
                bi = 3 + (dc % 2)
                which, val = bank_free[bi]
                ew("tensor", which, val)
                for cc in range(NCC):
                    ei("tensor",
                       lambda e, l=lhs_w(W0B, cc, dc), r=KB[:, cc * TA:(cc + 1) * TA],
                       o=PB[bi][:, 0:TA], st=(cc == 0):
                       e.matmul(o, l, r, start=st, stop=False))
                ei("tensor",
                   lambda e, l=biniB[:, dc * 128:(dc + 1) * 128], o=PB[bi][:, 0:TA]:
                   e.matmul(o, l, KAb[:, :], start=False, stop=True), "t")
                y_t = snap("t")
                ew("vector", "t", y_t)
                ei("vector", lambda e, c=dc, b2=bi: e.tensor_sub(ch(GRAD, c), PB[b2][:, 0:TA], ch(V32, c)))
                ei("vector", lambda e, c=dc: e.tensor_mul(ch(GRAD, c), ch(GRAD, c), ch(K32, c)))
                ei("vector", lambda e, c=dc: e.tensor_reduce(rcol[:, 0:1], ch(GRAD, c), AX, ALU.add))
                ei("vector", lambda e, c=dc: e.tensor_add(gtot[:, c:c + 1], gtot[:, c:c + 1], rcol[:, 0:1]), "v")
                bank_free[bi] = ("v", snap("v"))

            ew("vector", "t", lg_t)
            ew("tensor", "v", snap("v"))
            for hf in range(2):
                ei("tensor",
                   lambda e, h2=hf, r=lgsb[0:8, 0:128] if hf == 0 else lgsb[0:8, 128:256],
                   o=PB[6][0:128, 140 + hf * 8:148 + hf * 8]:
                   e.transpose(o, r, ident[0:8, 0:8]),
                   "t" if hf == 1 else None)
            tp_t = snap("t")
            ew("vector", "t", tp_t)
            vd(lambda e: e.tensor_copy(lgt[:, 0:16], PB[6][0:128, 140:156]))
            routing_tr_t = 0
            for hf in range(2):
                lg8 = lgt[:, hf * 8:(hf + 1) * 8]
                m1 = pk[:, 16:17]; m2 = pk[:, 17:18]; ssum = pk[:, 18:19]
                nm = rcol[:, 1:2]
                prb = rtmp[:, 0:8]; ohc = rtmp[:, 8:16]
                eq1 = rtmp[:, 16:24]; pr2 = rtmp[:, 24:32]
                oh1 = pk[:, 0:8]; oh2 = pk[:, 8:16]
                if hf == 1:
                    ew("vector", "t", routing_tr_t)
                vd(lambda e, a=lg8, m=m1: e.reduce_max(m, a, AX))
                vd(lambda e, m=m1, n=nm: e.tensor_sub(n, zcol[:, :], m))
                ew("scalar", "v", snap("v"))
                ei("scalar", lambda e, a=lg8, p=prb, n=nm: e.activation(p, a, AF.Exp, bias=n), "s")
                ew("vector", "s", snap("s"))
                vd(lambda e, p=prb, s_=ssum: e.tensor_reduce(s_, p, AX, ALU.add))
                vd(lambda e, p=prb, m=m1: e.reduce_max(m, p, AX))
                vd(lambda e, p=prb, m=m1, o=oh1: e.tensor_scalar(o, p, m, 0.0, ALU.subtract, ALU.is_ge))
                vd(lambda e, o=oh1, c2=ohc: e.tensor_tensor_scan(c2, ones512[:, 0:8], o, 0.0, ALU.mult, ALU.add))
                vd(lambda e, c2=ohc, q=eq1: e.tensor_scalar(q, c2, 1.0, 0.0, ALU.subtract, ALU.is_equal))
                vd(lambda e, o=oh1, q=eq1: e.tensor_mul(o, o, q))
                vd(lambda e, o=oh1, p=prb, p2=pr2: e.scalar_tensor_tensor(p2, o, -1e30, p, ALU.mult, ALU.add))
                vd(lambda e, p2=pr2, m=m2: e.reduce_max(m, p2, AX))
                vd(lambda e, p2=pr2, m=m2, o=oh2: e.tensor_scalar(o, p2, m, 0.0, ALU.subtract, ALU.is_ge))
                vd(lambda e, o=oh2, c2=ohc: e.tensor_tensor_scan(c2, ones512[:, 0:8], o, 0.0, ALU.mult, ALU.add))
                vd(lambda e, c2=ohc, q=eq1: e.tensor_scalar(q, c2, 1.0, 0.0, ALU.subtract, ALU.is_equal))
                vd(lambda e, o=oh2, q=eq1: e.tensor_mul(o, o, q))
                pk_v = snap("v")
                ew("tensor", "v", pk_v)
                ei("tensor", lambda e: e.transpose(PB[6][0:8, 0:128], pk[:, 0:8], ident[:, :]))
                ei("tensor", lambda e: e.transpose(PB[6][0:8, 128:256], pk[:, 8:16], ident[:, :]))
                ei("tensor", lambda e: e.transpose(PB[6][0:1, 256:384], pk[:, 16:17], ident[:, :]))
                ei("tensor", lambda e: e.transpose(PB[6][0:1, 384:512], pk[:, 17:18], ident[:, :]))
                ei("tensor", lambda e: e.transpose(PB[7][0:1, 0:128], pk[:, 18:19], ident[:, :]), "t")
                tr_t = snap("t")
                ew("vector", "t", tr_t)
                ei("vector",
                   lambda e, h2=hf: e.tensor_copy(ohst[:, h2 * 128:(h2 + 1) * 128],
                                                  PB[6][0:8, 0:128]))
                ei("vector",
                   lambda e, h2=hf: e.tensor_copy(ohst[:, TA + h2 * 128:TA + (h2 + 1) * 128],
                                                  PB[6][0:8, 128:256]))
                ei("vector",
                   lambda e, h2=hf: e.tensor_copy(w12t[0:1, h2 * 128:(h2 + 1) * 128],
                                                  PB[6][0:1, 256:384]))
                ei("vector",
                   lambda e, h2=hf: e.tensor_copy(w12t[0:1, TA + h2 * 128:TA + (h2 + 1) * 128],
                                                  PB[6][0:1, 384:512]))
                ei("vector",
                   lambda e, h2=hf: e.tensor_copy(w12t[0:1, 2 * TA + h2 * 128:2 * TA + (h2 + 1) * 128],
                                                  PB[7][0:1, 0:128]), "v")
                if hf == 0:
                    routing_tr_t = tr_t
            vd(lambda e: e.tensor_reduce(rcol[0:8, 0:1], ohst[0:8, 0:TA], AX, ALU.add))
            vd(lambda e: e.tensor_add(cnttot[:, 0:1], cnttot[:, 0:1], rcol[0:8, 0:1]))
            vd(lambda e: e.tensor_reduce(rcol[0:8, 0:1], ohst[0:8, TA:2 * TA], AX, ALU.add))
            vd(lambda e: e.tensor_add(cnttot[:, 1:2], cnttot[:, 1:2], rcol[0:8, 0:1]))
            oh_v = snap("v")
            ew("sync", "v", oh_v)
            dma("sync", ohs_d[:, 2 * i * TA:2 * (i + 1) * TA], ohst[:, :], sst)
            dma("sync", w12_d[0:1, 3 * i * TA:3 * (i + 1) * TA], w12t[:, :], sst)

            store_wm[i] = snapw(sst)
            vfree[("a1x", i)] = snap("v")
            vfree[("a1xbt", i)] = snap("t")

        a1_t, a1_v, a1_s = snap("t"), snap("v"), snap("s")
        ew("sync", "v", a1_v)
        dma("sync", grow_d[0:1, 0:1024].rearrange("o (cc p) -> (o p) cc", p=128), gtot[:, :], "st0", slow=True)
        dma("sync", grow_d[0:1, 1024:1040].rearrange("o (u k) -> (o k) u", u=2), cnttot[:, :], "st0", slow=True)
        ew("sync", snapw("st0"))
        dma("sync", myrow[:, :], grow_d[:, :], "x0")
        myrow_x = snapw("x0")
        ew("tensor", myrow_x)
        ew("tensor", "v", a1_v)
        ei("tensor", lambda e: e.matmul(PB[6][0:8, 0:512], onehotS[:, :], myrow[0:1, 0:512], start=True, stop=True))
        ei("tensor", lambda e: e.matmul(PB[7][0:8, 0:512], onehotS[:, :], myrow[0:1, 512:1024], start=True, stop=True))
        ei("tensor", lambda e: e.matmul(PB[5][0:8, 0:16], onehotS[:, :], myrow[0:1, 1024:1040], start=True, stop=True), "t")
        ew("vector", "t", snap("t"))
        ei("vector", lambda e: e.tensor_copy(contrib[:, 0:512], PB[6][0:8, 0:512]))
        ei("vector", lambda e: e.tensor_copy(contrib[:, 512:1024], PB[7][0:8, 0:512]))
        ei("vector", lambda e: e.tensor_copy(contrib[:, 1024:1040], PB[5][0:8, 0:16]), "v")
        ew("gpsimd", "v", snap("v"))
        dma("gpsimd", ccin_d[:, :], contrib[:, :], "g0")
        ew("gpsimd", snapw("g0"))
        ei("gpsimd", lambda e: e.collective_compute(
            "AllReduce", ALU.add, replica_groups=[list(range(NCORE))],
            ins=[ccin_d], outs=[ccout_d]), "cc")
        ew("gpsimd", "cc", snap("cc"))
        dma("gpsimd", gathered[:, :], ccout_d[:, :], "g1")
        gath_g = snapw("g1")
        ew("tensor", gath_g)
        ei("tensor", lambda e: e.matmul(PB[6][0:1, 0:512], maskgS[:, :], gathered[:, 0:512], start=True, stop=True))
        ei("tensor", lambda e: e.matmul(PB[7][0:1, 0:512], maskgS[:, :], gathered[:, 512:1024], start=True, stop=True))
        ei("tensor", lambda e: e.matmul(PB[5][0:1, 0:16], maskcS[:, :], gathered[:, 1024:1040], start=True, stop=True), "t")
        ew("vector", "t", snap("t"))
        ei("vector", lambda e: e.tensor_copy(prows[0:1, 0:512], PB[6][0:1, 0:512]))
        ei("vector", lambda e: e.tensor_copy(prows[0:1, 512:1024], PB[7][0:1, 0:512]))
        ei("vector", lambda e: e.tensor_copy(prows[0:1, 1024:1040], PB[5][0:1, 0:16]), "v")
        ew("sync", "v", snap("v"))
        dma("sync", prow_d[:, :], prows[:, :], "st1")
        ew("sync", snapw("st1"))
        dma("sync", gprefS[:, :], prow_d[0:1, 0:1024].rearrange("o (cc p) -> (o p) cc", p=128), "x1", slow=True)
        dma("sync", cprefS[:, :], prow_d[0:1, 1024:1040].rearrange("o (u k) -> (o k) u", u=2), "x1", slow=True)
        pref_x = snapw("x1")
        ew("vector", pref_x)
        ei("vector", lambda e: e.tensor_copy(carryg[:, :], gprefS[:, :]))
        ei("vector", lambda e: e.tensor_copy(ccarry[:, :], cprefS[:, :]), "v")
        exch_v = snap("v")

        ew("tensor", "v", exch_v)
        ei("tensor", lambda e: e.nop(), "t")
        qch_guard = {0: ("s", 0), 1: ("s", 0)}
        a2store_wm = {}
        a2bc_wm = {}
        for i in range(NTA):
            b = i % 2
            sx = f"x{i % 4}"
            sst = f"st{i % 4}"
            X = X0 if b == 0 else X1
            XB = XBF0 if b == 0 else XBF1
            if i < 2:
                ew("sync", "v", exch_v)
            else:
                ew("sync", "v", vfree.get(("a2x", i - 2), 0))
            dma("sync", X.rearrange("p (cc n) -> p cc n", n=TA), d["x"][:, i, :, :], sx)
            ew("sync", "t", a2bc_wm.get(i - 2, 0))
            dma("sync", RBs[i % 2][:, :], rows_d[0:1, i * 1024:(i + 1) * 1024], sx)
            ew("sync", "v", vfree.get(("a2x", i - 1), 0))
            dma("sync", ohst2[:, :], ohs_d[:, 2 * i * TA:2 * (i + 1) * TA], sx)
            dma("sync", w12t[:, :], w12_d[0:1, 3 * i * TA:3 * (i + 1) * TA], sx)
            x_x = snapw(sx)

            ew("vector", a2store_wm.get(i - 1))
            ew("tensor", x_x)
            ew("tensor", "v", vfree.get(("a2pb1", i - 1), 0))
            bcast_row(rbv(i, 0), PB[1][:, 0:TA])
            bcast_row(rbv(i, 2), PB[1][:, TA:2 * TA])
            ew("tensor", "v", vfree.get(("a2pb2", i - 1), 0))
            bcast_row(rbv(i, 1), PB[2][:, 0:TA])
            bcast_row(rbv(i, 3), PB[2][:, TA:2 * TA], "t")
            bc_t = snap("t")
            a2bc_wm[i] = bc_t

            ew("vector", "t", bc_t)
            ew("vector", x_x)
            for cc in range(NCC):
                ei("vector", lambda e, c=cc, xx=X: e.tensor_mul(ch(XN32, c), ch(xx, c), PB[1][:, 0:TA]))
            for cc in range(NCC):
                ei("vector", lambda e, c=cc: e.tensor_sub(ch(XN32, c), ch(XN32, c), PB[1][:, TA:2 * TA]))
            ei("vector", lambda e, xb=XB: e.tensor_copy(xb[:, :], XN32[:, :]), "v")
            xnbf_v = snap("v")
            ew("sync", "v", xnbf_v)
            dma("sync", xnbf_d[:, :, i * TA:(i + 1) * TA],
                XB.rearrange("p (cc n) -> p cc n", n=TA), sst)
            for cc in range(NCC):
                ei("vector", lambda e, c=cc: e.tensor_mul(ch(GRAD, c), ch(XN32, c), PB[2][:, 0:TA]),
                   "v" if cc == NCC - 1 else None)
            h32_v = snap("v")
            vfree[("a2pb1", i)] = h32_v
            ew("scalar", "v", h32_v)
            for cc in range(NCC):
                ei("scalar",
                   lambda e, c=cc: e.activation(HB[:, c * TA:(c + 1) * TA], ch(GRAD, c), AF.Identity,
                                                bias=lnbS[:, c:c + 1], scale=lngS[:, c:c + 1]),
                   "s" if cc == NCC - 1 else None)
            hb_s = snap("s")

            ew("tensor", "s", hb_s)
            bank_free = {3: ("t", 0), 4: ("t", 0)}
            for w, kind in ((WK, "K"), (WV, "V")):
                for dc in range(NCC):
                    bi = 3 + (dc % 2)
                    which, val = bank_free[bi]
                    ew("tensor", which, val)
                    for cc in range(NCC):
                        ei("tensor",
                           lambda e, l=lhs_w(w, cc, dc), r=HB[:, cc * TA:(cc + 1) * TA],
                           o=PB[bi][:, 0:TA], st=(cc == 0), sp=(cc == NCC - 1):
                           e.matmul(o, l, r, start=st, stop=sp),
                           "t" if cc == NCC - 1 else None)
                    mm_t = snap("t")
                    ew("vector", "t", mm_t)
                    if kind == "K":
                        ei("vector", lambda e, c=dc, b2=bi: e.tensor_copy(ch(K32, c), PB[b2][:, 0:TA]))
                        ei("vector", lambda e, c=dc, b2=bi: e.tensor_copy(KB[:, c * TA:(c + 1) * TA], PB[b2][:, 0:TA]), "v")
                    else:
                        ei("vector", lambda e, c=dc, b2=bi: e.tensor_copy(ch(V32, c), PB[b2][:, 0:TA]), "v")
                    bank_free[bi] = ("v", snap("v"))
            kv_v = snap("v")

            ew("tensor", "v", kv_v)
            for cc in range(NCC):
                ei("tensor",
                   lambda e, l=ainiB[:, cc * 8:(cc + 1) * 8], r=KB[:, cc * TA:(cc + 1) * TA],
                   o=PB[5][0:8, 0:TA], st=(cc == 0), sp=(cc == NCC - 1):
                   e.matmul(o, l, r, start=st, stop=sp),
                   "t" if cc == NCC - 1 else None)
            ka_t = snap("t")
            ew("vector", "t", ka_t)
            ei("vector", lambda e: e.tensor_copy(KAb[:, :], PB[5][0:8, 0:TA]), "v")
            kab_v = snap("v")

            ew("tensor", "v", kab_v)
            for dc in range(NCC):
                bi = 3 + (dc % 2)
                which, val = bank_free[bi]
                ew("tensor", which, val)
                for cc in range(NCC):
                    ei("tensor",
                       lambda e, l=lhs_w(W0B, cc, dc), r=KB[:, cc * TA:(cc + 1) * TA],
                       o=PB[bi][:, 0:TA], st=(cc == 0):
                       e.matmul(o, l, r, start=st, stop=False))
                ei("tensor",
                   lambda e, l=biniB[:, dc * 128:(dc + 1) * 128], o=PB[bi][:, 0:TA]:
                   e.matmul(o, l, KAb[:, :], start=False, stop=True), "t")
                y_t = snap("t")
                ew("vector", "t", y_t)
                ei("vector", lambda e, c=dc, b2=bi: e.tensor_copy(ch(Y032, c), PB[b2][:, 0:TA]))
                ei("vector", lambda e, c=dc, b2=bi: e.tensor_sub(ch(GRAD, c), PB[b2][:, 0:TA], ch(V32, c)))
                ei("vector", lambda e, c=dc: e.tensor_mul(ch(GRAD, c), ch(GRAD, c), ch(K32, c)), "v")
                bank_free[bi] = ("v", snap("v"))

            for cc in range(NCC):
                vd(lambda e, c=cc: e.tensor_tensor_scan(ch(V32, c), ones512[:, 0:TA], ch(GRAD, c),
                                                        carryg[:, c:c + 1], ALU.mult, ALU.add))
                vd(lambda e, c=cc: e.tensor_copy(carryg[:, c:c + 1], ch(V32, c)[:, TA - 1:TA]))
                ei("vector",
                   lambda e, c=cc: e.scalar_tensor_tensor(ch(GRAD, c), ch(V32, c), 1.0, ch(GRAD, c),
                                                          ALU.mult, ALU.subtract),
                   "v" if cc == NCC - 1 else None)
            scan_v = snap("v")

            ew("tensor", "v", scan_v)
            for dc in range(NCC):
                bi = 3 + (dc % 2)
                which, val = bank_free[bi]
                ew("tensor", which, val)
                for cc in range(NCC):
                    ei("tensor",
                       lambda e, l=lhs_w(WG, cc, dc), r=XB[:, cc * TA:(cc + 1) * TA],
                       o=PB[bi][:, 0:TA], st=(cc == 0), sp=(cc == NCC - 1):
                       e.matmul(o, l, r, start=st, stop=sp),
                       "t" if cc == NCC - 1 else None)
                mm_t = snap("t")
                ew("scalar", "t", mm_t)
                ei("scalar", lambda e, c=dc, b2=bi: e.activation(ch(K32, c), PB[b2][:, 0:TA], AF.Sigmoid), "s")
                bank_free[bi] = ("s", snap("s"))
            gate_s = snap("s")

            ew("vector", "s", gate_s)
            for dc in range(NCC):
                bi = 3 + (dc % 2)
                which, val = bank_free[bi]
                ew("tensor", which, val)
                for cc in range(NCC):
                    ei("tensor",
                       lambda e, l=lhs_w(WQ, cc, dc), r=HB[:, cc * TA:(cc + 1) * TA],
                       o=PB[bi][:, 0:TA], st=(cc == 0), sp=(cc == NCC - 1):
                       e.matmul(o, l, r, start=st, stop=sp),
                       "t" if cc == NCC - 1 else None)
                mm_t = snap("t")
                q = QCH[:, 0:TA]
                which2, val2 = qch_guard[0]
                ew("scalar", which2, val2)
                ew("scalar", "t", mm_t)
                ei("scalar", lambda e, c=dc, b2=bi, q2=q: e.activation(q2, PB[b2][:, 0:TA], AF.Identity,
                                                                      scale=lrS[:, c:c + 1]), "s")
                q_s = snap("s")
                bank_free[bi] = ("s", q_s)
                ew("vector", "s", q_s)
                ei("vector", lambda e, q2=q: e.tensor_mul(q2, q2, PB[2][:, TA:2 * TA]))
                ei("vector", lambda e, c=dc, q2=q: e.tensor_mul(q2, q2, ch(GRAD, c)))
                ei("vector", lambda e, c=dc, q2=q: e.scalar_tensor_tensor(ch(Y032, c), q2, -1.0,
                                                                          ch(Y032, c), ALU.mult, ALU.add))
                ei("vector", lambda e, c=dc: e.tensor_mul(OG[:, c * TA:(c + 1) * TA], ch(Y032, c), ch(K32, c)), "v")
                qch_guard[0] = ("v", snap("v"))
            og_v = snap("v")
            vfree[("a2pb2", i)] = og_v

            ew("tensor", "v", og_v)
            for dc in range(NCC):
                bi = 3 + (dc % 2)
                which, val = bank_free[bi]
                ew("tensor", which, val)
                for cc in range(NCC):
                    ei("tensor",
                       lambda e, l=lhs_w(WO, cc, dc), r=OG[:, cc * TA:(cc + 1) * TA],
                       o=PB[bi][:, 0:TA], st=(cc == 0), sp=(cc == NCC - 1):
                       e.matmul(o, l, r, start=st, stop=sp),
                       "t" if cc == NCC - 1 else None)
                mm_t = snap("t")
                ew("scalar", "t", mm_t)
                ei("scalar", lambda e, c=dc, b2=bi: e.activation(ch(Y032, c), PB[b2][:, 0:TA], AF.Identity,
                                                                 scale=lsgS[:, c:c + 1]), "s")
                bank_free[bi] = ("s", snap("s"))
            wo_s = snap("s")
            ew("vector", "s", wo_s)
            ei("vector", lambda e, xx=X: e.tensor_add(Y032[:, :], Y032[:, :], xx[:, :]))
            ei("vector", lambda e: e.tensor_add(Y032[:, :], Y032[:, :], XN32[:, :]), "v")
            racc_v = snap("v")
            ew("sync", "v", racc_v)
            dma("sync", racc_d[:, :, i * TA:(i + 1) * TA],
                Y032.rearrange("p (cc n) -> p cc n", n=TA), sst)

            ew("vector", x_x)
            for k2 in range(2):
                sl = slice(k2 * TA, (k2 + 1) * TA)
                vd(lambda e, s2_=sl, k3=k2: e.tensor_tensor_scan(keepS[:, s2_], ones512[0:8, 0:TA],
                                                                 ohst2[:, s2_], ccarry[:, k3:k3 + 1],
                                                                 ALU.mult, ALU.add))
                vd(lambda e, k3=k2: e.tensor_copy(
                    ccarry[:, k3:k3 + 1], keepS[:, (k3 + 1) * TA - 1:(k3 + 1) * TA]))
                vd(lambda e, s2_=sl: e.tensor_scalar(keepS[:, s2_], keepS[:, s2_],
                                                     CAPP1, 0.0, ALU.subtract, ALU.is_lt))
                vd(lambda e, s2_=sl: e.tensor_mul(keepS[:, s2_], keepS[:, s2_],
                                                  ohst2[:, s2_]))
            keep_v = snap("v")
            ew("tensor", "v", keep_v)
            ei("tensor", lambda e: e.matmul(PB[5][0:1, 0:TA], ones512[0:8, 0:1],
                                            keepS[:, 0:TA], start=True, stop=True))
            ei("tensor", lambda e: e.matmul(PB[5][0:1, TA:2 * TA], ones512[0:8, 0:1],
                                            keepS[:, TA:2 * TA], start=True, stop=True), "t")
            sel_t = snap("t")
            ew("vector", "t", sel_t)
            vd(lambda e: e.tensor_add(rdenT[:, :], w12t[0:1, 0:TA], w12t[0:1, TA:2 * TA]))
            vd(lambda e: e.scalar_tensor_tensor(rdenT[:, :], w12t[0:1, 2 * TA:3 * TA], 1e-9,
                                                rdenT[:, :], ALU.mult, ALU.add))
            vd(lambda e: e.reciprocal(rdenT[:, :], rdenT[:, :]))
            vd(lambda e: e.tensor_mul(kwS[0:1, 0:TA], PB[5][0:1, 0:TA], w12t[0:1, 0:TA]))
            vd(lambda e: e.tensor_mul(kwS[0:1, TA:2 * TA], PB[5][0:1, TA:2 * TA], w12t[0:1, TA:2 * TA]))
            vd(lambda e: e.tensor_mul(kwS[0:1, 0:TA], kwS[0:1, 0:TA], rdenT[:, :]))
            vd(lambda e: e.tensor_mul(kwS[0:1, TA:2 * TA], kwS[0:1, TA:2 * TA], rdenT[:, :]))
            kw_v = snap("v")
            ew("tensor", "v", kw_v)
            ei("tensor",
               lambda e: e.matmul(PB[7][0:8, 0:TA], onesrow[0:1, 0:8], kwS[0:1, 0:TA],
                                  start=True, stop=True))
            ei("tensor",
               lambda e: e.matmul(PB[7][0:8, TA:2 * TA], onesrow[0:1, 0:8], kwS[0:1, TA:2 * TA],
                                  start=True, stop=True), "t")
            kwb_t = snap("t")
            ew("vector", "t", kwb_t)
            vd(lambda e: e.tensor_mul(combt[:, :], ohst2[:, 0:TA], PB[7][0:8, 0:TA]))
            vd(lambda e: e.tensor_mul(keepS[:, 0:TA], ohst2[:, TA:2 * TA], PB[7][0:8, TA:2 * TA]))
            vd(lambda e: e.tensor_add(combt[:, :], combt[:, :], keepS[:, 0:TA]))
            comb_v = snap("v")
            ew("sync", "v", comb_v)
            dma("sync", comb_d[:, i * TA:(i + 1) * TA], combt[:, :], sst)

            a2store_wm[i] = snapw(sst)
            vfree[("a2x", i)] = snap("v")
            vfree[("a2xt", i)] = snap("t")

        a2_t, a2_v, a2_s = snap("t"), snap("v"), snap("s")
        a2_st = [snapw(f"st{j}") for j in range(4)]
        ew("tensor", "v", a2_v)
        ew("tensor", "s", a2_s)
        ei("tensor", lambda e: e.nop(), "t")
        moe_barrier_t = snap("t")
        ew("vector", "t", moe_barrier_t)
        ew("scalar", "t", moe_barrier_t)
        ew("gpsimd", "t", moe_barrier_t)
        ew("sync", "t", moe_barrier_t)
        for wm in a2_st:
            ew("sync", wm)

        slot_free = {j: ("t", moe_barrier_t) for j in range(5)}
        gu_bank_guard = {0: ("s", 0), 1: ("s", 0)}
        u_bank_guard = {0: ("v", 0), 1: ("v", 0)}
        y_bank_guard = {0: ("v", 0), 1: ("v", 0)}
        xnp_guard = 0
        combE_guard = [0]
        xnq_guard = 0
        gu_read_guard = {0: 0, 1: 0}
        out_store_st = None
        prev_pass_v = a2_v

        def emit_w_load(p, ve):
            e_i, half = ve // 2, ve % 2
            gsl, usl, dsl = (3 * ve) % 5, (3 * ve + 1) % 5, (3 * ve + 2) % 5
            for sl, src in ((gsl, d["gt"][e_i, :, :, half * HH:(half + 1) * HH]),
                            (usl, d["ut"][e_i, :, :, half * HH:(half + 1) * HH]),
                            (dsl, d["dtw"][e_i, :, half * 8:(half + 1) * 8, :])):
                which, val = slot_free[sl]
                ew("gpsimd", which, val)
                dma("gpsimd", WSL[sl].rearrange("p (cc n) -> p cc n", n=C), src, f"g{ve % 4}")
            return snapw(f"g{ve % 4}")

        ld_wm = {}
        for p in range(NPASS):
            spx = f"x{p % 4}"
            ew("sync", "v", prev_pass_v)
            ew("sync", out_store_st)
            pass_x = None

            if p == 0:
                ld_wm[(0, 0)] = emit_w_load(0, 0)

            for ve in range(16):
                e_i, half = ve // 2, ve % 2
                gsl, usl, dsl = (3 * ve) % 5, (3 * ve + 1) % 5, (3 * ve + 2) % 5
                GWs, UWs, DWs = WSL[gsl], WSL[usl], WSL[dsl]

                svx = f"x{1 + (ve % 3)}"
                ew("sync", "t", xnq_guard)
                for tl in range(2):
                    dma("sync", XNQ[tl].rearrange("p (cc n) -> p cc n", n=TM),
                        xnbf_d[:, :, p * PTOK + tl * TM:p * PTOK + (tl + 1) * TM], svx)
                xnq_x = snapw(svx)
                if ve == 0:
                    dma("sync", OUT32.rearrange("p (cc n) -> p cc n", n=PTOK),
                        racc_d[:, :, p * PTOK:(p + 1) * PTOK], "x0")
                    pass_x = snapw("x0")

                if half == 0:
                    ew("tensor", xnq_x)
                    for tl in range(2):
                        ew("sync", "t", combE_guard[0])
                        dma("sync", combE[:, :],
                            comb_d[e_i:e_i + 1, p * PTOK + tl * TM:p * PTOK + (tl + 1) * TM], svx)
                        ce_x = snapw(svx)
                        if tl == 1:
                            ew("tensor", "v", snap("v"))
                        ew("tensor", ce_x)
                        ei("tensor",
                           lambda e: e.matmul(PB[6][:, 0:TM], onesrow[:, :], combE[:, :],
                                              start=True, stop=True), "t")
                        cb_t = snap("t")
                        combE_guard[0] = cb_t
                        ew("vector", "t", cb_t)
                        ew("vector", xnq_x)
                        ew("vector", "t", xnp_guard)
                        for cc in range(NCC):
                            ei("vector",
                               lambda e, c=cc, t2=tl: e.tensor_mul(mch(XNP[t2], c), mch(XNQ[t2], c),
                                                                   PB[6][:, 0:TM]),
                               "v" if cc == NCC - 1 else None)
                    xnp_v = snap("v")
                else:
                    xnp_v = 0

                ew("tensor", ld_wm[(p, ve)])
                ew("tensor", xnq_x)
                if half == 0:
                    ew("tensor", "v", xnp_v)
                gu_done = {}
                for tl in range(2):
                    ew("tensor", "t", gu_read_guard[tl])
                    for hc in range(NCC):
                        pbi = hc % 2
                        which, val = gu_bank_guard[pbi]
                        ew("tensor", which, val)
                        for cc in range(NCC):
                            ei("tensor",
                               lambda e, l=GWs[:, cc * C + hc * 128:cc * C + hc * 128 + 128],
                               r=mch(XNQ[tl], cc), o=PB[pbi][:, 0:TM],
                               st=(cc == 0), sp=(cc == NCC - 1):
                               e.matmul(o, l, r, start=st, stop=sp))
                        which, val = u_bank_guard[pbi]
                        ew("tensor", which, val)
                        for cc in range(NCC):
                            ei("tensor",
                               lambda e, l=UWs[:, cc * C + hc * 128:cc * C + hc * 128 + 128],
                               r=mch(XNP[tl], cc), o=PB[2 + pbi][:, 0:TM],
                               st=(cc == 0), sp=(cc == NCC - 1):
                               e.matmul(o, l, r, start=st, stop=sp),
                               "t" if cc == NCC - 1 else None)
                        gu_t = snap("t")
                        ew("scalar", "t", gu_t)
                        ei("scalar",
                           lambda e, b2=pbi, t2=tl, h2=hc:
                           e.activation(GU[t2][:, h2 * TM:(h2 + 1) * TM],
                                        PB[b2][:, 0:TM], AF.Silu), "s")
                        sg_s = snap("s")
                        gu_bank_guard[pbi] = ("s", sg_s)
                        ew("vector", "s", sg_s)
                        ei("vector",
                           lambda e, b2=pbi, t2=tl, h2=hc:
                           e.tensor_mul(GU[t2][:, h2 * TM:(h2 + 1) * TM],
                                        GU[t2][:, h2 * TM:(h2 + 1) * TM], PB[2 + b2][:, 0:TM]), "v")
                        u_bank_guard[pbi] = ("v", snap("v"))
                    gu_done[tl] = snap("v")
                xnq_guard = snap("t")
                if half == 1:
                    xnp_guard = snap("t")

                slot_free[gsl] = ("t", snap("t"))
                slot_free[usl] = ("t", snap("t"))
                nxt = (p, ve + 1) if ve + 1 < 16 else ((p + 1, 0) if p + 1 < NPASS else None)

                for tl in range(2):
                    ew("tensor", "v", gu_done[tl])
                    for dc in range(NCC):
                        pbi = dc % 2
                        which, val = y_bank_guard[pbi]
                        ew("tensor", which, val)
                        for hc in range(NCC):
                            ei("tensor",
                               lambda e, l=DWs[:, hc * C + dc * 128:hc * C + dc * 128 + 128],
                               r=GU[tl][:, hc * TM:(hc + 1) * TM], o=PB[4 + pbi][:, 0:TM],
                               st=(hc == 0), sp=(hc == NCC - 1):
                               e.matmul(o, l, r, start=st, stop=sp),
                               "t" if hc == NCC - 1 else None)
                        y_t = snap("t")
                        ew("vector", "t", y_t)
                        ew("vector", pass_x)
                        ei("vector",
                           lambda e, c=dc, t2=tl, b2=pbi:
                           e.tensor_add(OUT32[:, c * PTOK + t2 * TM:c * PTOK + t2 * TM + TM],
                                        OUT32[:, c * PTOK + t2 * TM:c * PTOK + t2 * TM + TM],
                                        PB[4 + b2][:, 0:TM]), "v")
                        y_bank_guard[pbi] = ("v", snap("v"))
                    gu_read_guard[tl] = snap("t")
                slot_free[dsl] = ("t", snap("t"))
                if nxt is not None:
                    ld_wm[nxt] = emit_w_load(*nxt)

            prev_pass_v = snap("v")
            ew("sync", "v", prev_pass_v)
            dma("sync", out_d[:, :, p * PTOK:(p + 1) * PTOK],
                OUT32.rearrange("p (cc n) -> p cc n", n=PTOK), f"st{p % 4}")
            out_store_st = snapw(f"st{p % 4}")

        for j in range(4):
            ew("sync", snapw(f"st{j}"))
            ew("sync", snapw(f"x{j}"))
            ew("sync", snapw(f"g{j}"))

        block = ES.enter_context(nc.Block())

        @block.tensor
        def _(e):
            for f in prog["tensor"]:
                f(e)

        @block.vector
        def _(e):
            for f in prog["vector"]:
                f(e)

        @block.scalar
        def _(e):
            for f in prog["scalar"]:
                f(e)

        @block.sync
        def _(e):
            for f in prog["sync"]:
                f(e)

        @block.gpsimd
        def _(e):
            for f in prog["gpsimd"]:
                f(e)

    return nc


def _prep_shared(inputs):
    f32 = np.float32
    sh = {}

    def cm_pack(w):
        return np.ascontiguousarray(w.reshape(NCC, 128, -1).transpose(1, 0, 2))

    sh["wq"] = cm_pack(np.asarray(inputs["L0_attn_Wq"], f32).T)
    sh["wk"] = cm_pack(np.asarray(inputs["L0_attn_Wk"], f32).T)
    sh["wv"] = cm_pack(np.asarray(inputs["L0_attn_Wv"], f32).T)
    sh["wg"] = cm_pack(np.asarray(inputs["L0_attn_Wg"], f32).T)
    sh["wo"] = cm_pack(np.asarray(inputs["L0_attn_Wo"], f32).T)
    sh["w0"] = cm_pack(np.asarray(inputs["L0_attn_W0"], f32))
    sh["aini"] = cm_pack(np.asarray(inputs["L0_attn_A_init"], f32))
    sh["bini"] = np.ascontiguousarray(np.asarray(inputs["L0_attn_B_init"], f32))
    sh["gw"] = cm_pack(np.asarray(inputs["L0_ffn_gate_w"], f32).T)
    for nm, key in (("lng", "L0_attn_ln_g"), ("lnb", "L0_attn_ln_b"),
                    ("lsg", "L0_attn_ls_g"), ("loglr", "L0_attn_log_lr")):
        v = np.asarray(inputs[key], f32)
        sh[nm] = np.ascontiguousarray(v.reshape(NCC, 128).T)
    gt = np.asarray(inputs["L0_ffn_gate_proj_w"], f32)
    ut = np.asarray(inputs["L0_ffn_up_proj_w"], f32)
    dw = np.asarray(inputs["L0_ffn_down_proj_w"], f32)
    sh["gt"] = np.ascontiguousarray(
        gt.transpose(0, 2, 1).reshape(NE, NCC, 128, 2048).transpose(0, 2, 1, 3))
    sh["ut"] = np.ascontiguousarray(
        ut.transpose(0, 2, 1).reshape(NE, NCC, 128, 2048).transpose(0, 2, 1, 3))
    sh["dtw"] = np.ascontiguousarray(
        dw.transpose(0, 2, 1).reshape(NE, 16, 128, 1024).transpose(0, 2, 1, 3))
    sh["ident"] = np.eye(128, dtype=f32)
    sh["ones512"] = np.ones((128, 256), f32)
    sh["onesrow"] = np.ones((1, 128), f32)
    sh["onescolb"] = np.ones((128, 1), ml_dtypes.bfloat16)

    return sh


def kernel(**inputs):
    x = np.asarray(inputs["L0_attn_x"], np.float32)
    B, T, _ = x.shape
    xf = x.reshape(B * T, C)

    sh = _prep_shared(inputs)
    nc = build_nc()

    in_maps = []
    for c in range(NCORE):
        m = dict(sh)
        shard = xf[c * TOK:(c + 1) * TOK]
        xcm = np.ascontiguousarray(shard.T)
        m["x"] = np.ascontiguousarray(
            xcm.reshape(NCC, 128, NTA, TA).transpose(1, 2, 0, 3))
        oh = np.zeros((1, 8), np.float32)
        oh[0, c] = 1.0
        m["onehot"] = oh
        mg = np.zeros((8, 1), np.float32)
        for cp in range(NCORE):
            if cp < c and cp // 4 == c // 4:
                mg[cp, 0] = 1.0
        m["maskg"] = mg
        mc = np.zeros((8, 1), np.float32)
        mc[0:c, 0] = 1.0
        m["maskc"] = mc
        pos0 = (c * TOK) % T
        m["pos"] = (np.arange(pos0 + 1, pos0 + TOK + 1, dtype=np.float64)
                    .astype(np.float32).reshape(1, TOK))
        in_maps.append(m)

    res = run_bass_kernel_spmd(nc, in_maps, core_ids=list(range(NCORE)))

    out = np.empty((B * T, C), np.float32)
    for c in range(NCORE):
        oc = np.asarray(res.results[c]["out"]).reshape(128, NCC, TOK)
        ocm = oc.transpose(1, 0, 2).reshape(C, TOK)
        out[c * TOK:(c + 1) * TOK] = ocm.T
    return out.reshape(B, T, C)


# version 06
# speedup vs baseline: 1.1091x; 1.1091x over previous
import contextlib
import numpy as np
import ml_dtypes
import concourse.bass as bass
import concourse.mybir as mybir
from concourse.bass_utils import run_bass_kernel_spmd

F32 = mybir.dt.float32
BF16 = mybir.dt.bfloat16
AF = mybir.ActivationFunctionType
ALU = mybir.AluOpType
AX = mybir.AxisListType.X

NCORE = 8
C = 1024
NCC = 8
TOK = 4096
TA = 256
NTA = 16
TM = 512
NPASS = 4
PTOK = 1024
NE = 8
HH = 1024
CAPP1 = 5121.0
EPS = 1e-5

ENGS = ("tensor", "vector", "scalar", "sync", "gpsimd")


def build_nc():
    nc = bass.Bass("TRN2", target_bir_lowering=False, detect_race_conditions=False)

    d = {}
    d["x"] = nc.dram_tensor("x", [128, NTA, NCC, TA], F32, kind="ExternalInput").ap()
    for nm in ("wq", "wk", "wv", "wg", "wo", "w0"):
        d[nm] = nc.dram_tensor(nm, [128, NCC, C], F32, kind="ExternalInput").ap()
    d["aini"] = nc.dram_tensor("aini", [128, NCC, 8], F32, kind="ExternalInput").ap()
    d["bini"] = nc.dram_tensor("bini", [8, C], F32, kind="ExternalInput").ap()
    d["gw"] = nc.dram_tensor("gw", [128, NCC, 8], F32, kind="ExternalInput").ap()
    for nm in ("lng", "lnb", "lsg", "loglr"):
        d[nm] = nc.dram_tensor(nm, [128, NCC], F32, kind="ExternalInput").ap()
    d["gt"] = nc.dram_tensor("gt", [NE, 128, NCC, 2048], F32, kind="ExternalInput").ap()
    d["ut"] = nc.dram_tensor("ut", [NE, 128, NCC, 2048], F32, kind="ExternalInput").ap()
    d["dtw"] = nc.dram_tensor("dtw", [NE, 128, 16, C], F32, kind="ExternalInput").ap()
    d["ident"] = nc.dram_tensor("ident", [128, 128], F32, kind="ExternalInput").ap()
    d["ones512"] = nc.dram_tensor("ones512", [128, 256], F32, kind="ExternalInput").ap()
    d["onesrow"] = nc.dram_tensor("onesrow", [1, 128], F32, kind="ExternalInput").ap()
    d["onescolb"] = nc.dram_tensor("onescolb", [128, 1], BF16, kind="ExternalInput").ap()

    d["onehot"] = nc.dram_tensor("onehot", [1, 8], F32, kind="ExternalInput").ap()
    d["maskg"] = nc.dram_tensor("maskg", [8, 1], F32, kind="ExternalInput").ap()
    d["maskc"] = nc.dram_tensor("maskc", [8, 1], F32, kind="ExternalInput").ap()
    d["pos"] = nc.dram_tensor("pos", [1, TOK], F32, kind="ExternalInput").ap()

    out_d = nc.dram_tensor("out", [128, NCC, TOK], F32, kind="ExternalOutput").ap()

    KIND = "ExternalOutput" if globals().get("DBG") else None
    def scratch(nm, shp, dt):
        if KIND:
            return nc.dram_tensor(nm, shp, dt, kind=KIND).ap()
        return nc.dram_tensor(nm, shp, dt).ap()
    racc_d = scratch("racc_s", [128, NCC, TOK], F32)
    xnbf_d = scratch("xnbf_s", [128, NCC, TOK], BF16)
    ohs_d = scratch("ohs_s", [8, 2 * TOK], F32)
    w12_d = scratch("w12_s", [1, 3 * TOK], F32)
    comb_d = scratch("comb_s", [8, TOK], F32)
    rows_d = scratch("rows_s", [1, NTA * 1024], F32)
    grow_d = scratch("grow_s", [1, 1040], F32)
    prow_d = scratch("prow_s", [1, 1040], F32)
    ccin_d = nc.dram_tensor("ccin_s", [8, 1040], F32).ap()
    ccout_d = nc.dram_tensor("ccout_s", [8, 1040], F32).ap()

    ES = contextlib.ExitStack()
    with ES:
        def sbuf(name, shape, dt):
            return ES.enter_context(nc.sbuf_tensor(name, shape, dt)).ap()

        def psum(name, shape, dt=F32):
            return ES.enter_context(nc.psum_tensor(name, shape, dt)).ap()

        ABW = sbuf("ABW", [128, 5 * NCC * C], BF16)
        AFA = sbuf("AFA", [128, 7 * NCC * TA], F32)
        ABS = sbuf("ABSA", [128, 16384], BF16)
        GUA = sbuf("GUA", [128, 2 * NCC * TM], BF16)
        W0B = GUA[:, 0:NCC * C]
        FREE2 = ABS[:, 10240:16384].bitcast(F32)
        QCH = FREE2[:, 2816:3072]

        WSL = [ABW[:, j * NCC * C:(j + 1) * NCC * C] for j in range(5)]
        WQ, WK, WV, WG, WO = WSL

        def lhs_w(w, cc, dc):
            return w[:, cc * C + dc * 128: cc * C + (dc + 1) * 128]

        def afv(i):
            return AFA[:, i * NCC * TA:(i + 1) * NCC * TA]

        X0, X1, XN32, K32, V32, Y032, GRAD = (afv(i) for i in range(7))
        OUT32 = AFA[:, 0:NCC * PTOK]

        def ch(buf, cc):
            return buf[:, cc * TA:(cc + 1) * TA]

        XBF0 = ABS[:, 0:2048]
        XBF1 = ABS[:, 2048:4096]
        HB = ABS[:, 4096:6144]
        KB = ABS[:, 6144:8192]
        OG = ABS[:, 8192:10240]
        XSQ = OG
        XNQ = (ABS[:, 0:4096], ABS[:, 4096:8192])
        XNP = (ABS[:, 8192:12288], ABS[:, 12288:16384])
        GU = (GUA[:, 0:NCC * TM], GUA[:, NCC * TM:2 * NCC * TM])

        def mch(buf, cc):
            return buf[:, cc * TM:(cc + 1) * TM]

        ident = sbuf("identS", [128, 128], F32)
        ones512 = sbuf("ones512S", [128, 256], F32)
        onesrow = sbuf("onesrowS", [1, 128], F32)
        onescolb = sbuf("onescolbS", [128, 1], BF16)

        onehotS = sbuf("onehotS", [1, 8], F32)
        maskgS = sbuf("maskgS", [8, 1], F32)
        maskcS = sbuf("maskcS", [8, 1], F32)
        lngS = sbuf("lngS", [128, NCC], F32)
        lnbS = sbuf("lnbS", [128, NCC], F32)
        lsgS = sbuf("lsgS", [128, NCC], F32)
        lrS = sbuf("lrS", [128, NCC], F32)
        ainiB = sbuf("ainiB", [128, NCC * 8], BF16)
        biniB = sbuf("biniB", [8, C], BF16)
        gwB = sbuf("gwB", [128, NCC * 8], BF16)
        RT = sbuf("RT", [1, 2048], F32)
        RB = RT[0:1, 1024:2048]
        zcol = sbuf("zcol", [128, 1], F32)
        gtot = sbuf("gtot", [128, NCC], F32)
        carryg = sbuf("carryg", [128, NCC], F32)
        cnttot = sbuf("cnttot", [8, 2], F32)
        ccarry = sbuf("ccarry", [8, 2], F32)
        KAb = sbuf("KAb", [8, TA], BF16)
        lgt = sbuf("lgt", [128, 16], F32)
        pk = sbuf("pk", [128, 19], F32)
        rtmp = sbuf("rtmp", [128, 32], F32)
        rcol = sbuf("rcol", [128, 8], F32)
        ohst = FREE2[0:8, 0:512]
        ohst2 = ohst
        w12t = FREE2[0:1, 512:1280]
        keepS = FREE2[0:8, 1280:1792]
        lgsb = keepS[0:8, 0:TA]
        kwS = FREE2[0:1, 2048:2560]
        rdenT = FREE2[0:1, 2560:2816]
        combPF = sbuf("combPF", [8, 1040], F32)
        combt = combPF[0:8, 0:TA]
        combE = AFA[0:1, 4 * NCC * TA:4 * NCC * TA + 2 * PTOK]
        contrib = combPF[0:8, 0:1040]
        myrow = RT[0:1, 0:1040]
        gathered = contrib
        prows = myrow
        gprefS = sbuf("gprefS", [128, NCC], F32)
        cprefS = sbuf("cprefS", [8, 2], F32)

        PB = [psum(f"pb{i}", [128, 512]) for i in range(8)]

        S = {}
        semnames = ["t", "v", "s", "cc"]
        semnames += [f"x{j}" for j in range(4)]
        semnames += [f"st{j}" for j in range(4)]
        semnames += [f"g{j}" for j in range(4)]
        for nm in semnames:
            S[nm] = ES.enter_context(nc.semaphore("sem_" + nm))
        cnt = {k: 0 for k in S}
        prog = {e: [] for e in ENGS}

        def ew(eng, sem, val=None):
            if val is None:
                if sem is None:
                    return
                sem, val = sem
            assert val <= cnt[sem], (eng, sem, val, cnt[sem])
            if val > 0:
                prog[eng].append(lambda e, h=S[sem], v=val: e.wait_ge(h, v))

        def ei(eng, fn, sem=None, n=1):
            if sem is None:
                prog[eng].append(fn)
            else:
                cnt[sem] += n
                prog[eng].append(lambda e, f=fn, h=S[sem], k=n: f(e).then_inc(h, k))

        def dma(eng, out, in_, sem, slow=False):
            ei(eng, lambda e, o=out, i=in_, sl=slow:
               e.dma_start(out=o, in_=i, allow_slow_non_contiguous=sl), sem, 16)

        def snap(sem):
            return cnt[sem]

        def snapw(sem):
            return (sem, cnt[sem])

        R_mu = RT[0:1, 0:256]
        R_e2 = RT[0:1, 256:512]; R_var = R_e2; R_bb = R_e2
        R_aa = RT[0:1, 512:768]
        R_ra = RT[0:1, 768:1024]
        R_r1 = RT[0:1, 1024:1280]; R_s2 = RT[0:1, 1280:1536]
        R_mus = RT[0:1, 1536:1792]; R_ps = RT[0:1, 1792:2048]
        RBs = (RT[0:1, 1024:2048], RT[0:1, 0:1024])
        def rbv(i, j):
            return RBs[i % 2][0:1, j * 256:(j + 1) * 256]

        dma("sync", ident[:, :], d["ident"], "x0")
        dma("sync", ones512[:, :], d["ones512"], "x0")
        dma("sync", onesrow[:, :], d["onesrow"], "x0")

        dma("sync", onehotS[:, :], d["onehot"], "x0")
        dma("sync", maskgS[:, :], d["maskg"], "x0")
        dma("sync", maskcS[:, :], d["maskc"], "x0")
        dma("sync", lngS[:, :], d["lng"], "x0")
        dma("sync", lnbS[:, :], d["lnb"], "x0")
        dma("sync", lsgS[:, :], d["lsg"], "x0")
        dma("sync", lrS[:, :], d["loglr"], "x0")
        consts_x = snapw("x0")

        dma("gpsimd", onescolb[:, :], d["onescolb"], "g0")
        for w, nm in ((WQ, "wq"), (WK, "wk"), (WV, "wv"), (WG, "wg"), (WO, "wo"), (W0B, "w0")):
            dma("gpsimd", w.rearrange("p (cc n) -> p cc n", n=C), d[nm], "g0")
        dma("gpsimd", ainiB.rearrange("p (cc n) -> p cc n", n=8), d["aini"], "g0")
        dma("gpsimd", biniB[:, :], d["bini"], "g0")
        dma("gpsimd", gwB.rearrange("p (cc n) -> p cc n", n=8), d["gw"], "g0")
        preload_g = snapw("g0")

        ew("scalar", consts_x)
        ei("scalar", lambda e: e.activation(lrS[:, :], lrS[:, :], AF.Exp), "s")
        ew("vector", "s", snap("s"))
        ew("vector", consts_x)
        ei("vector", lambda e: e.tensor_tensor(lrS[:, :], lrS[:, :], ones512[:, 0:NCC], ALU.min))
        ei("vector", lambda e: e.memset(zcol[:, :], 0.0))
        ei("vector", lambda e: e.memset(gtot[:, :], 0.0))
        ei("vector", lambda e: e.memset(cnttot[:, :], 0.0))
        ei("vector", lambda e: e.memset(ccarry[:, :], 0.0), "v")
        prelude_v = snap("v")

        def vd(fn):
            ei("vector", fn, "v")
            ew("vector", "v", snap("v"))

        def bcast_row(row, bank_ap, mark=None):
            ei("tensor",
               lambda e, r=row, o=bank_ap: e.matmul(o, onesrow[:, :], r, start=True, stop=True),
               mark)

        vfree = {}

        ew("tensor", preload_g)
        ew("tensor", consts_x)
        ew("vector", "v", prelude_v)
        ei("tensor", lambda e: e.nop(), "t")

        store_wm = {}
        for i in range(NTA):
            b = i % 2
            sx = f"x{i % 4}"
            sg = f"g{i % 4}"
            sst = f"st{i % 4}"
            X = X0 if b == 0 else X1
            XB = XBF0 if b == 0 else XBF1
            ew("sync", "v", vfree.get(("a1x", i - 2), 0))
            dma("sync", X.rearrange("p (cc n) -> p cc n", n=TA), d["x"][:, i, :, :], sx)
            dma("sync", RT[0:1, 1792:2048], d["pos"][0:1, i * TA:(i + 1) * TA], sx)
            x_x = snapw(sx)
            ew("gpsimd", "v", vfree.get(("a1x", i - 2), 0))
            ew("gpsimd", "t", vfree.get(("a1xbt", i - 2), 0))
            dma("gpsimd", XB.rearrange("p (cc n) -> p cc n", n=TA), d["x"][:, i, :, :], sg)
            xb_g = snapw(sg)

            ew("vector", store_wm.get(i - 1))
            ew("scalar", xb_g)
            ew("scalar", "v", vfree.get(("a1rows", i - 1), 0))
            ew("scalar", x_x)
            ew("scalar", store_wm.get(i - 1))
            ei("scalar", lambda e, xb=XB: e.activation(XSQ[:, :], xb[:, :], AF.Square), "s")
            ei("scalar", lambda e: e.activation(R_ps, R_ps, AF.Ln), "s")
            xsq_s = snap("s")
            ew("tensor", "s", xsq_s)
            ew("tensor", "v", vfree.get(("a1pb0", i - 1), 0))
            for src, c0 in ((XB, 0), (XSQ, TA)):
                for cc in range(NCC):
                    ei("tensor",
                       lambda e, r=src[:, cc * TA:(cc + 1) * TA], o=PB[0][0:1, c0:c0 + TA],
                       st=(cc == 0), sp=(cc == NCC - 1):
                       e.matmul(o, onescolb[:, :], r, start=st, stop=sp),
                       "t" if (cc == NCC - 1) else None)
            stats_t = snap("t")

            ew("vector", "t", stats_t)
            vd(lambda e: e.tensor_scalar_mul(R_mu, PB[0][0:1, 0:TA], 1.0 / C))
            vd(lambda e: e.tensor_scalar_mul(R_e2, PB[0][0:1, TA:2 * TA], 1.0 / C))
            vfree[("a1pb0", i)] = snap("v")
            vd(lambda e: e.tensor_mul(R_aa, R_mu, R_mu))
            vd(lambda e: e.tensor_sub(R_var, R_e2, R_aa))
            vd(lambda e: e.tensor_scalar_add(R_aa, R_var, EPS))
            vd(lambda e: e.reciprocal(R_ra, R_aa))
            vd(lambda e: e.tensor_scalar(R_ps, R_ps, 0.1, 1.0, ALU.mult, ALU.add))
            vd(lambda e: e.reciprocal(R_ps, R_ps))
            st_v1 = snap("v")
            ew("scalar", "v", st_v1)
            ei("scalar", lambda e: e.activation(R_r1, R_aa, AF.Ln), "s")
            ew("vector", "s", snap("s"))
            vd(lambda e: e.tensor_mul(R_bb, R_var, R_ra))
            vd(lambda e: e.tensor_scalar_add(R_bb, R_bb, EPS))
            ew("scalar", "v", snap("v"))
            ei("scalar", lambda e: e.activation(R_r1, R_r1, AF.Exp, scale=-0.5), "s")
            ei("scalar", lambda e: e.activation(R_s2, R_bb, AF.Ln), "s")
            ei("scalar", lambda e: e.activation(R_s2, R_s2, AF.Exp, scale=-0.5), "s")
            rows_s = snap("s")
            ew("vector", "s", rows_s)
            vd(lambda e: e.tensor_mul(R_mus, R_mu, R_r1))
            rows_v = snap("v")
            ew("sync", "v", rows_v)
            dma("sync", rows_d[0:1, i * 1024:(i + 1) * 1024], RT[0:1, 1024:2048], sst)

            ew("tensor", "v", rows_v)
            ew("tensor", "v", vfree.get(("a1pb", i - 1), 0))
            bcast_row(R_r1, PB[1][:, 0:TA])
            bcast_row(R_mus, PB[1][:, TA:2 * TA])
            bcast_row(R_s2, PB[2][:, 0:TA], "t")
            bc_t = snap("t")

            ew("vector", "t", bc_t)
            ew("vector", x_x)
            for cc in range(NCC):
                ei("vector", lambda e, c=cc, xx=X: e.tensor_mul(ch(XN32, c), ch(xx, c), PB[1][:, 0:TA]))
            for cc in range(NCC):
                ei("vector", lambda e, c=cc: e.tensor_sub(ch(XN32, c), ch(XN32, c), PB[1][:, TA:2 * TA]))
            ei("vector", lambda e, xb=XB: e.tensor_copy(xb[:, :], XN32[:, :]))
            for cc in range(NCC):
                ei("vector", lambda e, c=cc: e.tensor_mul(ch(GRAD, c), ch(XN32, c), PB[2][:, 0:TA]),
                   "v" if cc == NCC - 1 else None)
            h32_v = snap("v")
            vfree[("a1rows", i)] = h32_v
            vfree[("a1pb", i)] = h32_v
            ew("scalar", "v", h32_v)
            for cc in range(NCC):
                ei("scalar",
                   lambda e, c=cc: e.activation(HB[:, c * TA:(c + 1) * TA], ch(GRAD, c), AF.Identity,
                                                bias=lnbS[:, c:c + 1], scale=lngS[:, c:c + 1]),
                   "s" if cc == NCC - 1 else None)
            hb_s = snap("s")

            ew("tensor", "s", hb_s)
            bank_free = {3: ("t", 0), 4: ("t", 0)}
            for w, kind in ((WK, "K"), (WV, "V")):
                for dc in range(NCC):
                    bi = 3 + (dc % 2)
                    which, val = bank_free[bi]
                    ew("tensor", which, val)
                    for cc in range(NCC):
                        ei("tensor",
                           lambda e, l=lhs_w(w, cc, dc), r=HB[:, cc * TA:(cc + 1) * TA],
                           o=PB[bi][:, 0:TA], st=(cc == 0), sp=(cc == NCC - 1):
                           e.matmul(o, l, r, start=st, stop=sp),
                           "t" if cc == NCC - 1 else None)
                    mm_t = snap("t")
                    ew("vector", "t", mm_t)
                    if kind == "K":
                        ei("vector", lambda e, c=dc, b2=bi: e.tensor_copy(ch(K32, c), PB[b2][:, 0:TA]))
                        ei("vector", lambda e, c=dc, b2=bi: e.tensor_copy(KB[:, c * TA:(c + 1) * TA], PB[b2][:, 0:TA]), "v")
                    else:
                        ei("vector", lambda e, c=dc, b2=bi: e.tensor_copy(ch(V32, c), PB[b2][:, 0:TA]), "v")
                    bank_free[bi] = ("v", snap("v"))
            kv_v = snap("v")

            for cc in range(NCC):
                ei("tensor",
                   lambda e, l=gwB[:, cc * 8:(cc + 1) * 8], r=XB[:, cc * TA:(cc + 1) * TA],
                   o=PB[5][0:8, TA:2 * TA], st=(cc == 0), sp=(cc == NCC - 1):
                   e.matmul(o, l, r, start=st, stop=sp),
                   "t" if cc == NCC - 1 else None)
            lg_t = snap("t")

            ew("tensor", "v", kv_v)
            for cc in range(NCC):
                ei("tensor",
                   lambda e, l=ainiB[:, cc * 8:(cc + 1) * 8], r=KB[:, cc * TA:(cc + 1) * TA],
                   o=PB[5][0:8, 0:TA], st=(cc == 0), sp=(cc == NCC - 1):
                   e.matmul(o, l, r, start=st, stop=sp),
                   "t" if cc == NCC - 1 else None)
            ka_t = snap("t")
            ew("vector", "t", ka_t)
            ei("vector", lambda e: e.tensor_copy(KAb[:, :], PB[5][0:8, 0:TA]))
            ei("vector", lambda e: e.tensor_copy(lgsb[:, :], PB[5][0:8, TA:2 * TA]), "v")
            kab_v = snap("v")

            ew("tensor", "v", kab_v)
            for dc in range(NCC):
                bi = 3 + (dc % 2)
                which, val = bank_free[bi]
                ew("tensor", which, val)
                for cc in range(NCC):
                    ei("tensor",
                       lambda e, l=lhs_w(W0B, cc, dc), r=KB[:, cc * TA:(cc + 1) * TA],
                       o=PB[bi][:, 0:TA], st=(cc == 0):
                       e.matmul(o, l, r, start=st, stop=False))
                ei("tensor",
                   lambda e, l=biniB[:, dc * 128:(dc + 1) * 128], o=PB[bi][:, 0:TA]:
                   e.matmul(o, l, KAb[:, :], start=False, stop=True), "t")
                y_t = snap("t")
                ew("vector", "t", y_t)
                ei("vector", lambda e, c=dc, b2=bi: e.tensor_sub(ch(GRAD, c), PB[b2][:, 0:TA], ch(V32, c)))
                ei("vector", lambda e, c=dc: e.tensor_mul(ch(GRAD, c), ch(GRAD, c), ch(K32, c)))
                ei("vector", lambda e, c=dc: e.tensor_reduce(rcol[:, 0:1], ch(GRAD, c), AX, ALU.add))
                ei("vector", lambda e, c=dc: e.tensor_add(gtot[:, c:c + 1], gtot[:, c:c + 1], rcol[:, 0:1]), "v")
                bank_free[bi] = ("v", snap("v"))

            ew("vector", "t", lg_t)
            ew("tensor", "v", snap("v"))
            for hf in range(2):
                ei("tensor",
                   lambda e, h2=hf, r=lgsb[0:8, 0:128] if hf == 0 else lgsb[0:8, 128:256],
                   o=PB[6][0:128, 140 + hf * 8:148 + hf * 8]:
                   e.transpose(o, r, ident[0:8, 0:8]),
                   "t" if hf == 1 else None)
            tp_t = snap("t")
            ew("vector", "t", tp_t)
            vd(lambda e: e.tensor_copy(lgt[:, 0:16], PB[6][0:128, 140:156]))
            routing_tr_t = 0
            for hf in range(2):
                lg8 = lgt[:, hf * 8:(hf + 1) * 8]
                m1 = pk[:, 16:17]; m2 = pk[:, 17:18]; ssum = pk[:, 18:19]
                nm = rcol[:, 1:2]
                prb = rtmp[:, 0:8]; ohc = rtmp[:, 8:16]
                eq1 = rtmp[:, 16:24]; pr2 = rtmp[:, 24:32]
                oh1 = pk[:, 0:8]; oh2 = pk[:, 8:16]
                if hf == 1:
                    ew("vector", "t", routing_tr_t)
                vd(lambda e, a=lg8, m=m1: e.reduce_max(m, a, AX))
                vd(lambda e, m=m1, n=nm: e.tensor_sub(n, zcol[:, :], m))
                ew("scalar", "v", snap("v"))
                ei("scalar", lambda e, a=lg8, p=prb, n=nm: e.activation(p, a, AF.Exp, bias=n), "s")
                ew("vector", "s", snap("s"))
                vd(lambda e, p=prb, s_=ssum: e.tensor_reduce(s_, p, AX, ALU.add))
                vd(lambda e, p=prb, m=m1: e.reduce_max(m, p, AX))
                vd(lambda e, p=prb, m=m1, o=oh1: e.tensor_scalar(o, p, m, 0.0, ALU.subtract, ALU.is_ge))
                vd(lambda e, o=oh1, c2=ohc: e.tensor_tensor_scan(c2, ones512[:, 0:8], o, 0.0, ALU.mult, ALU.add))
                vd(lambda e, c2=ohc, q=eq1: e.tensor_scalar(q, c2, 1.0, 0.0, ALU.subtract, ALU.is_equal))
                vd(lambda e, o=oh1, q=eq1: e.tensor_mul(o, o, q))
                vd(lambda e, o=oh1, p=prb, p2=pr2: e.scalar_tensor_tensor(p2, o, -1e30, p, ALU.mult, ALU.add))
                vd(lambda e, p2=pr2, m=m2: e.reduce_max(m, p2, AX))
                vd(lambda e, p2=pr2, m=m2, o=oh2: e.tensor_scalar(o, p2, m, 0.0, ALU.subtract, ALU.is_ge))
                vd(lambda e, o=oh2, c2=ohc: e.tensor_tensor_scan(c2, ones512[:, 0:8], o, 0.0, ALU.mult, ALU.add))
                vd(lambda e, c2=ohc, q=eq1: e.tensor_scalar(q, c2, 1.0, 0.0, ALU.subtract, ALU.is_equal))
                vd(lambda e, o=oh2, q=eq1: e.tensor_mul(o, o, q))
                pk_v = snap("v")
                ew("tensor", "v", pk_v)
                ei("tensor", lambda e: e.transpose(PB[6][0:8, 0:128], pk[:, 0:8], ident[:, :]))
                ei("tensor", lambda e: e.transpose(PB[6][0:8, 128:256], pk[:, 8:16], ident[:, :]))
                ei("tensor", lambda e: e.transpose(PB[6][0:1, 256:384], pk[:, 16:17], ident[:, :]))
                ei("tensor", lambda e: e.transpose(PB[6][0:1, 384:512], pk[:, 17:18], ident[:, :]))
                ei("tensor", lambda e: e.transpose(PB[7][0:1, 0:128], pk[:, 18:19], ident[:, :]), "t")
                tr_t = snap("t")
                ew("vector", "t", tr_t)
                ei("vector",
                   lambda e, h2=hf: e.tensor_copy(ohst[:, h2 * 128:(h2 + 1) * 128],
                                                  PB[6][0:8, 0:128]))
                ei("vector",
                   lambda e, h2=hf: e.tensor_copy(ohst[:, TA + h2 * 128:TA + (h2 + 1) * 128],
                                                  PB[6][0:8, 128:256]))
                ei("vector",
                   lambda e, h2=hf: e.tensor_copy(w12t[0:1, h2 * 128:(h2 + 1) * 128],
                                                  PB[6][0:1, 256:384]))
                ei("vector",
                   lambda e, h2=hf: e.tensor_copy(w12t[0:1, TA + h2 * 128:TA + (h2 + 1) * 128],
                                                  PB[6][0:1, 384:512]))
                ei("vector",
                   lambda e, h2=hf: e.tensor_copy(w12t[0:1, 2 * TA + h2 * 128:2 * TA + (h2 + 1) * 128],
                                                  PB[7][0:1, 0:128]), "v")
                if hf == 0:
                    routing_tr_t = tr_t
            vd(lambda e: e.tensor_reduce(rcol[0:8, 0:1], ohst[0:8, 0:TA], AX, ALU.add))
            vd(lambda e: e.tensor_add(cnttot[:, 0:1], cnttot[:, 0:1], rcol[0:8, 0:1]))
            vd(lambda e: e.tensor_reduce(rcol[0:8, 0:1], ohst[0:8, TA:2 * TA], AX, ALU.add))
            vd(lambda e: e.tensor_add(cnttot[:, 1:2], cnttot[:, 1:2], rcol[0:8, 0:1]))
            oh_v = snap("v")
            ew("sync", "v", oh_v)
            dma("sync", ohs_d[:, 2 * i * TA:2 * (i + 1) * TA], ohst[:, :], sst)
            dma("sync", w12_d[0:1, 3 * i * TA:3 * (i + 1) * TA], w12t[:, :], sst)

            store_wm[i] = snapw(sst)
            vfree[("a1x", i)] = snap("v")
            vfree[("a1xbt", i)] = snap("t")

        a1_t, a1_v, a1_s = snap("t"), snap("v"), snap("s")
        ew("sync", "v", a1_v)
        dma("sync", grow_d[0:1, 0:1024].rearrange("o (cc p) -> (o p) cc", p=128), gtot[:, :], "st0", slow=True)
        dma("sync", grow_d[0:1, 1024:1040].rearrange("o (u k) -> (o k) u", u=2), cnttot[:, :], "st0", slow=True)
        ew("sync", snapw("st0"))
        dma("sync", myrow[:, :], grow_d[:, :], "x0")
        myrow_x = snapw("x0")
        ew("tensor", myrow_x)
        ew("tensor", "v", a1_v)
        ei("tensor", lambda e: e.matmul(PB[6][0:8, 0:512], onehotS[:, :], myrow[0:1, 0:512], start=True, stop=True))
        ei("tensor", lambda e: e.matmul(PB[7][0:8, 0:512], onehotS[:, :], myrow[0:1, 512:1024], start=True, stop=True))
        ei("tensor", lambda e: e.matmul(PB[5][0:8, 0:16], onehotS[:, :], myrow[0:1, 1024:1040], start=True, stop=True), "t")
        ew("vector", "t", snap("t"))
        ei("vector", lambda e: e.tensor_copy(contrib[:, 0:512], PB[6][0:8, 0:512]))
        ei("vector", lambda e: e.tensor_copy(contrib[:, 512:1024], PB[7][0:8, 0:512]))
        ei("vector", lambda e: e.tensor_copy(contrib[:, 1024:1040], PB[5][0:8, 0:16]), "v")
        ew("gpsimd", "v", snap("v"))
        dma("gpsimd", ccin_d[:, :], contrib[:, :], "g0")
        ew("gpsimd", snapw("g0"))
        ei("gpsimd", lambda e: e.collective_compute(
            "AllReduce", ALU.add, replica_groups=[list(range(NCORE))],
            ins=[ccin_d], outs=[ccout_d]), "cc")
        ew("gpsimd", "cc", snap("cc"))
        dma("gpsimd", gathered[:, :], ccout_d[:, :], "g1")
        gath_g = snapw("g1")
        ew("tensor", gath_g)
        ei("tensor", lambda e: e.matmul(PB[6][0:1, 0:512], maskgS[:, :], gathered[:, 0:512], start=True, stop=True))
        ei("tensor", lambda e: e.matmul(PB[7][0:1, 0:512], maskgS[:, :], gathered[:, 512:1024], start=True, stop=True))
        ei("tensor", lambda e: e.matmul(PB[5][0:1, 0:16], maskcS[:, :], gathered[:, 1024:1040], start=True, stop=True), "t")
        ew("vector", "t", snap("t"))
        ei("vector", lambda e: e.tensor_copy(prows[0:1, 0:512], PB[6][0:1, 0:512]))
        ei("vector", lambda e: e.tensor_copy(prows[0:1, 512:1024], PB[7][0:1, 0:512]))
        ei("vector", lambda e: e.tensor_copy(prows[0:1, 1024:1040], PB[5][0:1, 0:16]), "v")
        ew("sync", "v", snap("v"))
        dma("sync", prow_d[:, :], prows[:, :], "st1")
        ew("sync", snapw("st1"))
        dma("sync", gprefS[:, :], prow_d[0:1, 0:1024].rearrange("o (cc p) -> (o p) cc", p=128), "x1", slow=True)
        dma("sync", cprefS[:, :], prow_d[0:1, 1024:1040].rearrange("o (u k) -> (o k) u", u=2), "x1", slow=True)
        pref_x = snapw("x1")
        ew("vector", pref_x)
        ei("vector", lambda e: e.tensor_copy(carryg[:, :], gprefS[:, :]))
        ei("vector", lambda e: e.tensor_copy(ccarry[:, :], cprefS[:, :]), "v")
        exch_v = snap("v")

        ew("tensor", "v", exch_v)
        ei("tensor", lambda e: e.nop(), "t")
        qch_guard = {0: ("s", 0), 1: ("s", 0)}
        a2store_wm = {}
        a2bc_wm = {}
        for i in range(NTA):
            b = i % 2
            sx = f"x{i % 4}"
            sst = f"st{i % 4}"
            X = X0 if b == 0 else X1
            XB = XBF0 if b == 0 else XBF1
            if i < 2:
                ew("sync", "v", exch_v)
            else:
                ew("sync", "v", vfree.get(("a2x", i - 2), 0))
            dma("sync", X.rearrange("p (cc n) -> p cc n", n=TA), d["x"][:, i, :, :], sx)
            ew("sync", "t", a2bc_wm.get(i - 2, 0))
            dma("sync", RBs[i % 2][:, :], rows_d[0:1, i * 1024:(i + 1) * 1024], sx)
            ew("sync", "v", vfree.get(("a2x", i - 1), 0))
            dma("sync", ohst2[:, :], ohs_d[:, 2 * i * TA:2 * (i + 1) * TA], sx)
            dma("sync", w12t[:, :], w12_d[0:1, 3 * i * TA:3 * (i + 1) * TA], sx)
            x_x = snapw(sx)

            ew("vector", a2store_wm.get(i - 1))
            ew("tensor", x_x)
            ew("tensor", "v", vfree.get(("a2pb1", i - 1), 0))
            bcast_row(rbv(i, 0), PB[1][:, 0:TA])
            bcast_row(rbv(i, 2), PB[1][:, TA:2 * TA])
            ew("tensor", "v", vfree.get(("a2pb2", i - 1), 0))
            bcast_row(rbv(i, 1), PB[2][:, 0:TA])
            bcast_row(rbv(i, 3), PB[2][:, TA:2 * TA], "t")
            bc_t = snap("t")
            a2bc_wm[i] = bc_t

            ew("vector", "t", bc_t)
            ew("vector", x_x)
            for cc in range(NCC):
                ei("vector", lambda e, c=cc, xx=X: e.tensor_mul(ch(XN32, c), ch(xx, c), PB[1][:, 0:TA]))
            for cc in range(NCC):
                ei("vector", lambda e, c=cc: e.tensor_sub(ch(XN32, c), ch(XN32, c), PB[1][:, TA:2 * TA]))
            ei("vector", lambda e, xb=XB: e.tensor_copy(xb[:, :], XN32[:, :]), "v")
            xnbf_v = snap("v")
            ew("sync", "v", xnbf_v)
            dma("sync", xnbf_d[:, :, i * TA:(i + 1) * TA],
                XB.rearrange("p (cc n) -> p cc n", n=TA), sst)
            for cc in range(NCC):
                ei("vector", lambda e, c=cc: e.tensor_mul(ch(GRAD, c), ch(XN32, c), PB[2][:, 0:TA]),
                   "v" if cc == NCC - 1 else None)
            h32_v = snap("v")
            vfree[("a2pb1", i)] = h32_v
            ew("scalar", "v", h32_v)
            for cc in range(NCC):
                ei("scalar",
                   lambda e, c=cc: e.activation(HB[:, c * TA:(c + 1) * TA], ch(GRAD, c), AF.Identity,
                                                bias=lnbS[:, c:c + 1], scale=lngS[:, c:c + 1]),
                   "s" if cc == NCC - 1 else None)
            hb_s = snap("s")

            ew("tensor", "s", hb_s)
            bank_free = {3: ("t", 0), 4: ("t", 0)}
            for w, kind in ((WK, "K"), (WV, "V")):
                for dc in range(NCC):
                    bi = 3 + (dc % 2)
                    which, val = bank_free[bi]
                    ew("tensor", which, val)
                    for cc in range(NCC):
                        ei("tensor",
                           lambda e, l=lhs_w(w, cc, dc), r=HB[:, cc * TA:(cc + 1) * TA],
                           o=PB[bi][:, 0:TA], st=(cc == 0), sp=(cc == NCC - 1):
                           e.matmul(o, l, r, start=st, stop=sp),
                           "t" if cc == NCC - 1 else None)
                    mm_t = snap("t")
                    ew("vector", "t", mm_t)
                    if kind == "K":
                        ei("vector", lambda e, c=dc, b2=bi: e.tensor_copy(ch(K32, c), PB[b2][:, 0:TA]))
                        ei("vector", lambda e, c=dc, b2=bi: e.tensor_copy(KB[:, c * TA:(c + 1) * TA], PB[b2][:, 0:TA]), "v")
                    else:
                        ei("vector", lambda e, c=dc, b2=bi: e.tensor_copy(ch(V32, c), PB[b2][:, 0:TA]), "v")
                    bank_free[bi] = ("v", snap("v"))
            kv_v = snap("v")

            ew("tensor", "v", kv_v)
            for cc in range(NCC):
                ei("tensor",
                   lambda e, l=ainiB[:, cc * 8:(cc + 1) * 8], r=KB[:, cc * TA:(cc + 1) * TA],
                   o=PB[5][0:8, 0:TA], st=(cc == 0), sp=(cc == NCC - 1):
                   e.matmul(o, l, r, start=st, stop=sp),
                   "t" if cc == NCC - 1 else None)
            ka_t = snap("t")
            ew("vector", "t", ka_t)
            ei("vector", lambda e: e.tensor_copy(KAb[:, :], PB[5][0:8, 0:TA]), "v")
            kab_v = snap("v")

            ew("tensor", "v", kab_v)
            for dc in range(NCC):
                bi = 3 + (dc % 2)
                which, val = bank_free[bi]
                ew("tensor", which, val)
                for cc in range(NCC):
                    ei("tensor",
                       lambda e, l=lhs_w(W0B, cc, dc), r=KB[:, cc * TA:(cc + 1) * TA],
                       o=PB[bi][:, 0:TA], st=(cc == 0):
                       e.matmul(o, l, r, start=st, stop=False))
                ei("tensor",
                   lambda e, l=biniB[:, dc * 128:(dc + 1) * 128], o=PB[bi][:, 0:TA]:
                   e.matmul(o, l, KAb[:, :], start=False, stop=True), "t")
                y_t = snap("t")
                ew("vector", "t", y_t)
                ei("vector", lambda e, c=dc, b2=bi: e.tensor_copy(ch(Y032, c), PB[b2][:, 0:TA]))
                ei("vector", lambda e, c=dc, b2=bi: e.tensor_sub(ch(GRAD, c), PB[b2][:, 0:TA], ch(V32, c)))
                ei("vector", lambda e, c=dc: e.tensor_mul(ch(GRAD, c), ch(GRAD, c), ch(K32, c)), "v")
                bank_free[bi] = ("v", snap("v"))

            for cc in range(NCC):
                vd(lambda e, c=cc: e.tensor_tensor_scan(ch(V32, c), ones512[:, 0:TA], ch(GRAD, c),
                                                        carryg[:, c:c + 1], ALU.mult, ALU.add))
                vd(lambda e, c=cc: e.tensor_copy(carryg[:, c:c + 1], ch(V32, c)[:, TA - 1:TA]))
                ei("vector",
                   lambda e, c=cc: e.scalar_tensor_tensor(ch(GRAD, c), ch(V32, c), 1.0, ch(GRAD, c),
                                                          ALU.mult, ALU.subtract),
                   "v" if cc == NCC - 1 else None)
            scan_v = snap("v")

            ew("tensor", "v", scan_v)
            for dc in range(NCC):
                bi = 3 + (dc % 2)
                which, val = bank_free[bi]
                ew("tensor", which, val)
                for cc in range(NCC):
                    ei("tensor",
                       lambda e, l=lhs_w(WG, cc, dc), r=XB[:, cc * TA:(cc + 1) * TA],
                       o=PB[bi][:, 0:TA], st=(cc == 0), sp=(cc == NCC - 1):
                       e.matmul(o, l, r, start=st, stop=sp),
                       "t" if cc == NCC - 1 else None)
                mm_t = snap("t")
                ew("scalar", "t", mm_t)
                ei("scalar", lambda e, c=dc, b2=bi: e.activation(ch(K32, c), PB[b2][:, 0:TA], AF.Sigmoid), "s")
                bank_free[bi] = ("s", snap("s"))
            gate_s = snap("s")

            ew("vector", "s", gate_s)
            for dc in range(NCC):
                bi = 3 + (dc % 2)
                which, val = bank_free[bi]
                ew("tensor", which, val)
                for cc in range(NCC):
                    ei("tensor",
                       lambda e, l=lhs_w(WQ, cc, dc), r=HB[:, cc * TA:(cc + 1) * TA],
                       o=PB[bi][:, 0:TA], st=(cc == 0), sp=(cc == NCC - 1):
                       e.matmul(o, l, r, start=st, stop=sp),
                       "t" if cc == NCC - 1 else None)
                mm_t = snap("t")
                q = QCH[:, 0:TA]
                which2, val2 = qch_guard[0]
                ew("scalar", which2, val2)
                ew("scalar", "t", mm_t)
                ei("scalar", lambda e, c=dc, b2=bi, q2=q: e.activation(q2, PB[b2][:, 0:TA], AF.Identity,
                                                                      scale=lrS[:, c:c + 1]), "s")
                q_s = snap("s")
                bank_free[bi] = ("s", q_s)
                ew("vector", "s", q_s)
                ei("vector", lambda e, q2=q: e.tensor_mul(q2, q2, PB[2][:, TA:2 * TA]))
                ei("vector", lambda e, c=dc, q2=q: e.tensor_mul(q2, q2, ch(GRAD, c)))
                ei("vector", lambda e, c=dc, q2=q: e.scalar_tensor_tensor(ch(Y032, c), q2, -1.0,
                                                                          ch(Y032, c), ALU.mult, ALU.add))
                ei("vector", lambda e, c=dc: e.tensor_mul(OG[:, c * TA:(c + 1) * TA], ch(Y032, c), ch(K32, c)), "v")
                qch_guard[0] = ("v", snap("v"))
            og_v = snap("v")
            vfree[("a2pb2", i)] = og_v

            ew("tensor", "v", og_v)
            for dc in range(NCC):
                bi = 3 + (dc % 2)
                which, val = bank_free[bi]
                ew("tensor", which, val)
                for cc in range(NCC):
                    ei("tensor",
                       lambda e, l=lhs_w(WO, cc, dc), r=OG[:, cc * TA:(cc + 1) * TA],
                       o=PB[bi][:, 0:TA], st=(cc == 0), sp=(cc == NCC - 1):
                       e.matmul(o, l, r, start=st, stop=sp),
                       "t" if cc == NCC - 1 else None)
                mm_t = snap("t")
                ew("scalar", "t", mm_t)
                ei("scalar", lambda e, c=dc, b2=bi: e.activation(ch(Y032, c), PB[b2][:, 0:TA], AF.Identity,
                                                                 scale=lsgS[:, c:c + 1]), "s")
                bank_free[bi] = ("s", snap("s"))
            wo_s = snap("s")
            ew("vector", "s", wo_s)
            ei("vector", lambda e, xx=X: e.tensor_add(Y032[:, :], Y032[:, :], xx[:, :]))
            ei("vector", lambda e: e.tensor_add(Y032[:, :], Y032[:, :], XN32[:, :]), "v")
            racc_v = snap("v")
            ew("sync", "v", racc_v)
            dma("sync", racc_d[:, :, i * TA:(i + 1) * TA],
                Y032.rearrange("p (cc n) -> p cc n", n=TA), sst)

            ew("vector", x_x)
            for k2 in range(2):
                sl = slice(k2 * TA, (k2 + 1) * TA)
                vd(lambda e, s2_=sl, k3=k2: e.tensor_tensor_scan(keepS[:, s2_], ones512[0:8, 0:TA],
                                                                 ohst2[:, s2_], ccarry[:, k3:k3 + 1],
                                                                 ALU.mult, ALU.add))
                vd(lambda e, k3=k2: e.tensor_copy(
                    ccarry[:, k3:k3 + 1], keepS[:, (k3 + 1) * TA - 1:(k3 + 1) * TA]))
                vd(lambda e, s2_=sl: e.tensor_scalar(keepS[:, s2_], keepS[:, s2_],
                                                     CAPP1, 0.0, ALU.subtract, ALU.is_lt))
                vd(lambda e, s2_=sl: e.tensor_mul(keepS[:, s2_], keepS[:, s2_],
                                                  ohst2[:, s2_]))
            keep_v = snap("v")
            ew("tensor", "v", keep_v)
            ei("tensor", lambda e: e.matmul(PB[5][0:1, 0:TA], ones512[0:8, 0:1],
                                            keepS[:, 0:TA], start=True, stop=True))
            ei("tensor", lambda e: e.matmul(PB[5][0:1, TA:2 * TA], ones512[0:8, 0:1],
                                            keepS[:, TA:2 * TA], start=True, stop=True), "t")
            sel_t = snap("t")
            ew("vector", "t", sel_t)
            vd(lambda e: e.tensor_add(rdenT[:, :], w12t[0:1, 0:TA], w12t[0:1, TA:2 * TA]))
            vd(lambda e: e.scalar_tensor_tensor(rdenT[:, :], w12t[0:1, 2 * TA:3 * TA], 1e-9,
                                                rdenT[:, :], ALU.mult, ALU.add))
            vd(lambda e: e.reciprocal(rdenT[:, :], rdenT[:, :]))
            vd(lambda e: e.tensor_mul(kwS[0:1, 0:TA], PB[5][0:1, 0:TA], w12t[0:1, 0:TA]))
            vd(lambda e: e.tensor_mul(kwS[0:1, TA:2 * TA], PB[5][0:1, TA:2 * TA], w12t[0:1, TA:2 * TA]))
            vd(lambda e: e.tensor_mul(kwS[0:1, 0:TA], kwS[0:1, 0:TA], rdenT[:, :]))
            vd(lambda e: e.tensor_mul(kwS[0:1, TA:2 * TA], kwS[0:1, TA:2 * TA], rdenT[:, :]))
            kw_v = snap("v")
            ew("tensor", "v", kw_v)
            ei("tensor",
               lambda e: e.matmul(PB[7][0:8, 0:TA], onesrow[0:1, 0:8], kwS[0:1, 0:TA],
                                  start=True, stop=True))
            ei("tensor",
               lambda e: e.matmul(PB[7][0:8, TA:2 * TA], onesrow[0:1, 0:8], kwS[0:1, TA:2 * TA],
                                  start=True, stop=True), "t")
            kwb_t = snap("t")
            ew("vector", "t", kwb_t)
            vd(lambda e: e.tensor_mul(combt[:, :], ohst2[:, 0:TA], PB[7][0:8, 0:TA]))
            vd(lambda e: e.tensor_mul(keepS[:, 0:TA], ohst2[:, TA:2 * TA], PB[7][0:8, TA:2 * TA]))
            vd(lambda e: e.tensor_add(combt[:, :], combt[:, :], keepS[:, 0:TA]))
            comb_v = snap("v")
            ew("sync", "v", comb_v)
            dma("sync", comb_d[:, i * TA:(i + 1) * TA], combt[:, :], sst)

            a2store_wm[i] = snapw(sst)
            vfree[("a2x", i)] = snap("v")
            vfree[("a2xt", i)] = snap("t")

        a2_t, a2_v, a2_s = snap("t"), snap("v"), snap("s")
        a2_st = [snapw(f"st{j}") for j in range(4)]
        ew("tensor", "v", a2_v)
        ew("tensor", "s", a2_s)
        ei("tensor", lambda e: e.nop(), "t")
        moe_barrier_t = snap("t")
        ew("vector", "t", moe_barrier_t)
        ew("scalar", "t", moe_barrier_t)
        ew("gpsimd", "t", moe_barrier_t)
        ew("sync", "t", moe_barrier_t)
        for wm in a2_st:
            ew("sync", wm)

        slot_free = {j: ("t", moe_barrier_t) for j in range(5)}
        gu_bank_guard = {0: ("s", 0), 1: ("s", 0)}
        u_bank_guard = {0: ("v", 0), 1: ("v", 0)}
        y_bank_guard = {0: ("v", 0), 1: ("v", 0)}
        xnp_guard = 0
        combE_guard = [0]
        xnq_guard = 0
        gu_read_guard = {0: 0, 1: 0}
        out_store_st = None
        prev_pass_v = a2_v

        def emit_w_load(p, ve):
            e_i, half = ve // 2, ve % 2
            gsl, usl, dsl = (3 * ve) % 5, (3 * ve + 1) % 5, (3 * ve + 2) % 5
            for sl, src in ((gsl, d["gt"][e_i, :, :, half * HH:(half + 1) * HH]),
                            (usl, d["ut"][e_i, :, :, half * HH:(half + 1) * HH]),
                            (dsl, d["dtw"][e_i, :, half * 8:(half + 1) * 8, :])):
                which, val = slot_free[sl]
                ew("gpsimd", which, val)
                dma("gpsimd", WSL[sl].rearrange("p (cc n) -> p cc n", n=C), src, f"g{ve % 4}")
            return snapw(f"g{ve % 4}")

        ld_wm = {}
        fold_wm = {}

        def emit_fold(p, e_i, pass_x, xnq_x):
            ceb = combE[0:1, (e_i % 2) * PTOK:(e_i % 2 + 1) * PTOK]
            sce = f"x{2 + e_i % 2}"
            ew("sync", "t", combE_guard[e_i % 2])
            dma("sync", ceb, comb_d[e_i:e_i + 1, p * PTOK:(p + 1) * PTOK], sce)
            ce_x = snapw(sce)
            for tl in range(2):
                bank = PB[6] if tl == 0 else PB[7]
                ew("tensor", moefold_bank_guard[tl])
                ew("tensor", ce_x)
                ei("tensor",
                   lambda e, r=ceb[0:1, tl * TM:(tl + 1) * TM], bk=bank:
                   e.matmul(bk[:, 0:TM], onesrow[:, :], r, start=True, stop=True), "t")
                cb_t = snap("t")
                ew("vector", "t", cb_t)
                ew("vector", xnq_x)
                ew("vector", "t", xnp_guard)
                for cc in range(NCC):
                    ei("vector",
                       lambda e, c=cc, t2=tl, bk=bank: e.tensor_mul(
                           mch(XNP[t2], c), mch(XNQ[t2], c), bk[:, 0:TM]),
                       "v" if cc == NCC - 1 else None)
                moefold_bank_guard[tl] = ("v", snap("v"))
            combE_guard[e_i % 2] = snap("t")
            return snap("v")

        moefold_bank_guard = {0: ("t", 0), 1: ("t", 0)}
        combE_guard = {0: 0, 1: 0}
        for p in range(NPASS):
            ew("sync", "t", xnq_guard)
            svx = "x1"
            for tl in range(2):
                dma("sync", XNQ[tl].rearrange("p (cc n) -> p cc n", n=TM),
                    xnbf_d[:, :, p * PTOK + tl * TM:p * PTOK + (tl + 1) * TM], svx)
            xnq_x = snapw(svx)
            ew("sync", out_store_st)
            dma("sync", OUT32.rearrange("p (cc n) -> p cc n", n=PTOK),
                racc_d[:, :, p * PTOK:(p + 1) * PTOK], "x0")
            pass_x = snapw("x0")

            if p == 0:
                ld_wm[(0, 0)] = emit_w_load(0, 0)
            fold_wm[(p, 0)] = emit_fold(p, 0, pass_x, xnq_x)

            for ve in range(16):
                e_i, half = ve // 2, ve % 2
                gsl, usl, dsl = (3 * ve) % 5, (3 * ve + 1) % 5, (3 * ve + 2) % 5
                GWs, UWs, DWs = WSL[gsl], WSL[usl], WSL[dsl]
                xnp_v = fold_wm[(p, e_i)]

                ew("tensor", ld_wm[(p, ve)])
                ew("tensor", xnq_x)
                if half == 0:
                    ew("tensor", "v", xnp_v)
                gu_done = {}
                for tl in range(2):
                    ew("tensor", "t", gu_read_guard[tl])
                    for hc in range(NCC):
                        pbi = hc % 2
                        which, val = gu_bank_guard[pbi]
                        ew("tensor", which, val)
                        for cc in range(NCC):
                            ei("tensor",
                               lambda e, l=GWs[:, cc * C + hc * 128:cc * C + hc * 128 + 128],
                               r=mch(XNQ[tl], cc), o=PB[pbi][:, 0:TM],
                               st=(cc == 0), sp=(cc == NCC - 1):
                               e.matmul(o, l, r, start=st, stop=sp))
                        which, val = u_bank_guard[pbi]
                        ew("tensor", which, val)
                        for cc in range(NCC):
                            ei("tensor",
                               lambda e, l=UWs[:, cc * C + hc * 128:cc * C + hc * 128 + 128],
                               r=mch(XNP[tl], cc), o=PB[2 + pbi][:, 0:TM],
                               st=(cc == 0), sp=(cc == NCC - 1):
                               e.matmul(o, l, r, start=st, stop=sp),
                               "t" if cc == NCC - 1 else None)
                        gu_t = snap("t")
                        ew("scalar", "t", gu_t)
                        ei("scalar",
                           lambda e, b2=pbi, t2=tl, h2=hc:
                           e.activation(GU[t2][:, h2 * TM:(h2 + 1) * TM],
                                        PB[b2][:, 0:TM], AF.Silu), "s")
                        sg_s = snap("s")
                        gu_bank_guard[pbi] = ("s", sg_s)
                        ew("vector", "s", sg_s)
                        ei("vector",
                           lambda e, b2=pbi, t2=tl, h2=hc:
                           e.tensor_mul(GU[t2][:, h2 * TM:(h2 + 1) * TM],
                                        GU[t2][:, h2 * TM:(h2 + 1) * TM], PB[2 + b2][:, 0:TM]), "v")
                        u_bank_guard[pbi] = ("v", snap("v"))
                    gu_done[tl] = snap("v")
                xnq_guard = snap("t")
                if half == 1:
                    xnp_guard = snap("t")

                slot_free[gsl] = ("t", snap("t"))
                slot_free[usl] = ("t", snap("t"))
                nxt = (p, ve + 1) if ve + 1 < 16 else ((p + 1, 0) if p + 1 < NPASS else None)
                if half == 1 and e_i + 1 < NE:
                    fold_wm[(p, e_i + 1)] = emit_fold(p, e_i + 1, pass_x, xnq_x)

                for tl in range(2):
                    ew("tensor", "v", gu_done[tl])
                    for dc in range(NCC):
                        pbi = dc % 2
                        which, val = y_bank_guard[pbi]
                        ew("tensor", which, val)
                        for hc in range(NCC):
                            ei("tensor",
                               lambda e, l=DWs[:, hc * C + dc * 128:hc * C + dc * 128 + 128],
                               r=GU[tl][:, hc * TM:(hc + 1) * TM], o=PB[4 + pbi][:, 0:TM],
                               st=(hc == 0), sp=(hc == NCC - 1):
                               e.matmul(o, l, r, start=st, stop=sp),
                               "t" if hc == NCC - 1 else None)
                        y_t = snap("t")
                        ew("vector", "t", y_t)
                        ew("vector", pass_x)
                        ei("vector",
                           lambda e, c=dc, t2=tl, b2=pbi:
                           e.tensor_add(OUT32[:, c * PTOK + t2 * TM:c * PTOK + t2 * TM + TM],
                                        OUT32[:, c * PTOK + t2 * TM:c * PTOK + t2 * TM + TM],
                                        PB[4 + b2][:, 0:TM]), "v")
                        y_bank_guard[pbi] = ("v", snap("v"))
                    gu_read_guard[tl] = snap("t")
                slot_free[dsl] = ("t", snap("t"))
                if nxt is not None:
                    ld_wm[nxt] = emit_w_load(*nxt)

            prev_pass_v = snap("v")
            ew("sync", "v", prev_pass_v)
            dma("sync", out_d[:, :, p * PTOK:(p + 1) * PTOK],
                OUT32.rearrange("p (cc n) -> p cc n", n=PTOK), f"st{p % 4}")
            out_store_st = snapw(f"st{p % 4}")

        for j in range(4):
            ew("sync", snapw(f"st{j}"))
            ew("sync", snapw(f"x{j}"))
            ew("sync", snapw(f"g{j}"))

        block = ES.enter_context(nc.Block())

        @block.tensor
        def _(e):
            for f in prog["tensor"]:
                f(e)

        @block.vector
        def _(e):
            for f in prog["vector"]:
                f(e)

        @block.scalar
        def _(e):
            for f in prog["scalar"]:
                f(e)

        @block.sync
        def _(e):
            for f in prog["sync"]:
                f(e)

        @block.gpsimd
        def _(e):
            for f in prog["gpsimd"]:
                f(e)

    return nc


def _prep_shared(inputs):
    f32 = np.float32
    sh = {}

    def cm_pack(w):
        return np.ascontiguousarray(w.reshape(NCC, 128, -1).transpose(1, 0, 2))

    sh["wq"] = cm_pack(np.asarray(inputs["L0_attn_Wq"], f32).T)
    sh["wk"] = cm_pack(np.asarray(inputs["L0_attn_Wk"], f32).T)
    sh["wv"] = cm_pack(np.asarray(inputs["L0_attn_Wv"], f32).T)
    sh["wg"] = cm_pack(np.asarray(inputs["L0_attn_Wg"], f32).T)
    sh["wo"] = cm_pack(np.asarray(inputs["L0_attn_Wo"], f32).T)
    sh["w0"] = cm_pack(np.asarray(inputs["L0_attn_W0"], f32))
    sh["aini"] = cm_pack(np.asarray(inputs["L0_attn_A_init"], f32))
    sh["bini"] = np.ascontiguousarray(np.asarray(inputs["L0_attn_B_init"], f32))
    sh["gw"] = cm_pack(np.asarray(inputs["L0_ffn_gate_w"], f32).T)
    for nm, key in (("lng", "L0_attn_ln_g"), ("lnb", "L0_attn_ln_b"),
                    ("lsg", "L0_attn_ls_g"), ("loglr", "L0_attn_log_lr")):
        v = np.asarray(inputs[key], f32)
        sh[nm] = np.ascontiguousarray(v.reshape(NCC, 128).T)
    gt = np.asarray(inputs["L0_ffn_gate_proj_w"], f32)
    ut = np.asarray(inputs["L0_ffn_up_proj_w"], f32)
    dw = np.asarray(inputs["L0_ffn_down_proj_w"], f32)
    sh["gt"] = np.ascontiguousarray(
        gt.transpose(0, 2, 1).reshape(NE, NCC, 128, 2048).transpose(0, 2, 1, 3))
    sh["ut"] = np.ascontiguousarray(
        ut.transpose(0, 2, 1).reshape(NE, NCC, 128, 2048).transpose(0, 2, 1, 3))
    sh["dtw"] = np.ascontiguousarray(
        dw.transpose(0, 2, 1).reshape(NE, 16, 128, 1024).transpose(0, 2, 1, 3))
    sh["ident"] = np.eye(128, dtype=f32)
    sh["ones512"] = np.ones((128, 256), f32)
    sh["onesrow"] = np.ones((1, 128), f32)
    sh["onescolb"] = np.ones((128, 1), ml_dtypes.bfloat16)

    return sh


def kernel(**inputs):
    x = np.asarray(inputs["L0_attn_x"], np.float32)
    B, T, _ = x.shape
    xf = x.reshape(B * T, C)

    sh = _prep_shared(inputs)
    nc = build_nc()

    in_maps = []
    for c in range(NCORE):
        m = dict(sh)
        shard = xf[c * TOK:(c + 1) * TOK]
        xcm = np.ascontiguousarray(shard.T)
        m["x"] = np.ascontiguousarray(
            xcm.reshape(NCC, 128, NTA, TA).transpose(1, 2, 0, 3))
        oh = np.zeros((1, 8), np.float32)
        oh[0, c] = 1.0
        m["onehot"] = oh
        mg = np.zeros((8, 1), np.float32)
        for cp in range(NCORE):
            if cp < c and cp // 4 == c // 4:
                mg[cp, 0] = 1.0
        m["maskg"] = mg
        mc = np.zeros((8, 1), np.float32)
        mc[0:c, 0] = 1.0
        m["maskc"] = mc
        pos0 = (c * TOK) % T
        m["pos"] = (np.arange(pos0 + 1, pos0 + TOK + 1, dtype=np.float64)
                    .astype(np.float32).reshape(1, TOK))
        in_maps.append(m)

    res = run_bass_kernel_spmd(nc, in_maps, core_ids=list(range(NCORE)))

    out = np.empty((B * T, C), np.float32)
    for c in range(NCORE):
        oc = np.asarray(res.results[c]["out"]).reshape(128, NCC, TOK)
        ocm = oc.transpose(1, 0, 2).reshape(C, TOK)
        out[c * TOK:(c + 1) * TOK] = ocm.T
    return out.reshape(B, T, C)


# version 09
# speedup vs baseline: 1.1908x; 1.0737x over previous
import contextlib
import numpy as np
import ml_dtypes
import concourse.bass as bass
import concourse.mybir as mybir
from concourse.bass_utils import run_bass_kernel_spmd

F32 = mybir.dt.float32
BF16 = mybir.dt.bfloat16
AF = mybir.ActivationFunctionType
ALU = mybir.AluOpType
AX = mybir.AxisListType.X

NCORE = 8
C = 1024
NCC = 8
TOK = 4096
TA = 256
NTA = 16
TM = 512
NPASS = 4
PTOK = 1024
NE = 8
HH = 1024
CAPP1 = 5121.0
EPS = 1e-5

ENGS = ("tensor", "vector", "scalar", "sync", "gpsimd")


def build_nc():
    nc = bass.Bass("TRN2", target_bir_lowering=False, detect_race_conditions=False)

    d = {}
    d["x"] = nc.dram_tensor("x", [128, NTA, NCC, TA], F32, kind="ExternalInput").ap()
    for nm in ("wq", "wk", "wv", "wg", "wo", "w0"):
        d[nm] = nc.dram_tensor(nm, [128, NCC, C], F32, kind="ExternalInput").ap()
    d["aini"] = nc.dram_tensor("aini", [128, NCC, 8], F32, kind="ExternalInput").ap()
    d["bini"] = nc.dram_tensor("bini", [8, C], F32, kind="ExternalInput").ap()
    d["gw"] = nc.dram_tensor("gw", [128, NCC, 8], F32, kind="ExternalInput").ap()
    for nm in ("lng", "lnb", "lsg", "loglr"):
        d[nm] = nc.dram_tensor(nm, [128, NCC], F32, kind="ExternalInput").ap()
    d["gt"] = nc.dram_tensor("gt", [NE, 128, NCC, 2048], F32, kind="ExternalInput").ap()
    d["ut"] = nc.dram_tensor("ut", [NE, 128, NCC, 2048], F32, kind="ExternalInput").ap()
    d["dtw"] = nc.dram_tensor("dtw", [NE, 128, 16, C], F32, kind="ExternalInput").ap()
    d["ident"] = nc.dram_tensor("ident", [128, 128], F32, kind="ExternalInput").ap()
    d["ones512"] = nc.dram_tensor("ones512", [128, 256], F32, kind="ExternalInput").ap()
    d["onesrow"] = nc.dram_tensor("onesrow", [1, 128], F32, kind="ExternalInput").ap()
    d["onescolb"] = nc.dram_tensor("onescolb", [128, 1], BF16, kind="ExternalInput").ap()

    d["onehot"] = nc.dram_tensor("onehot", [1, 8], F32, kind="ExternalInput").ap()
    d["maskg"] = nc.dram_tensor("maskg", [8, 1], F32, kind="ExternalInput").ap()
    d["maskc"] = nc.dram_tensor("maskc", [8, 1], F32, kind="ExternalInput").ap()
    d["pos"] = nc.dram_tensor("pos", [1, TOK], F32, kind="ExternalInput").ap()

    out_d = nc.dram_tensor("out", [128, NCC, TOK], F32, kind="ExternalOutput").ap()

    KIND = "ExternalOutput" if globals().get("DBG") else None
    def scratch(nm, shp, dt):
        if KIND:
            return nc.dram_tensor(nm, shp, dt, kind=KIND).ap()
        return nc.dram_tensor(nm, shp, dt).ap()
    racc_d = scratch("racc_s", [128, NCC, TOK], F32)
    xnbf_d = scratch("xnbf_s", [128, NCC, TOK], BF16)
    ohs_d = scratch("ohs_s", [8, 2 * TOK], F32)
    w12_d = scratch("w12_s", [1, 3 * TOK], F32)
    comb_d = scratch("comb_s", [8, TOK], F32)
    rows_d = scratch("rows_s", [1, NTA * 1024], F32)
    grow_d = scratch("grow_s", [1, 1040], F32)
    prow_d = scratch("prow_s", [1, 1040], F32)
    ccin_d = nc.dram_tensor("ccin_s", [8, 1040], F32).ap()
    ccout_d = nc.dram_tensor("ccout_s", [8, 1040], F32).ap()

    ES = contextlib.ExitStack()
    with ES:
        def sbuf(name, shape, dt):
            return ES.enter_context(nc.sbuf_tensor(name, shape, dt)).ap()

        def psum(name, shape, dt=F32):
            return ES.enter_context(nc.psum_tensor(name, shape, dt)).ap()

        ABW = sbuf("ABW", [128, 5 * NCC * C], BF16)
        AFA = sbuf("AFA", [128, 7 * NCC * TA], F32)
        ABS = sbuf("ABSA", [128, 16384], BF16)
        GUA = sbuf("GUA", [128, 2 * NCC * TM], BF16)
        W0B = GUA[:, 0:NCC * C]
        FREE2 = ABS[:, 10240:16384].bitcast(F32)
        QCH = sbuf("QCHS", [128, 2 * TA], F32)

        WSL = [ABW[:, j * NCC * C:(j + 1) * NCC * C] for j in range(5)]
        WQ, WK, WV, WG, WO = WSL

        def lhs_w(w, cc, dc):
            return w[:, cc * C + dc * 128: cc * C + (dc + 1) * 128]

        def afv(i):
            return AFA[:, i * NCC * TA:(i + 1) * NCC * TA]

        X0, X1, XN32, K32, V32, Y032, GRAD = (afv(i) for i in range(7))
        OUT32 = AFA[:, 0:NCC * PTOK]

        def ch(buf, cc):
            return buf[:, cc * TA:(cc + 1) * TA]

        XBF0 = ABS[:, 0:2048]
        XBF1 = ABS[:, 2048:4096]
        HB = ABS[:, 4096:6144]
        KB = ABS[:, 6144:8192]
        OG = ABS[:, 8192:10240]
        XSQ = OG
        XNQ = (ABS[:, 0:4096], ABS[:, 4096:8192])
        XNP = (ABS[:, 8192:12288], ABS[:, 12288:16384])
        GU = (GUA[:, 0:NCC * TM], GUA[:, NCC * TM:2 * NCC * TM])

        def mch(buf, cc):
            return buf[:, cc * TM:(cc + 1) * TM]

        ident = sbuf("identS", [128, 128], F32)
        ones512 = sbuf("ones512S", [128, 256], F32)
        onesrow = sbuf("onesrowS", [1, 128], F32)
        onescolb = sbuf("onescolbS", [128, 1], BF16)

        onehotS = sbuf("onehotS", [1, 8], F32)
        maskgS = sbuf("maskgS", [8, 1], F32)
        maskcS = sbuf("maskcS", [8, 1], F32)
        lngS = sbuf("lngS", [128, NCC], F32)
        lnbS = sbuf("lnbS", [128, NCC], F32)
        lsgS = sbuf("lsgS", [128, NCC], F32)
        lrS = sbuf("lrS", [128, NCC], F32)
        ainiB = sbuf("ainiB", [128, NCC * 8], BF16)
        biniB = sbuf("biniB", [8, C], BF16)
        gwB = sbuf("gwB", [128, NCC * 8], BF16)
        RT = sbuf("RT", [1, 2048], F32)
        RB = RT[0:1, 1024:2048]
        zcol = sbuf("zcol", [128, 1], F32)
        gtot = sbuf("gtot", [128, NCC], F32)
        carryg = sbuf("carryg", [128, NCC], F32)
        cnttot = sbuf("cnttot", [8, 2], F32)
        ccarry = sbuf("ccarry", [8, 2], F32)
        KAb = sbuf("KAb", [8, TA], BF16)
        lgt = sbuf("lgt", [128, 16], F32)
        pk = sbuf("pk", [128, 19], F32)
        rtmp = sbuf("rtmp", [128, 32], F32)
        rcol = sbuf("rcol", [128, 8], F32)
        ohst = FREE2[0:8, 0:512]
        ohst2 = ohst
        w12t = FREE2[0:1, 512:1280]
        keepS = FREE2[0:8, 1280:1792]
        lgsb = keepS[0:8, 0:TA]
        kwS = FREE2[0:1, 2048:2560]
        rdenT = FREE2[0:1, 2560:2816]
        combPF = sbuf("combPF", [8, 1040], F32)
        combt = combPF[0:8, 0:TA]
        combE = AFA[0:1, 4 * NCC * TA:4 * NCC * TA + 2 * PTOK]
        contrib = combPF[0:8, 0:1040]
        myrow = RT[0:1, 0:1040]
        gathered = contrib
        prows = myrow
        gprefS = sbuf("gprefS", [128, NCC], F32)
        cprefS = sbuf("cprefS", [8, 2], F32)

        PB = [psum(f"pb{i}", [128, 512]) for i in range(8)]

        S = {}
        semnames = ["t", "v", "s", "cc"]
        semnames += [f"x{j}" for j in range(4)]
        semnames += [f"st{j}" for j in range(4)]
        semnames += [f"g{j}" for j in range(4)]
        for nm in semnames:
            S[nm] = ES.enter_context(nc.semaphore("sem_" + nm))
        cnt = {k: 0 for k in S}
        prog = {e: [] for e in ENGS}

        def ew(eng, sem, val=None):
            if val is None:
                if sem is None:
                    return
                sem, val = sem
            assert val <= cnt[sem], (eng, sem, val, cnt[sem])
            if val > 0:
                prog[eng].append(lambda e, h=S[sem], v=val: e.wait_ge(h, v))

        def ei(eng, fn, sem=None, n=1):
            if sem is None:
                prog[eng].append(fn)
            else:
                cnt[sem] += n
                prog[eng].append(lambda e, f=fn, h=S[sem], k=n: f(e).then_inc(h, k))

        def dma(eng, out, in_, sem, slow=False):
            ei(eng, lambda e, o=out, i=in_, sl=slow:
               e.dma_start(out=o, in_=i, allow_slow_non_contiguous=sl), sem, 16)

        def snap(sem):
            return cnt[sem]

        def snapw(sem):
            return (sem, cnt[sem])

        R_mu = RT[0:1, 0:256]
        R_e2 = RT[0:1, 256:512]; R_var = R_e2; R_bb = R_e2
        R_aa = RT[0:1, 512:768]
        R_ra = RT[0:1, 768:1024]
        R_r1 = RT[0:1, 1024:1280]; R_s2 = RT[0:1, 1280:1536]
        R_mus = RT[0:1, 1536:1792]; R_ps = RT[0:1, 1792:2048]
        RBs = (RT[0:1, 1024:2048], RT[0:1, 0:1024])
        def rbv(i, j):
            return RBs[i % 2][0:1, j * 256:(j + 1) * 256]

        dma("sync", ident[:, :], d["ident"], "x0")
        dma("sync", ones512[:, :], d["ones512"], "x0")
        dma("sync", onesrow[:, :], d["onesrow"], "x0")

        dma("sync", onehotS[:, :], d["onehot"], "x0")
        dma("sync", maskgS[:, :], d["maskg"], "x0")
        dma("sync", maskcS[:, :], d["maskc"], "x0")
        dma("sync", lngS[:, :], d["lng"], "x0")
        dma("sync", lnbS[:, :], d["lnb"], "x0")
        dma("sync", lsgS[:, :], d["lsg"], "x0")
        dma("sync", lrS[:, :], d["loglr"], "x0")
        consts_x = snapw("x0")

        dma("gpsimd", onescolb[:, :], d["onescolb"], "g0")
        for w, nm in ((WQ, "wq"), (WK, "wk"), (WV, "wv"), (WG, "wg"), (WO, "wo"), (W0B, "w0")):
            dma("gpsimd", w.rearrange("p (cc n) -> p cc n", n=C), d[nm], "g0")
        dma("gpsimd", ainiB.rearrange("p (cc n) -> p cc n", n=8), d["aini"], "g0")
        dma("gpsimd", biniB[:, :], d["bini"], "g0")
        dma("gpsimd", gwB.rearrange("p (cc n) -> p cc n", n=8), d["gw"], "g0")
        preload_g = snapw("g0")

        ew("scalar", consts_x)
        ei("scalar", lambda e: e.activation(lrS[:, :], lrS[:, :], AF.Exp), "s")
        ew("vector", "s", snap("s"))
        ew("vector", consts_x)
        ei("vector", lambda e: e.tensor_tensor(lrS[:, :], lrS[:, :], ones512[:, 0:NCC], ALU.min))
        ei("vector", lambda e: e.memset(zcol[:, :], 0.0))
        ei("vector", lambda e: e.memset(gtot[:, :], 0.0))
        ei("vector", lambda e: e.memset(cnttot[:, :], 0.0))
        ei("vector", lambda e: e.memset(ccarry[:, :], 0.0), "v")
        prelude_v = snap("v")

        def vd(fn):
            ei("vector", fn, "v")
            ew("vector", "v", snap("v"))

        def bcast_row(row, bank_ap, mark=None):
            ei("tensor",
               lambda e, r=row, o=bank_ap: e.matmul(o, onesrow[:, :], r, start=True, stop=True),
               mark)

        vfree = {}

        ew("tensor", preload_g)
        ew("tensor", consts_x)
        ew("vector", "v", prelude_v)
        ei("tensor", lambda e: e.nop(), "t")

        store_wm = {}
        rowsinfo = {}
        rows_bc_wm = {}

        def a1_stats(i):
            b = i % 2
            sx = f"x{i % 4}"
            sg = f"g{i % 4}"
            sst = f"st{i % 4}"
            X = X0 if b == 0 else X1
            XB = XBF0 if b == 0 else XBF1
            XSQb = XSQ
            RTb = RT
            R_mu = RTb[0:1, 0:256]
            R_e2 = RTb[0:1, 256:512]; R_var = R_e2; R_bb = R_e2
            R_aa = RTb[0:1, 512:768]
            R_ra = RTb[0:1, 768:1024]
            R_r1 = RTb[0:1, 1024:1280]; R_s2 = RTb[0:1, 1280:1536]
            R_mus = RTb[0:1, 1536:1792]; R_ps = RTb[0:1, 1792:2048]
            ew("sync", "v", vfree.get(("a1x", i - 2), 0))
            ew("sync", "t", rows_bc_wm.get(i - 1, 0))
            dma("sync", X.rearrange("p (cc n) -> p cc n", n=TA), d["x"][:, i, :, :], sx)
            dma("sync", R_ps, d["pos"][0:1, i * TA:(i + 1) * TA], sx)
            x_x = snapw(sx)
            ew("gpsimd", "v", vfree.get(("a1x", i - 2), 0))
            ew("gpsimd", "t", vfree.get(("a1xbt", i - 2), 0))
            dma("gpsimd", XB.rearrange("p (cc n) -> p cc n", n=TA), d["x"][:, i, :, :], sg)
            xb_g = snapw(sg)

            ew("vector", store_wm.get(i - 1))
            ew("scalar", xb_g)
            ew("scalar", "v", vfree.get(("a1rows", i - 1), 0))
            ew("scalar", "t", rows_bc_wm.get(i - 1, 0))
            ew("scalar", x_x)
            ew("scalar", store_wm.get(i - 1))
            ei("scalar", lambda e, xb=XB, xq=XSQb: e.activation(xq[:, :], xb[:, :], AF.Square), "s")
            ei("scalar", lambda e, rp=R_ps: e.activation(rp, rp, AF.Ln), "s")
            xsq_s = snap("s")
            ew("tensor", "s", xsq_s)
            ew("tensor", "v", vfree.get(("a1pb0", i - 1), 0))
            for src, c0 in ((XB, 0), (XSQb, TA)):
                for cc in range(NCC):
                    ei("tensor",
                       lambda e, r=src[:, cc * TA:(cc + 1) * TA], o=PB[0][0:1, c0:c0 + TA],
                       st=(cc == 0), sp=(cc == NCC - 1):
                       e.matmul(o, onescolb[:, :], r, start=st, stop=sp),
                       "t" if (cc == NCC - 1) else None)
            stats_t = snap("t")

            ew("vector", "t", stats_t)
            vd(lambda e, rm=R_mu: e.tensor_scalar_mul(rm, PB[0][0:1, 0:TA], 1.0 / C))
            vd(lambda e, re=R_e2: e.tensor_scalar_mul(re, PB[0][0:1, TA:2 * TA], 1.0 / C))
            vfree[("a1pb0", i)] = snap("v")
            vd(lambda e, ra=R_aa, rm=R_mu: e.tensor_mul(ra, rm, rm))
            vd(lambda e, rv=R_var, re=R_e2, ra=R_aa: e.tensor_sub(rv, re, ra))
            vd(lambda e, ra=R_aa, rv=R_var: e.tensor_scalar_add(ra, rv, EPS))
            vd(lambda e, rr=R_ra, ra=R_aa: e.reciprocal(rr, ra))
            vd(lambda e, rp=R_ps: e.tensor_scalar(rp, rp, 0.1, 1.0, ALU.mult, ALU.add))
            vd(lambda e, rp=R_ps: e.reciprocal(rp, rp))
            st_v1 = snap("v")
            ew("scalar", "v", st_v1)
            ei("scalar", lambda e, rr=R_r1, ra=R_aa: e.activation(rr, ra, AF.Ln), "s")
            ew("vector", "s", snap("s"))
            vd(lambda e, rb=R_bb, rv=R_var, rr=R_ra: e.tensor_mul(rb, rv, rr))
            vd(lambda e, rb=R_bb: e.tensor_scalar_add(rb, rb, EPS))
            ew("scalar", "v", snap("v"))
            ei("scalar", lambda e, rr=R_r1: e.activation(rr, rr, AF.Exp, scale=-0.5), "s")
            ei("scalar", lambda e, rs=R_s2, rb=R_bb: e.activation(rs, rb, AF.Ln), "s")
            ei("scalar", lambda e, rs=R_s2: e.activation(rs, rs, AF.Exp, scale=-0.5), "s")
            rows_s = snap("s")
            ew("vector", "s", rows_s)
            vd(lambda e, rm2=R_mus, rm=R_mu, rr=R_r1: e.tensor_mul(rm2, rm, rr))
            rows_v = snap("v")
            vfree[("a1rows", i)] = rows_v
            ew("sync", "v", rows_v)
            dma("sync", rows_d[0:1, i * 1024:(i + 1) * 1024], RTb[0:1, 1024:2048], sst)
            rowsinfo[i] = (rows_v, R_r1, R_mus, R_s2, x_x, xb_g)

        a1_stats(0)
        for i in range(NTA):
            b = i % 2
            sst = f"st{i % 4}"
            X = X0 if b == 0 else X1
            XB = XBF0 if b == 0 else XBF1
            rows_v, R_r1, R_mus, R_s2, x_x, xb_g = rowsinfo[i]

            ew("tensor", "v", rows_v)
            ew("tensor", "v", vfree.get(("a1pb", i - 1), 0))
            bcast_row(R_r1, PB[1][:, 0:TA])
            bcast_row(R_mus, PB[1][:, TA:2 * TA])
            bcast_row(R_s2, PB[2][:, 0:TA], "t")
            bc_t = snap("t")
            rows_bc_wm[i] = bc_t

            ew("vector", "t", bc_t)
            ew("vector", x_x)
            for cc in range(NCC):
                ei("vector", lambda e, c=cc, xx=X: e.tensor_mul(ch(XN32, c), ch(xx, c), PB[1][:, 0:TA]))
            for cc in range(NCC):
                ei("vector", lambda e, c=cc: e.tensor_sub(ch(XN32, c), ch(XN32, c), PB[1][:, TA:2 * TA]),
                   "v" if cc == NCC - 1 else None)
            xsub_v = snap("v")
            ew("scalar", "v", xsub_v)
            ei("scalar", lambda e, xb=XB: e.copy(xb[:, :], XN32[:, :]), "s")
            xnbf_s1 = snap("s")
            for cc in range(NCC):
                ei("vector", lambda e, c=cc: e.tensor_mul(ch(GRAD, c), ch(XN32, c), PB[2][:, 0:TA]),
                   "v" if cc == NCC - 1 else None)
            h32_v = snap("v")
            vfree[("a1pb", i)] = h32_v
            ew("scalar", "v", h32_v)
            for cc in range(NCC):
                ei("scalar",
                   lambda e, c=cc: e.activation(HB[:, c * TA:(c + 1) * TA], ch(GRAD, c), AF.Identity,
                                                bias=lnbS[:, c:c + 1], scale=lngS[:, c:c + 1]),
                   "s" if cc == NCC - 1 else None)
            hb_s = snap("s")

            if i + 1 < NTA:
                a1_stats(i + 1)

            ew("tensor", "s", hb_s)
            bank_free = {3: ("t", 0), 4: ("t", 0)}
            for w, kind in ((WK, "K"), (WV, "V")):
                for dc in range(NCC):
                    bi = 3 + (dc % 2)
                    ew("tensor", bank_free[bi])
                    for cc in range(NCC):
                        ei("tensor",
                           lambda e, l=lhs_w(w, cc, dc), r=HB[:, cc * TA:(cc + 1) * TA],
                           o=PB[bi][:, 0:TA], st=(cc == 0), sp=(cc == NCC - 1):
                           e.matmul(o, l, r, start=st, stop=sp),
                           "t" if cc == NCC - 1 else None)
                    mm_t = snap("t")
                    ew("scalar", "t", mm_t)
                    if kind == "K":
                        ei("scalar", lambda e, c=dc, b2=bi: e.copy(ch(K32, c), PB[b2][:, 0:TA]))
                        ei("scalar", lambda e, c=dc, b2=bi: e.copy(KB[:, c * TA:(c + 1) * TA], PB[b2][:, 0:TA]), "s")
                    else:
                        ei("scalar", lambda e, c=dc, b2=bi: e.copy(ch(V32, c), PB[b2][:, 0:TA]), "s")
                    bank_free[bi] = ("s", snap("s"))
            kv_s = ("s", snap("s"))

            for cc in range(NCC):
                ei("tensor",
                   lambda e, l=gwB[:, cc * 8:(cc + 1) * 8], r=XB[:, cc * TA:(cc + 1) * TA],
                   o=PB[5][0:8, TA:2 * TA], st=(cc == 0), sp=(cc == NCC - 1):
                   e.matmul(o, l, r, start=st, stop=sp),
                   "t" if cc == NCC - 1 else None)
            lg_t = snap("t")

            ew("tensor", kv_s)
            for cc in range(NCC):
                ei("tensor",
                   lambda e, l=ainiB[:, cc * 8:(cc + 1) * 8], r=KB[:, cc * TA:(cc + 1) * TA],
                   o=PB[5][0:8, 0:TA], st=(cc == 0), sp=(cc == NCC - 1):
                   e.matmul(o, l, r, start=st, stop=sp),
                   "t" if cc == NCC - 1 else None)
            ka_t = snap("t")
            ew("scalar", "t", ka_t)
            ei("scalar", lambda e: e.copy(KAb[:, :], PB[5][0:8, 0:TA]), "s")
            kab_s = snap("s")
            ew("vector", "t", ka_t)
            ew("vector", "s", kab_s)
            ei("vector", lambda e: e.tensor_copy(lgsb[:, :], PB[5][0:8, TA:2 * TA]), "v")

            ew("tensor", "s", kab_s)
            for dc in range(NCC):
                bi = 3 + (dc % 2)
                ew("tensor", bank_free[bi])
                for cc in range(NCC):
                    ei("tensor",
                       lambda e, l=lhs_w(W0B, cc, dc), r=KB[:, cc * TA:(cc + 1) * TA],
                       o=PB[bi][:, 0:TA], st=(cc == 0):
                       e.matmul(o, l, r, start=st, stop=False))
                ei("tensor",
                   lambda e, l=biniB[:, dc * 128:(dc + 1) * 128], o=PB[bi][:, 0:TA]:
                   e.matmul(o, l, KAb[:, :], start=False, stop=True), "t")
                y_t = snap("t")
                ew("vector", "t", y_t)
                ei("vector", lambda e, c=dc, b2=bi: e.tensor_sub(ch(GRAD, c), PB[b2][:, 0:TA], ch(V32, c)))
                ei("vector", lambda e, c=dc: e.tensor_mul(ch(GRAD, c), ch(GRAD, c), ch(K32, c)))
                ei("vector", lambda e, c=dc: e.tensor_reduce(rcol[:, 0:1], ch(GRAD, c), AX, ALU.add))
                ei("vector", lambda e, c=dc: e.tensor_add(gtot[:, c:c + 1], gtot[:, c:c + 1], rcol[:, 0:1]), "v")
                bank_free[bi] = ("v", snap("v"))

            ew("vector", "t", lg_t)
            ew("tensor", "v", snap("v"))
            for hf in range(2):
                ei("tensor",
                   lambda e, h2=hf, r=lgsb[0:8, 0:128] if hf == 0 else lgsb[0:8, 128:256],
                   o=PB[6][0:128, 140 + hf * 8:148 + hf * 8]:
                   e.transpose(o, r, ident[0:8, 0:8]),
                   "t" if hf == 1 else None)
            tp_t = snap("t")
            ew("vector", "t", tp_t)
            vd(lambda e: e.tensor_copy(lgt[:, 0:16], PB[6][0:128, 140:156]))
            routing_tr_t = 0
            for hf in range(2):
                lg8 = lgt[:, hf * 8:(hf + 1) * 8]
                m1 = pk[:, 16:17]; m2 = pk[:, 17:18]; ssum = pk[:, 18:19]
                nm = rcol[:, 1:2]
                prb = rtmp[:, 0:8]; ohc = rtmp[:, 8:16]
                eq1 = rtmp[:, 16:24]; pr2 = rtmp[:, 24:32]
                oh1 = pk[:, 0:8]; oh2 = pk[:, 8:16]
                if hf == 1:
                    ew("vector", "t", routing_tr_t)
                vd(lambda e, a=lg8, m=m1: e.reduce_max(m, a, AX))
                vd(lambda e, m=m1, n=nm: e.tensor_sub(n, zcol[:, :], m))
                ew("scalar", "v", snap("v"))
                ei("scalar", lambda e, a=lg8, p=prb, n=nm: e.activation(p, a, AF.Exp, bias=n), "s")
                ew("vector", "s", snap("s"))
                vd(lambda e, p=prb, s_=ssum: e.tensor_reduce(s_, p, AX, ALU.add))
                vd(lambda e, p=prb, m=m1: e.reduce_max(m, p, AX))
                vd(lambda e, p=prb, m=m1, o=oh1: e.tensor_scalar(o, p, m, 0.0, ALU.subtract, ALU.is_ge))
                vd(lambda e, o=oh1, c2=ohc: e.tensor_tensor_scan(c2, ones512[:, 0:8], o, 0.0, ALU.mult, ALU.add))
                vd(lambda e, c2=ohc, q=eq1: e.tensor_scalar(q, c2, 1.0, 0.0, ALU.subtract, ALU.is_equal))
                vd(lambda e, o=oh1, q=eq1: e.tensor_mul(o, o, q))
                vd(lambda e, o=oh1, p=prb, p2=pr2: e.scalar_tensor_tensor(p2, o, -1e30, p, ALU.mult, ALU.add))
                vd(lambda e, p2=pr2, m=m2: e.reduce_max(m, p2, AX))
                vd(lambda e, p2=pr2, m=m2, o=oh2: e.tensor_scalar(o, p2, m, 0.0, ALU.subtract, ALU.is_ge))
                vd(lambda e, o=oh2, c2=ohc: e.tensor_tensor_scan(c2, ones512[:, 0:8], o, 0.0, ALU.mult, ALU.add))
                vd(lambda e, c2=ohc, q=eq1: e.tensor_scalar(q, c2, 1.0, 0.0, ALU.subtract, ALU.is_equal))
                vd(lambda e, o=oh2, q=eq1: e.tensor_mul(o, o, q))
                pk_v = snap("v")
                ew("tensor", "v", pk_v)
                ei("tensor", lambda e: e.transpose(PB[6][0:8, 0:128], pk[:, 0:8], ident[:, :]))
                ei("tensor", lambda e: e.transpose(PB[6][0:8, 128:256], pk[:, 8:16], ident[:, :]))
                ei("tensor", lambda e: e.transpose(PB[6][0:1, 256:384], pk[:, 16:17], ident[:, :]))
                ei("tensor", lambda e: e.transpose(PB[6][0:1, 384:512], pk[:, 17:18], ident[:, :]))
                ei("tensor", lambda e: e.transpose(PB[7][0:1, 0:128], pk[:, 18:19], ident[:, :]), "t")
                tr_t = snap("t")
                ew("vector", "t", tr_t)
                ei("vector",
                   lambda e, h2=hf: e.tensor_copy(ohst[:, h2 * 128:(h2 + 1) * 128],
                                                  PB[6][0:8, 0:128]))
                ei("vector",
                   lambda e, h2=hf: e.tensor_copy(ohst[:, TA + h2 * 128:TA + (h2 + 1) * 128],
                                                  PB[6][0:8, 128:256]))
                ei("vector",
                   lambda e, h2=hf: e.tensor_copy(w12t[0:1, h2 * 128:(h2 + 1) * 128],
                                                  PB[6][0:1, 256:384]))
                ei("vector",
                   lambda e, h2=hf: e.tensor_copy(w12t[0:1, TA + h2 * 128:TA + (h2 + 1) * 128],
                                                  PB[6][0:1, 384:512]))
                ei("vector",
                   lambda e, h2=hf: e.tensor_copy(w12t[0:1, 2 * TA + h2 * 128:2 * TA + (h2 + 1) * 128],
                                                  PB[7][0:1, 0:128]), "v")
                if hf == 0:
                    routing_tr_t = tr_t
            vd(lambda e: e.tensor_reduce(rcol[0:8, 0:1], ohst[0:8, 0:TA], AX, ALU.add))
            vd(lambda e: e.tensor_add(cnttot[:, 0:1], cnttot[:, 0:1], rcol[0:8, 0:1]))
            vd(lambda e: e.tensor_reduce(rcol[0:8, 0:1], ohst[0:8, TA:2 * TA], AX, ALU.add))
            vd(lambda e: e.tensor_add(cnttot[:, 1:2], cnttot[:, 1:2], rcol[0:8, 0:1]))
            oh_v = snap("v")
            ew("sync", "v", oh_v)
            dma("sync", ohs_d[:, 2 * i * TA:2 * (i + 1) * TA], ohst[:, :], sst)
            dma("sync", w12_d[0:1, 3 * i * TA:3 * (i + 1) * TA], w12t[:, :], sst)

            store_wm[i] = snapw(sst)
            vfree[("a1x", i)] = snap("v")
            vfree[("a1xbt", i)] = snap("t")

        a1_t, a1_v, a1_s = snap("t"), snap("v"), snap("s")
        ew("sync", "v", a1_v)
        dma("sync", grow_d[0:1, 0:1024].rearrange("o (cc p) -> (o p) cc", p=128), gtot[:, :], "st0", slow=True)
        dma("sync", grow_d[0:1, 1024:1040].rearrange("o (u k) -> (o k) u", u=2), cnttot[:, :], "st0", slow=True)
        ew("sync", snapw("st0"))
        dma("sync", myrow[:, :], grow_d[:, :], "x0")
        myrow_x = snapw("x0")
        ew("tensor", myrow_x)
        ew("tensor", "v", a1_v)
        ei("tensor", lambda e: e.matmul(PB[6][0:8, 0:512], onehotS[:, :], myrow[0:1, 0:512], start=True, stop=True))
        ei("tensor", lambda e: e.matmul(PB[7][0:8, 0:512], onehotS[:, :], myrow[0:1, 512:1024], start=True, stop=True))
        ei("tensor", lambda e: e.matmul(PB[5][0:8, 0:16], onehotS[:, :], myrow[0:1, 1024:1040], start=True, stop=True), "t")
        ew("vector", "t", snap("t"))
        ei("vector", lambda e: e.tensor_copy(contrib[:, 0:512], PB[6][0:8, 0:512]))
        ei("vector", lambda e: e.tensor_copy(contrib[:, 512:1024], PB[7][0:8, 0:512]))
        ei("vector", lambda e: e.tensor_copy(contrib[:, 1024:1040], PB[5][0:8, 0:16]), "v")
        ew("gpsimd", "v", snap("v"))
        dma("gpsimd", ccin_d[:, :], contrib[:, :], "g0")
        ew("gpsimd", snapw("g0"))
        ei("gpsimd", lambda e: e.collective_compute(
            "AllReduce", ALU.add, replica_groups=[list(range(NCORE))],
            ins=[ccin_d], outs=[ccout_d]), "cc")
        ew("gpsimd", "cc", snap("cc"))
        dma("gpsimd", gathered[:, :], ccout_d[:, :], "g1")
        gath_g = snapw("g1")
        ew("tensor", gath_g)
        ei("tensor", lambda e: e.matmul(PB[6][0:1, 0:512], maskgS[:, :], gathered[:, 0:512], start=True, stop=True))
        ei("tensor", lambda e: e.matmul(PB[7][0:1, 0:512], maskgS[:, :], gathered[:, 512:1024], start=True, stop=True))
        ei("tensor", lambda e: e.matmul(PB[5][0:1, 0:16], maskcS[:, :], gathered[:, 1024:1040], start=True, stop=True), "t")
        ew("vector", "t", snap("t"))
        ei("vector", lambda e: e.tensor_copy(prows[0:1, 0:512], PB[6][0:1, 0:512]))
        ei("vector", lambda e: e.tensor_copy(prows[0:1, 512:1024], PB[7][0:1, 0:512]))
        ei("vector", lambda e: e.tensor_copy(prows[0:1, 1024:1040], PB[5][0:1, 0:16]), "v")
        ew("sync", "v", snap("v"))
        dma("sync", prow_d[:, :], prows[:, :], "st1")
        ew("sync", snapw("st1"))
        dma("sync", gprefS[:, :], prow_d[0:1, 0:1024].rearrange("o (cc p) -> (o p) cc", p=128), "x1", slow=True)
        dma("sync", cprefS[:, :], prow_d[0:1, 1024:1040].rearrange("o (u k) -> (o k) u", u=2), "x1", slow=True)
        pref_x = snapw("x1")
        ew("vector", pref_x)
        ei("vector", lambda e: e.tensor_copy(carryg[:, :], gprefS[:, :]))
        ei("vector", lambda e: e.tensor_copy(ccarry[:, :], cprefS[:, :]), "v")
        exch_v = snap("v")

        ew("tensor", "v", exch_v)
        ei("tensor", lambda e: e.nop(), "t")
        qch_guard = {0: ("s", 0), 1: ("s", 0)}
        a2store_wm = {}
        a2bc_wm = {}
        for i in range(NTA):
            b = i % 2
            sx = f"x{i % 4}"
            sst = f"st{i % 4}"
            X = X0 if b == 0 else X1
            XB = XBF0 if b == 0 else XBF1
            if i < 2:
                ew("sync", "v", exch_v)
            else:
                ew("sync", "v", vfree.get(("a2x", i - 2), 0))
            dma("sync", X.rearrange("p (cc n) -> p cc n", n=TA), d["x"][:, i, :, :], sx)
            ew("sync", "t", a2bc_wm.get(i - 2, 0))
            dma("sync", RBs[i % 2][:, :], rows_d[0:1, i * 1024:(i + 1) * 1024], sx)
            ew("sync", "v", vfree.get(("a2x", i - 1), 0))
            dma("sync", ohst2[:, :], ohs_d[:, 2 * i * TA:2 * (i + 1) * TA], sx)
            dma("sync", w12t[:, :], w12_d[0:1, 3 * i * TA:3 * (i + 1) * TA], sx)
            x_x = snapw(sx)

            ew("vector", a2store_wm.get(i - 1))
            ew("tensor", x_x)
            ew("tensor", "v", vfree.get(("a2pb1", i - 1), 0))
            bcast_row(rbv(i, 0), PB[1][:, 0:TA])
            bcast_row(rbv(i, 2), PB[1][:, TA:2 * TA])
            ew("tensor", "v", vfree.get(("a2pb2", i - 1), 0))
            bcast_row(rbv(i, 1), PB[2][:, 0:TA])
            bcast_row(rbv(i, 3), PB[2][:, TA:2 * TA], "t")
            bc_t = snap("t")
            a2bc_wm[i] = bc_t

            ew("vector", "t", bc_t)
            ew("vector", x_x)
            for cc in range(NCC):
                ei("vector", lambda e, c=cc, xx=X: e.tensor_mul(ch(XN32, c), ch(xx, c), PB[1][:, 0:TA]))
            for cc in range(NCC):
                ei("vector", lambda e, c=cc: e.tensor_sub(ch(XN32, c), ch(XN32, c), PB[1][:, TA:2 * TA]),
                   "v" if cc == NCC - 1 else None)
            xsub_v = snap("v")
            ew("scalar", "v", xsub_v)
            ei("scalar", lambda e, xb=XB: e.copy(xb[:, :], XN32[:, :]), "s")
            xnbf_s2 = snap("s")
            ew("sync", "s", xnbf_s2)
            dma("sync", xnbf_d[:, :, i * TA:(i + 1) * TA],
                XB.rearrange("p (cc n) -> p cc n", n=TA), sst)
            for cc in range(NCC):
                ei("vector", lambda e, c=cc: e.tensor_mul(ch(GRAD, c), ch(XN32, c), PB[2][:, 0:TA]),
                   "v" if cc == NCC - 1 else None)
            h32_v = snap("v")
            vfree[("a2pb1", i)] = h32_v
            ew("scalar", "v", h32_v)
            for cc in range(NCC):
                ei("scalar",
                   lambda e, c=cc: e.activation(HB[:, c * TA:(c + 1) * TA], ch(GRAD, c), AF.Identity,
                                                bias=lnbS[:, c:c + 1], scale=lngS[:, c:c + 1]),
                   "s" if cc == NCC - 1 else None)
            hb_s = snap("s")

            ew("tensor", "s", hb_s)
            bank_free = {3: ("t", 0), 4: ("t", 0)}
            for w, kind in ((WK, "K"), (WV, "V")):
                for dc in range(NCC):
                    bi = 3 + (dc % 2)
                    which, val = bank_free[bi]
                    ew("tensor", which, val)
                    for cc in range(NCC):
                        ei("tensor",
                           lambda e, l=lhs_w(w, cc, dc), r=HB[:, cc * TA:(cc + 1) * TA],
                           o=PB[bi][:, 0:TA], st=(cc == 0), sp=(cc == NCC - 1):
                           e.matmul(o, l, r, start=st, stop=sp),
                           "t" if cc == NCC - 1 else None)
                    mm_t = snap("t")
                    ew("vector", "t", mm_t)
                    if kind == "K":
                        ei("vector", lambda e, c=dc, b2=bi: e.tensor_copy(ch(K32, c), PB[b2][:, 0:TA]))
                        ei("vector", lambda e, c=dc, b2=bi: e.tensor_copy(KB[:, c * TA:(c + 1) * TA], PB[b2][:, 0:TA]), "v")
                    else:
                        ei("vector", lambda e, c=dc, b2=bi: e.tensor_copy(ch(V32, c), PB[b2][:, 0:TA]), "v")
                    bank_free[bi] = ("v", snap("v"))
            kv_v = snap("v")

            ew("tensor", "v", kv_v)
            for cc in range(NCC):
                ei("tensor",
                   lambda e, l=ainiB[:, cc * 8:(cc + 1) * 8], r=KB[:, cc * TA:(cc + 1) * TA],
                   o=PB[5][0:8, 0:TA], st=(cc == 0), sp=(cc == NCC - 1):
                   e.matmul(o, l, r, start=st, stop=sp),
                   "t" if cc == NCC - 1 else None)
            ka_t = snap("t")
            ew("vector", "t", ka_t)
            ei("vector", lambda e: e.tensor_copy(KAb[:, :], PB[5][0:8, 0:TA]), "v")
            kab_v = snap("v")

            ew("tensor", "v", kab_v)
            for dc in range(NCC):
                bi = 3 + (dc % 2)
                which, val = bank_free[bi]
                ew("tensor", which, val)
                for cc in range(NCC):
                    ei("tensor",
                       lambda e, l=lhs_w(W0B, cc, dc), r=KB[:, cc * TA:(cc + 1) * TA],
                       o=PB[bi][:, 0:TA], st=(cc == 0):
                       e.matmul(o, l, r, start=st, stop=False))
                ei("tensor",
                   lambda e, l=biniB[:, dc * 128:(dc + 1) * 128], o=PB[bi][:, 0:TA]:
                   e.matmul(o, l, KAb[:, :], start=False, stop=True), "t")
                y_t = snap("t")
                ew("vector", "t", y_t)
                ei("vector", lambda e, c=dc, b2=bi: e.tensor_copy(ch(Y032, c), PB[b2][:, 0:TA]))
                ei("vector", lambda e, c=dc, b2=bi: e.tensor_sub(ch(GRAD, c), PB[b2][:, 0:TA], ch(V32, c)))
                ei("vector", lambda e, c=dc: e.tensor_mul(ch(GRAD, c), ch(GRAD, c), ch(K32, c)), "v")
                bank_free[bi] = ("v", snap("v"))

            ew("tensor", "v", snap("v"))
            for cc in range(NCC):
                vd(lambda e, c=cc: e.tensor_tensor_scan(ch(V32, c), ones512[:, 0:TA], ch(GRAD, c),
                                                        carryg[:, c:c + 1], ALU.mult, ALU.add))
                vd(lambda e, c=cc: e.tensor_copy(carryg[:, c:c + 1], ch(V32, c)[:, TA - 1:TA]))
                ei("vector",
                   lambda e, c=cc: e.scalar_tensor_tensor(ch(GRAD, c), ch(V32, c), 1.0, ch(GRAD, c),
                                                          ALU.mult, ALU.subtract),
                   "v" if cc == NCC - 1 else None)
            scan_v = snap("v")
            for dc in range(NCC):
                bi = 3 + (dc % 2)
                which, val = bank_free[bi]
                ew("tensor", which, val)
                for cc in range(NCC):
                    ei("tensor",
                       lambda e, l=lhs_w(WG, cc, dc), r=XB[:, cc * TA:(cc + 1) * TA],
                       o=PB[bi][:, 0:TA], st=(cc == 0), sp=(cc == NCC - 1):
                       e.matmul(o, l, r, start=st, stop=sp),
                       "t" if cc == NCC - 1 else None)
                mm_t = snap("t")
                ew("scalar", "t", mm_t)
                ei("scalar", lambda e, c=dc, b2=bi: e.activation(ch(K32, c), PB[b2][:, 0:TA], AF.Sigmoid), "s")
                bank_free[bi] = ("s", snap("s"))
            gate_s = snap("s")

            ew("vector", "s", gate_s)
            for dc in range(NCC):
                bi = 3 + (dc % 2)
                which, val = bank_free[bi]
                ew("tensor", which, val)
                for cc in range(NCC):
                    ei("tensor",
                       lambda e, l=lhs_w(WQ, cc, dc), r=HB[:, cc * TA:(cc + 1) * TA],
                       o=PB[bi][:, 0:TA], st=(cc == 0), sp=(cc == NCC - 1):
                       e.matmul(o, l, r, start=st, stop=sp),
                       "t" if cc == NCC - 1 else None)
                mm_t = snap("t")
                q = QCH[:, (dc % 2) * TA:(dc % 2 + 1) * TA]
                which2, val2 = qch_guard[dc % 2]
                ew("scalar", which2, val2)
                ew("scalar", "t", mm_t)
                ei("scalar", lambda e, c=dc, b2=bi, q2=q: e.activation(q2, PB[b2][:, 0:TA], AF.Identity,
                                                                      scale=lrS[:, c:c + 1]), "s")
                q_s = snap("s")
                bank_free[bi] = ("s", q_s)
                ew("vector", "s", q_s)
                ei("vector", lambda e, q2=q: e.tensor_mul(q2, q2, PB[2][:, TA:2 * TA]))
                ei("vector", lambda e, c=dc, q2=q: e.tensor_mul(q2, q2, ch(GRAD, c)))
                ei("vector", lambda e, c=dc, q2=q: e.scalar_tensor_tensor(ch(Y032, c), q2, -1.0,
                                                                          ch(Y032, c), ALU.mult, ALU.add))
                ei("vector", lambda e, c=dc: e.tensor_mul(OG[:, c * TA:(c + 1) * TA], ch(Y032, c), ch(K32, c)), "v")
                qch_guard[dc % 2] = ("v", snap("v"))
            og_v = snap("v")
            vfree[("a2pb2", i)] = og_v

            ew("tensor", "v", og_v)
            for dc in range(NCC):
                bi = 3 + (dc % 2)
                which, val = bank_free[bi]
                ew("tensor", which, val)
                for cc in range(NCC):
                    ei("tensor",
                       lambda e, l=lhs_w(WO, cc, dc), r=OG[:, cc * TA:(cc + 1) * TA],
                       o=PB[bi][:, 0:TA], st=(cc == 0), sp=(cc == NCC - 1):
                       e.matmul(o, l, r, start=st, stop=sp),
                       "t" if cc == NCC - 1 else None)
                mm_t = snap("t")
                ew("scalar", "t", mm_t)
                ei("scalar", lambda e, c=dc, b2=bi: e.activation(ch(Y032, c), PB[b2][:, 0:TA], AF.Identity,
                                                                 scale=lsgS[:, c:c + 1]), "s")
                bank_free[bi] = ("s", snap("s"))
            wo_s = snap("s")
            ew("vector", "s", wo_s)
            ei("vector", lambda e, xx=X: e.tensor_add(Y032[:, :], Y032[:, :], xx[:, :]))
            ei("vector", lambda e: e.tensor_add(Y032[:, :], Y032[:, :], XN32[:, :]), "v")
            racc_v = snap("v")
            ew("sync", "v", racc_v)
            dma("sync", racc_d[:, :, i * TA:(i + 1) * TA],
                Y032.rearrange("p (cc n) -> p cc n", n=TA), sst)

            ew("vector", x_x)
            for k2 in range(2):
                sl = slice(k2 * TA, (k2 + 1) * TA)
                vd(lambda e, s2_=sl, k3=k2: e.tensor_tensor_scan(keepS[:, s2_], ones512[0:8, 0:TA],
                                                                 ohst2[:, s2_], ccarry[:, k3:k3 + 1],
                                                                 ALU.mult, ALU.add))
                vd(lambda e, k3=k2: e.tensor_copy(
                    ccarry[:, k3:k3 + 1], keepS[:, (k3 + 1) * TA - 1:(k3 + 1) * TA]))
                vd(lambda e, s2_=sl: e.tensor_scalar(keepS[:, s2_], keepS[:, s2_],
                                                     CAPP1, 0.0, ALU.subtract, ALU.is_lt))
                vd(lambda e, s2_=sl: e.tensor_mul(keepS[:, s2_], keepS[:, s2_],
                                                  ohst2[:, s2_]))
            keep_v = snap("v")
            ew("tensor", "v", keep_v)
            ei("tensor", lambda e: e.matmul(PB[5][0:1, 0:TA], ones512[0:8, 0:1],
                                            keepS[:, 0:TA], start=True, stop=True))
            ei("tensor", lambda e: e.matmul(PB[5][0:1, TA:2 * TA], ones512[0:8, 0:1],
                                            keepS[:, TA:2 * TA], start=True, stop=True), "t")
            sel_t = snap("t")
            ew("vector", "t", sel_t)
            vd(lambda e: e.tensor_add(rdenT[:, :], w12t[0:1, 0:TA], w12t[0:1, TA:2 * TA]))
            vd(lambda e: e.scalar_tensor_tensor(rdenT[:, :], w12t[0:1, 2 * TA:3 * TA], 1e-9,
                                                rdenT[:, :], ALU.mult, ALU.add))
            vd(lambda e: e.reciprocal(rdenT[:, :], rdenT[:, :]))
            vd(lambda e: e.tensor_mul(kwS[0:1, 0:TA], PB[5][0:1, 0:TA], w12t[0:1, 0:TA]))
            vd(lambda e: e.tensor_mul(kwS[0:1, TA:2 * TA], PB[5][0:1, TA:2 * TA], w12t[0:1, TA:2 * TA]))
            vd(lambda e: e.tensor_mul(kwS[0:1, 0:TA], kwS[0:1, 0:TA], rdenT[:, :]))
            vd(lambda e: e.tensor_mul(kwS[0:1, TA:2 * TA], kwS[0:1, TA:2 * TA], rdenT[:, :]))
            kw_v = snap("v")
            ew("tensor", "v", kw_v)
            ei("tensor",
               lambda e: e.matmul(PB[7][0:8, 0:TA], onesrow[0:1, 0:8], kwS[0:1, 0:TA],
                                  start=True, stop=True))
            ei("tensor",
               lambda e: e.matmul(PB[7][0:8, TA:2 * TA], onesrow[0:1, 0:8], kwS[0:1, TA:2 * TA],
                                  start=True, stop=True), "t")
            kwb_t = snap("t")
            ew("vector", "t", kwb_t)
            vd(lambda e: e.tensor_mul(combt[:, :], ohst2[:, 0:TA], PB[7][0:8, 0:TA]))
            vd(lambda e: e.tensor_mul(keepS[:, 0:TA], ohst2[:, TA:2 * TA], PB[7][0:8, TA:2 * TA]))
            vd(lambda e: e.tensor_add(combt[:, :], combt[:, :], keepS[:, 0:TA]))
            comb_v = snap("v")
            ew("sync", "v", comb_v)
            dma("sync", comb_d[:, i * TA:(i + 1) * TA], combt[:, :], sst)

            a2store_wm[i] = snapw(sst)
            vfree[("a2x", i)] = snap("v")
            vfree[("a2xt", i)] = snap("t")

        a2_t, a2_v, a2_s = snap("t"), snap("v"), snap("s")
        a2_st = [snapw(f"st{j}") for j in range(4)]
        ew("tensor", "v", a2_v)
        ew("tensor", "s", a2_s)
        ei("tensor", lambda e: e.nop(), "t")
        moe_barrier_t = snap("t")
        ew("vector", "t", moe_barrier_t)
        ew("scalar", "t", moe_barrier_t)
        ew("gpsimd", "t", moe_barrier_t)
        ew("sync", "t", moe_barrier_t)
        for wm in a2_st:
            ew("sync", wm)

        slot_free = {j: ("t", moe_barrier_t) for j in range(5)}
        gu_bank_guard = {0: ("s", 0), 1: ("s", 0)}
        u_bank_guard = {0: ("v", 0), 1: ("v", 0)}
        y_bank_guard = {0: ("v", 0), 1: ("v", 0)}
        xnp_guard = 0
        combE_guard = [0]
        xnq_guard = 0
        gu_read_guard = {0: 0, 1: 0}
        out_store_st = None
        prev_pass_v = a2_v

        def emit_w_load(p, ve):
            e_i, half = ve // 2, ve % 2
            gsl, usl, dsl = (3 * ve) % 5, (3 * ve + 1) % 5, (3 * ve + 2) % 5
            for sl, src in ((gsl, d["gt"][e_i, :, :, half * HH:(half + 1) * HH]),
                            (usl, d["ut"][e_i, :, :, half * HH:(half + 1) * HH]),
                            (dsl, d["dtw"][e_i, :, half * 8:(half + 1) * 8, :])):
                which, val = slot_free[sl]
                ew("gpsimd", which, val)
                dma("gpsimd", WSL[sl].rearrange("p (cc n) -> p cc n", n=C), src, f"g{ve % 4}")
            return snapw(f"g{ve % 4}")

        ld_wm = {}
        fold_wm = {}

        def emit_fold(p, e_i, pass_x, xnq_x):
            ceb = combE[0:1, (e_i % 2) * PTOK:(e_i % 2 + 1) * PTOK]
            sce = f"x{2 + e_i % 2}"
            ew("sync", "t", combE_guard[e_i % 2])
            dma("sync", ceb, comb_d[e_i:e_i + 1, p * PTOK:(p + 1) * PTOK], sce)
            ce_x = snapw(sce)
            for tl in range(2):
                bank = PB[6] if tl == 0 else PB[7]
                ew("tensor", moefold_bank_guard[tl])
                ew("tensor", ce_x)
                ei("tensor",
                   lambda e, r=ceb[0:1, tl * TM:(tl + 1) * TM], bk=bank:
                   e.matmul(bk[:, 0:TM], onesrow[:, :], r, start=True, stop=True), "t")
                cb_t = snap("t")
                ew("vector", "t", cb_t)
                ew("vector", xnq_x)
                ew("vector", "t", xnp_guard)
                for cc in range(NCC):
                    ei("vector",
                       lambda e, c=cc, t2=tl, bk=bank: e.tensor_mul(
                           mch(XNP[t2], c), mch(XNQ[t2], c), bk[:, 0:TM]),
                       "v" if cc == NCC - 1 else None)
                moefold_bank_guard[tl] = ("v", snap("v"))
            combE_guard[e_i % 2] = snap("t")
            return snap("v")

        moefold_bank_guard = {0: ("t", 0), 1: ("t", 0)}
        combE_guard = {0: 0, 1: 0}
        for p in range(NPASS):
            ew("sync", "t", xnq_guard)
            svx = "x1"
            for tl in range(2):
                dma("sync", XNQ[tl].rearrange("p (cc n) -> p cc n", n=TM),
                    xnbf_d[:, :, p * PTOK + tl * TM:p * PTOK + (tl + 1) * TM], svx)
            xnq_x = snapw(svx)
            if p == 0:
                ld_wm[(0, 0)] = emit_w_load(0, 0)
            fold_wm[(p, 0)] = emit_fold(p, 0, None, xnq_x)
            ew("sync", out_store_st)
            dma("sync", OUT32.rearrange("p (cc n) -> p cc n", n=PTOK),
                racc_d[:, :, p * PTOK:(p + 1) * PTOK], "x0")
            pass_x = snapw("x0")

            for ve in range(16):
                e_i, half = ve // 2, ve % 2
                gsl, usl, dsl = (3 * ve) % 5, (3 * ve + 1) % 5, (3 * ve + 2) % 5
                GWs, UWs, DWs = WSL[gsl], WSL[usl], WSL[dsl]
                xnp_v = fold_wm[(p, e_i)]

                ew("tensor", ld_wm[(p, ve)])
                ew("tensor", xnq_x)
                if half == 0:
                    ew("tensor", "v", xnp_v)
                gu_done = {}
                for tl in range(2):
                    ew("tensor", "t", gu_read_guard[tl])
                    for hc in range(NCC):
                        pbi = hc % 2
                        which, val = gu_bank_guard[pbi]
                        ew("tensor", which, val)
                        for cc in range(NCC):
                            ei("tensor",
                               lambda e, l=GWs[:, cc * C + hc * 128:cc * C + hc * 128 + 128],
                               r=mch(XNQ[tl], cc), o=PB[pbi][:, 0:TM],
                               st=(cc == 0), sp=(cc == NCC - 1):
                               e.matmul(o, l, r, start=st, stop=sp))
                        which, val = u_bank_guard[pbi]
                        ew("tensor", which, val)
                        for cc in range(NCC):
                            ei("tensor",
                               lambda e, l=UWs[:, cc * C + hc * 128:cc * C + hc * 128 + 128],
                               r=mch(XNP[tl], cc), o=PB[2 + pbi][:, 0:TM],
                               st=(cc == 0), sp=(cc == NCC - 1):
                               e.matmul(o, l, r, start=st, stop=sp),
                               "t" if cc == NCC - 1 else None)
                        gu_t = snap("t")
                        ew("scalar", "t", gu_t)
                        ei("scalar",
                           lambda e, b2=pbi, t2=tl, h2=hc:
                           e.activation(GU[t2][:, h2 * TM:(h2 + 1) * TM],
                                        PB[b2][:, 0:TM], AF.Silu), "s")
                        sg_s = snap("s")
                        gu_bank_guard[pbi] = ("s", sg_s)
                        ew("vector", "s", sg_s)
                        ei("vector",
                           lambda e, b2=pbi, t2=tl, h2=hc:
                           e.tensor_mul(GU[t2][:, h2 * TM:(h2 + 1) * TM],
                                        GU[t2][:, h2 * TM:(h2 + 1) * TM], PB[2 + b2][:, 0:TM]), "v")
                        u_bank_guard[pbi] = ("v", snap("v"))
                    gu_done[tl] = snap("v")
                xnq_guard = snap("t")
                if half == 1:
                    xnp_guard = snap("t")

                slot_free[gsl] = ("t", snap("t"))
                slot_free[usl] = ("t", snap("t"))
                nxt = (p, ve + 1) if ve + 1 < 16 else ((p + 1, 0) if p + 1 < NPASS else None)
                if half == 1 and e_i + 1 < NE:
                    fold_wm[(p, e_i + 1)] = emit_fold(p, e_i + 1, pass_x, xnq_x)

                for tl in range(2):
                    ew("tensor", "v", gu_done[tl])
                    for dc in range(NCC):
                        pbi = dc % 2
                        which, val = y_bank_guard[pbi]
                        ew("tensor", which, val)
                        for hc in range(NCC):
                            ei("tensor",
                               lambda e, l=DWs[:, hc * C + dc * 128:hc * C + dc * 128 + 128],
                               r=GU[tl][:, hc * TM:(hc + 1) * TM], o=PB[4 + pbi][:, 0:TM],
                               st=(hc == 0), sp=(hc == NCC - 1):
                               e.matmul(o, l, r, start=st, stop=sp),
                               "t" if hc == NCC - 1 else None)
                        y_t = snap("t")
                        ew("vector", "t", y_t)
                        ew("vector", pass_x)
                        ei("vector",
                           lambda e, c=dc, t2=tl, b2=pbi:
                           e.tensor_add(OUT32[:, c * PTOK + t2 * TM:c * PTOK + t2 * TM + TM],
                                        OUT32[:, c * PTOK + t2 * TM:c * PTOK + t2 * TM + TM],
                                        PB[4 + b2][:, 0:TM]), "v")
                        y_bank_guard[pbi] = ("v", snap("v"))
                    gu_read_guard[tl] = snap("t")
                slot_free[dsl] = ("t", snap("t"))
                if nxt is not None:
                    ld_wm[nxt] = emit_w_load(*nxt)

            prev_pass_v = snap("v")
            ew("sync", "v", prev_pass_v)
            dma("sync", out_d[:, :, p * PTOK:(p + 1) * PTOK],
                OUT32.rearrange("p (cc n) -> p cc n", n=PTOK), f"st{p % 4}")
            out_store_st = snapw(f"st{p % 4}")

        for j in range(4):
            ew("sync", snapw(f"st{j}"))
            ew("sync", snapw(f"x{j}"))
            ew("sync", snapw(f"g{j}"))

        block = ES.enter_context(nc.Block())

        @block.tensor
        def _(e):
            for f in prog["tensor"]:
                f(e)

        @block.vector
        def _(e):
            for f in prog["vector"]:
                f(e)

        @block.scalar
        def _(e):
            for f in prog["scalar"]:
                f(e)

        @block.sync
        def _(e):
            for f in prog["sync"]:
                f(e)

        @block.gpsimd
        def _(e):
            for f in prog["gpsimd"]:
                f(e)

    return nc


def _prep_shared(inputs):
    f32 = np.float32
    sh = {}

    def cm_pack(w):
        return np.ascontiguousarray(w.reshape(NCC, 128, -1).transpose(1, 0, 2))

    sh["wq"] = cm_pack(np.asarray(inputs["L0_attn_Wq"], f32).T)
    sh["wk"] = cm_pack(np.asarray(inputs["L0_attn_Wk"], f32).T)
    sh["wv"] = cm_pack(np.asarray(inputs["L0_attn_Wv"], f32).T)
    sh["wg"] = cm_pack(np.asarray(inputs["L0_attn_Wg"], f32).T)
    sh["wo"] = cm_pack(np.asarray(inputs["L0_attn_Wo"], f32).T)
    sh["w0"] = cm_pack(np.asarray(inputs["L0_attn_W0"], f32))
    sh["aini"] = cm_pack(np.asarray(inputs["L0_attn_A_init"], f32))
    sh["bini"] = np.ascontiguousarray(np.asarray(inputs["L0_attn_B_init"], f32))
    sh["gw"] = cm_pack(np.asarray(inputs["L0_ffn_gate_w"], f32).T)
    for nm, key in (("lng", "L0_attn_ln_g"), ("lnb", "L0_attn_ln_b"),
                    ("lsg", "L0_attn_ls_g"), ("loglr", "L0_attn_log_lr")):
        v = np.asarray(inputs[key], f32)
        sh[nm] = np.ascontiguousarray(v.reshape(NCC, 128).T)
    gt = np.asarray(inputs["L0_ffn_gate_proj_w"], f32)
    ut = np.asarray(inputs["L0_ffn_up_proj_w"], f32)
    dw = np.asarray(inputs["L0_ffn_down_proj_w"], f32)
    sh["gt"] = np.ascontiguousarray(
        gt.transpose(0, 2, 1).reshape(NE, NCC, 128, 2048).transpose(0, 2, 1, 3))
    sh["ut"] = np.ascontiguousarray(
        ut.transpose(0, 2, 1).reshape(NE, NCC, 128, 2048).transpose(0, 2, 1, 3))
    sh["dtw"] = np.ascontiguousarray(
        dw.transpose(0, 2, 1).reshape(NE, 16, 128, 1024).transpose(0, 2, 1, 3))
    sh["ident"] = np.eye(128, dtype=f32)
    sh["ones512"] = np.ones((128, 256), f32)
    sh["onesrow"] = np.ones((1, 128), f32)
    sh["onescolb"] = np.ones((128, 1), ml_dtypes.bfloat16)

    return sh


def kernel(**inputs):
    x = np.asarray(inputs["L0_attn_x"], np.float32)
    B, T, _ = x.shape
    xf = x.reshape(B * T, C)

    sh = _prep_shared(inputs)
    nc = build_nc()

    in_maps = []
    for c in range(NCORE):
        m = dict(sh)
        shard = xf[c * TOK:(c + 1) * TOK]
        xcm = np.ascontiguousarray(shard.T)
        m["x"] = np.ascontiguousarray(
            xcm.reshape(NCC, 128, NTA, TA).transpose(1, 2, 0, 3))
        oh = np.zeros((1, 8), np.float32)
        oh[0, c] = 1.0
        m["onehot"] = oh
        mg = np.zeros((8, 1), np.float32)
        for cp in range(NCORE):
            if cp < c and cp // 4 == c // 4:
                mg[cp, 0] = 1.0
        m["maskg"] = mg
        mc = np.zeros((8, 1), np.float32)
        mc[0:c, 0] = 1.0
        m["maskc"] = mc
        pos0 = (c * TOK) % T
        m["pos"] = (np.arange(pos0 + 1, pos0 + TOK + 1, dtype=np.float64)
                    .astype(np.float32).reshape(1, TOK))
        in_maps.append(m)

    res = run_bass_kernel_spmd(nc, in_maps, core_ids=list(range(NCORE)))

    out = np.empty((B * T, C), np.float32)
    for c in range(NCORE):
        oc = np.asarray(res.results[c]["out"]).reshape(128, NCC, TOK)
        ocm = oc.transpose(1, 0, 2).reshape(C, TOK)
        out[c * TOK:(c + 1) * TOK] = ocm.T
    return out.reshape(B, T, C)


# version 11 (final)
# speedup vs baseline: 1.1988x; 1.0067x over previous
import contextlib
import numpy as np
import ml_dtypes
import concourse.bass as bass
import concourse.mybir as mybir
from concourse.bass_utils import run_bass_kernel_spmd

F32 = mybir.dt.float32
BF16 = mybir.dt.bfloat16
AF = mybir.ActivationFunctionType
ALU = mybir.AluOpType
AX = mybir.AxisListType.X

NCORE = 8
C = 1024
NCC = 8
TOK = 4096
TA = 256
NTA = 16
TM = 512
NPASS = 4
PTOK = 1024
NE = 8
HH = 1024
CAPP1 = 5121.0
EPS = 1e-5

ENGS = ("tensor", "vector", "scalar", "sync", "gpsimd")


def build_nc():
    nc = bass.Bass("TRN2", target_bir_lowering=False, detect_race_conditions=False)

    d = {}
    d["x"] = nc.dram_tensor("x", [128, NTA, NCC, TA], F32, kind="ExternalInput").ap()
    for nm in ("wq", "wk", "wv", "wg", "wo", "w0"):
        d[nm] = nc.dram_tensor(nm, [128, NCC, C], F32, kind="ExternalInput").ap()
    d["aini"] = nc.dram_tensor("aini", [128, NCC, 8], F32, kind="ExternalInput").ap()
    d["bini"] = nc.dram_tensor("bini", [8, C], F32, kind="ExternalInput").ap()
    d["gw"] = nc.dram_tensor("gw", [128, NCC, 8], F32, kind="ExternalInput").ap()
    for nm in ("lng", "lnb", "lsg", "loglr"):
        d[nm] = nc.dram_tensor(nm, [128, NCC], F32, kind="ExternalInput").ap()
    d["gt"] = nc.dram_tensor("gt", [NE, 128, NCC, 2048], F32, kind="ExternalInput").ap()
    d["ut"] = nc.dram_tensor("ut", [NE, 128, NCC, 2048], F32, kind="ExternalInput").ap()
    d["dtw"] = nc.dram_tensor("dtw", [NE, 128, 16, C], F32, kind="ExternalInput").ap()
    d["ident"] = nc.dram_tensor("ident", [128, 128], F32, kind="ExternalInput").ap()
    d["ones512"] = nc.dram_tensor("ones512", [128, 256], F32, kind="ExternalInput").ap()
    d["onesrow"] = nc.dram_tensor("onesrow", [1, 128], F32, kind="ExternalInput").ap()
    d["onescolb"] = nc.dram_tensor("onescolb", [128, 1], BF16, kind="ExternalInput").ap()

    d["onehot"] = nc.dram_tensor("onehot", [1, 8], F32, kind="ExternalInput").ap()
    d["maskg"] = nc.dram_tensor("maskg", [8, 1], F32, kind="ExternalInput").ap()
    d["maskc"] = nc.dram_tensor("maskc", [8, 1], F32, kind="ExternalInput").ap()
    d["pos"] = nc.dram_tensor("pos", [1, TOK], F32, kind="ExternalInput").ap()

    out_d = nc.dram_tensor("out", [128, NCC, TOK], F32, kind="ExternalOutput").ap()

    KIND = "ExternalOutput" if globals().get("DBG") else None
    def scratch(nm, shp, dt):
        if KIND:
            return nc.dram_tensor(nm, shp, dt, kind=KIND).ap()
        return nc.dram_tensor(nm, shp, dt).ap()
    racc_d = scratch("racc_s", [128, NCC, TOK], F32)
    xnbf_d = scratch("xnbf_s", [128, NCC, TOK], BF16)
    ohs_d = scratch("ohs_s", [8, 2 * TOK], F32)
    w12_d = scratch("w12_s", [1, 3 * TOK], F32)
    comb_d = scratch("comb_s", [8, TOK], F32)
    rows_d = scratch("rows_s", [1, NTA * 1024], F32)
    grow_d = scratch("grow_s", [1, 1040], F32)
    prow_d = scratch("prow_s", [1, 1040], F32)
    ccin_d = nc.dram_tensor("ccin_s", [8, 1040], F32).ap()
    ccout_d = nc.dram_tensor("ccout_s", [8, 1040], F32).ap()

    ES = contextlib.ExitStack()
    with ES:
        def sbuf(name, shape, dt):
            return ES.enter_context(nc.sbuf_tensor(name, shape, dt)).ap()

        def psum(name, shape, dt=F32):
            return ES.enter_context(nc.psum_tensor(name, shape, dt)).ap()

        ABW = sbuf("ABW", [128, 5 * NCC * C], BF16)
        AFA = sbuf("AFA", [128, 7 * NCC * TA], F32)
        ABS = sbuf("ABSA", [128, 16384], BF16)
        GUA = sbuf("GUA", [128, 2 * NCC * TM], BF16)
        W0B = GUA[:, 0:NCC * C]
        FREE2 = ABS[:, 10240:16384].bitcast(F32)
        QCH = sbuf("QCHS", [128, 2 * TA], F32)

        WSL = [ABW[:, j * NCC * C:(j + 1) * NCC * C] for j in range(5)]
        WQ, WK, WV, WG, WO = WSL

        def lhs_w(w, cc, dc):
            return w[:, cc * C + dc * 128: cc * C + (dc + 1) * 128]

        def afv(i):
            return AFA[:, i * NCC * TA:(i + 1) * NCC * TA]

        X0, X1, XN32, K32, V32, Y032, GRAD = (afv(i) for i in range(7))
        OUT32 = AFA[:, 0:NCC * PTOK]

        def ch(buf, cc):
            return buf[:, cc * TA:(cc + 1) * TA]

        XBF0 = ABS[:, 0:2048]
        XBF1 = ABS[:, 2048:4096]
        HB = ABS[:, 4096:6144]
        KB = ABS[:, 6144:8192]
        OG = ABS[:, 8192:10240]
        XSQ = OG
        XNQ = (ABS[:, 0:4096], ABS[:, 4096:8192])
        XNP = (ABS[:, 8192:12288], ABS[:, 12288:16384])
        GU = (GUA[:, 0:NCC * TM], GUA[:, NCC * TM:2 * NCC * TM])

        def mch(buf, cc):
            return buf[:, cc * TM:(cc + 1) * TM]

        ident = sbuf("identS", [128, 128], F32)
        ones512 = sbuf("ones512S", [128, 256], F32)
        onesrow = sbuf("onesrowS", [1, 128], F32)
        onescolb = sbuf("onescolbS", [128, 1], BF16)

        onehotS = sbuf("onehotS", [1, 8], F32)
        maskgS = sbuf("maskgS", [8, 1], F32)
        maskcS = sbuf("maskcS", [8, 1], F32)
        lngS = sbuf("lngS", [128, NCC], F32)
        lnbS = sbuf("lnbS", [128, NCC], F32)
        lsgS = sbuf("lsgS", [128, NCC], F32)
        lrS = sbuf("lrS", [128, NCC], F32)
        ainiB = sbuf("ainiB", [128, NCC * 8], BF16)
        biniB = sbuf("biniB", [8, C], BF16)
        gwB = sbuf("gwB", [128, NCC * 8], BF16)
        RT = sbuf("RT", [1, 2048], F32)
        RB = RT[0:1, 1024:2048]
        zcol = sbuf("zcol", [128, 1], F32)
        gtot = sbuf("gtot", [128, NCC], F32)
        carryg = sbuf("carryg", [128, NCC], F32)
        cnttot = sbuf("cnttot", [8, 2], F32)
        ccarry = sbuf("ccarry", [8, 2], F32)
        KAb = sbuf("KAb", [8, TA], BF16)
        lgt = sbuf("lgt", [128, 16], F32)
        pk = sbuf("pk", [128, 19], F32)
        rtmp = sbuf("rtmp", [128, 32], F32)
        rcol = sbuf("rcol", [128, 8], F32)
        ohst = FREE2[0:8, 0:512]
        ohst2 = ohst
        w12t = FREE2[0:1, 512:1280]
        keepS = FREE2[0:8, 1280:1792]
        lgsb = keepS[0:8, 0:TA]
        kwS = FREE2[0:1, 2048:2560]
        rdenT = FREE2[0:1, 2560:2816]
        combPF = sbuf("combPF", [8, 1040], F32)
        combt = combPF[0:8, 0:TA]
        combE = AFA[0:1, 4 * NCC * TA:4 * NCC * TA + 2 * PTOK]
        contrib = combPF[0:8, 0:1040]
        myrow = RT[0:1, 0:1040]
        gathered = contrib
        prows = myrow
        gprefS = sbuf("gprefS", [128, NCC], F32)
        cprefS = sbuf("cprefS", [8, 2], F32)

        PB = [psum(f"pb{i}", [128, 512]) for i in range(8)]

        S = {}
        semnames = ["t", "v", "s", "cc"]
        semnames += [f"x{j}" for j in range(4)]
        semnames += [f"st{j}" for j in range(4)]
        semnames += [f"g{j}" for j in range(4)]
        for nm in semnames:
            S[nm] = ES.enter_context(nc.semaphore("sem_" + nm))
        cnt = {k: 0 for k in S}
        prog = {e: [] for e in ENGS}

        def ew(eng, sem, val=None):
            if val is None:
                if sem is None:
                    return
                sem, val = sem
            assert val <= cnt[sem], (eng, sem, val, cnt[sem])
            if val > 0:
                prog[eng].append(lambda e, h=S[sem], v=val: e.wait_ge(h, v))

        def ei(eng, fn, sem=None, n=1):
            if sem is None:
                prog[eng].append(fn)
            else:
                cnt[sem] += n
                prog[eng].append(lambda e, f=fn, h=S[sem], k=n: f(e).then_inc(h, k))

        def dma(eng, out, in_, sem, slow=False):
            ei(eng, lambda e, o=out, i=in_, sl=slow:
               e.dma_start(out=o, in_=i, allow_slow_non_contiguous=sl), sem, 16)

        def snap(sem):
            return cnt[sem]

        def snapw(sem):
            return (sem, cnt[sem])

        R_mu = RT[0:1, 0:256]
        R_e2 = RT[0:1, 256:512]; R_var = R_e2; R_bb = R_e2
        R_aa = RT[0:1, 512:768]
        R_ra = RT[0:1, 768:1024]
        R_r1 = RT[0:1, 1024:1280]; R_s2 = RT[0:1, 1280:1536]
        R_mus = RT[0:1, 1536:1792]; R_ps = RT[0:1, 1792:2048]
        RBs = (RT[0:1, 1024:2048], RT[0:1, 0:1024])
        def rbv(i, j):
            return RBs[i % 2][0:1, j * 256:(j + 1) * 256]

        dma("sync", ident[:, :], d["ident"], "x0")
        dma("sync", ones512[:, :], d["ones512"], "x0")
        dma("sync", onesrow[:, :], d["onesrow"], "x0")

        dma("sync", onehotS[:, :], d["onehot"], "x0")
        dma("sync", maskgS[:, :], d["maskg"], "x0")
        dma("sync", maskcS[:, :], d["maskc"], "x0")
        dma("sync", lngS[:, :], d["lng"], "x0")
        dma("sync", lnbS[:, :], d["lnb"], "x0")
        dma("sync", lsgS[:, :], d["lsg"], "x0")
        dma("sync", lrS[:, :], d["loglr"], "x0")
        consts_x = snapw("x0")

        dma("gpsimd", onescolb[:, :], d["onescolb"], "g0")
        for w, nm in ((WQ, "wq"), (WK, "wk"), (WV, "wv"), (WG, "wg"), (WO, "wo"), (W0B, "w0")):
            dma("gpsimd", w.rearrange("p (cc n) -> p cc n", n=C), d[nm], "g0")
        dma("gpsimd", ainiB.rearrange("p (cc n) -> p cc n", n=8), d["aini"], "g0")
        dma("gpsimd", biniB[:, :], d["bini"], "g0")
        dma("gpsimd", gwB.rearrange("p (cc n) -> p cc n", n=8), d["gw"], "g0")
        preload_g = snapw("g0")

        ew("scalar", consts_x)
        ei("scalar", lambda e: e.activation(lrS[:, :], lrS[:, :], AF.Exp), "s")
        ew("vector", "s", snap("s"))
        ew("vector", consts_x)
        ei("vector", lambda e: e.tensor_tensor(lrS[:, :], lrS[:, :], ones512[:, 0:NCC], ALU.min))
        ei("vector", lambda e: e.memset(zcol[:, :], 0.0))
        ei("vector", lambda e: e.memset(gtot[:, :], 0.0))
        ei("vector", lambda e: e.memset(cnttot[:, :], 0.0))
        ei("vector", lambda e: e.memset(ccarry[:, :], 0.0), "v")
        prelude_v = snap("v")

        def bc_ap(bank_slice):
            return bank_slice.rearrange("p (c n) -> p c n", c=1).broadcast_to([128, NCC, TA])

        def cm3(buf):
            return buf.rearrange("p (c n) -> p c n", n=TA)

        def vd(fn):
            ei("vector", fn, "v")
            ew("vector", "v", snap("v"))

        def bcast_row(row, bank_ap, mark=None):
            ei("tensor",
               lambda e, r=row, o=bank_ap: e.matmul(o, onesrow[:, :], r, start=True, stop=True),
               mark)

        vfree = {}

        ew("tensor", preload_g)
        ew("tensor", consts_x)
        ew("vector", "v", prelude_v)
        ei("tensor", lambda e: e.nop(), "t")

        store_wm = {}
        rowsinfo = {}
        rows_bc_wm = {}

        def a1_stats(i):
            b = i % 2
            sx = f"x{i % 4}"
            sg = f"g{i % 4}"
            sst = f"st{i % 4}"
            X = X0 if b == 0 else X1
            XB = XBF0 if b == 0 else XBF1
            XSQb = XSQ
            RTb = RT
            R_mu = RTb[0:1, 0:256]
            R_e2 = RTb[0:1, 256:512]; R_var = R_e2; R_bb = R_e2
            R_aa = RTb[0:1, 512:768]
            R_ra = RTb[0:1, 768:1024]
            R_r1 = RTb[0:1, 1024:1280]; R_s2 = RTb[0:1, 1280:1536]
            R_mus = RTb[0:1, 1536:1792]; R_ps = RTb[0:1, 1792:2048]
            ew("sync", "v", vfree.get(("a1x", i - 2), 0))
            ew("sync", "t", rows_bc_wm.get(i - 1, 0))
            dma("sync", X.rearrange("p (cc n) -> p cc n", n=TA), d["x"][:, i, :, :], sx)
            dma("sync", R_ps, d["pos"][0:1, i * TA:(i + 1) * TA], sx)
            x_x = snapw(sx)
            ew("gpsimd", "v", vfree.get(("a1x", i - 2), 0))
            ew("gpsimd", "t", vfree.get(("a1xbt", i - 2), 0))
            dma("gpsimd", XB.rearrange("p (cc n) -> p cc n", n=TA), d["x"][:, i, :, :], sg)
            xb_g = snapw(sg)

            ew("vector", store_wm.get(i - 1))
            ew("scalar", xb_g)
            ew("scalar", "v", vfree.get(("a1rows", i - 1), 0))
            ew("scalar", "t", rows_bc_wm.get(i - 1, 0))
            ew("scalar", x_x)
            ew("scalar", store_wm.get(i - 1))
            ei("scalar", lambda e, xb=XB, xq=XSQb: e.activation(xq[:, :], xb[:, :], AF.Square), "s")
            ei("scalar", lambda e, rp=R_ps: e.activation(rp, rp, AF.Ln), "s")
            xsq_s = snap("s")
            ew("tensor", "s", xsq_s)
            ew("tensor", "v", vfree.get(("a1pb0", i - 1), 0))
            for src, c0 in ((XB, 0), (XSQb, TA)):
                for cc in range(NCC):
                    ei("tensor",
                       lambda e, r=src[:, cc * TA:(cc + 1) * TA], o=PB[0][0:1, c0:c0 + TA],
                       st=(cc == 0), sp=(cc == NCC - 1):
                       e.matmul(o, onescolb[:, :], r, start=st, stop=sp),
                       "t" if (cc == NCC - 1) else None)
            stats_t = snap("t")

            ew("vector", "t", stats_t)
            ei("vector", lambda e, rm=R_mu: e.tensor_scalar_mul(rm, PB[0][0:1, 0:TA], 1.0 / C))
            ei("vector", lambda e, re=R_e2: e.tensor_scalar_mul(re, PB[0][0:1, TA:2 * TA], 1.0 / C), "v")
            vfree[("a1pb0", i)] = snap("v")
            ei("vector", lambda e, ra=R_aa, rm=R_mu: e.tensor_mul(ra, rm, rm))
            ei("vector", lambda e, rv=R_var, re=R_e2, ra=R_aa: e.tensor_sub(rv, re, ra))
            ei("vector", lambda e, ra=R_aa, rv=R_var: e.tensor_scalar_add(ra, rv, EPS))
            ei("vector", lambda e, rr=R_ra, ra=R_aa: e.reciprocal(rr, ra))
            ei("vector", lambda e, rp=R_ps: e.tensor_scalar(rp, rp, 0.1, 1.0, ALU.mult, ALU.add))
            ei("vector", lambda e, rp=R_ps: e.reciprocal(rp, rp), "v")
            st_v1 = snap("v")
            ew("scalar", "v", st_v1)
            ei("scalar", lambda e, rr=R_r1, ra=R_aa: e.activation(rr, ra, AF.Ln), "s")
            ew("vector", "s", snap("s"))
            ei("vector", lambda e, rb=R_bb, rv=R_var, rr=R_ra: e.tensor_mul(rb, rv, rr))
            ei("vector", lambda e, rb=R_bb: e.tensor_scalar_add(rb, rb, EPS), "v")
            ew("scalar", "v", snap("v"))
            ei("scalar", lambda e, rr=R_r1: e.activation(rr, rr, AF.Exp, scale=-0.5), "s")
            ei("scalar", lambda e, rs=R_s2, rb=R_bb: e.activation(rs, rb, AF.Ln), "s")
            ei("scalar", lambda e, rs=R_s2: e.activation(rs, rs, AF.Exp, scale=-0.5), "s")
            rows_s = snap("s")
            ew("vector", "s", rows_s)
            ei("vector", lambda e, rm2=R_mus, rm=R_mu, rr=R_r1: e.tensor_mul(rm2, rm, rr), "v")
            rows_v = snap("v")
            vfree[("a1rows", i)] = rows_v
            ew("sync", "v", rows_v)
            dma("sync", rows_d[0:1, i * 1024:(i + 1) * 1024], RTb[0:1, 1024:2048], sst)
            rowsinfo[i] = (rows_v, R_r1, R_mus, R_s2, x_x, xb_g)

        a1_stats(0)
        for i in range(NTA):
            b = i % 2
            sst = f"st{i % 4}"
            X = X0 if b == 0 else X1
            XB = XBF0 if b == 0 else XBF1
            rows_v, R_r1, R_mus, R_s2, x_x, xb_g = rowsinfo[i]

            ew("tensor", "v", rows_v)
            ew("tensor", "v", vfree.get(("a1pb", i - 1), 0))
            bcast_row(R_r1, PB[1][:, 0:TA])
            bcast_row(R_mus, PB[1][:, TA:2 * TA])
            bcast_row(R_s2, PB[2][:, 0:TA], "t")
            bc_t = snap("t")
            rows_bc_wm[i] = bc_t

            ew("vector", "t", bc_t)
            ew("vector", x_x)
            ei("vector", lambda e, xx=X: e.tensor_mul(cm3(XN32), cm3(xx), bc_ap(PB[1][:, 0:TA])))
            ei("vector", lambda e: e.tensor_sub(cm3(XN32), cm3(XN32), bc_ap(PB[1][:, TA:2 * TA])), "v")
            xsub_v = snap("v")
            ew("scalar", "v", xsub_v)
            ei("scalar", lambda e, xb=XB: e.copy(xb[:, :], XN32[:, :]), "s")
            xnbf_s1 = snap("s")
            ei("vector", lambda e: e.tensor_mul(cm3(GRAD), cm3(XN32), bc_ap(PB[2][:, 0:TA])), "v")
            h32_v = snap("v")
            vfree[("a1pb", i)] = h32_v
            ew("scalar", "v", h32_v)
            for cc in range(NCC):
                ei("scalar",
                   lambda e, c=cc: e.activation(HB[:, c * TA:(c + 1) * TA], ch(GRAD, c), AF.Identity,
                                                bias=lnbS[:, c:c + 1], scale=lngS[:, c:c + 1]),
                   "s" if cc == NCC - 1 else None)
            hb_s = snap("s")

            if i + 1 < NTA:
                a1_stats(i + 1)

            ew("tensor", "s", hb_s)
            bank_free = {3: ("t", 0), 4: ("t", 0)}
            for w, kind in ((WK, "K"), (WV, "V")):
                for dc in range(NCC):
                    bi = 3 + (dc % 2)
                    ew("tensor", bank_free[bi])
                    for cc in range(NCC):
                        ei("tensor",
                           lambda e, l=lhs_w(w, cc, dc), r=HB[:, cc * TA:(cc + 1) * TA],
                           o=PB[bi][:, 0:TA], st=(cc == 0), sp=(cc == NCC - 1):
                           e.matmul(o, l, r, start=st, stop=sp),
                           "t" if cc == NCC - 1 else None)
                    mm_t = snap("t")
                    ew("scalar", "t", mm_t)
                    if kind == "K":
                        ei("scalar", lambda e, c=dc, b2=bi: e.copy(ch(K32, c), PB[b2][:, 0:TA]))
                        ei("scalar", lambda e, c=dc, b2=bi: e.copy(KB[:, c * TA:(c + 1) * TA], PB[b2][:, 0:TA]), "s")
                    else:
                        ei("scalar", lambda e, c=dc, b2=bi: e.copy(ch(V32, c), PB[b2][:, 0:TA]), "s")
                    bank_free[bi] = ("s", snap("s"))
            kv_s = ("s", snap("s"))

            for cc in range(NCC):
                ei("tensor",
                   lambda e, l=gwB[:, cc * 8:(cc + 1) * 8], r=XB[:, cc * TA:(cc + 1) * TA],
                   o=PB[5][0:8, TA:2 * TA], st=(cc == 0), sp=(cc == NCC - 1):
                   e.matmul(o, l, r, start=st, stop=sp),
                   "t" if cc == NCC - 1 else None)
            lg_t = snap("t")

            ew("tensor", kv_s)
            for cc in range(NCC):
                ei("tensor",
                   lambda e, l=ainiB[:, cc * 8:(cc + 1) * 8], r=KB[:, cc * TA:(cc + 1) * TA],
                   o=PB[5][0:8, 0:TA], st=(cc == 0), sp=(cc == NCC - 1):
                   e.matmul(o, l, r, start=st, stop=sp),
                   "t" if cc == NCC - 1 else None)
            ka_t = snap("t")
            ew("scalar", "t", ka_t)
            ei("scalar", lambda e: e.copy(KAb[:, :], PB[5][0:8, 0:TA]), "s")
            kab_s = snap("s")
            ew("vector", "t", ka_t)
            ew("vector", "s", kab_s)
            ei("vector", lambda e: e.tensor_copy(lgsb[:, :], PB[5][0:8, TA:2 * TA]), "v")

            ew("tensor", "s", kab_s)
            for dc in range(NCC):
                bi = 3 + (dc % 2)
                ew("tensor", bank_free[bi])
                for cc in range(NCC):
                    ei("tensor",
                       lambda e, l=lhs_w(W0B, cc, dc), r=KB[:, cc * TA:(cc + 1) * TA],
                       o=PB[bi][:, 0:TA], st=(cc == 0):
                       e.matmul(o, l, r, start=st, stop=False))
                ei("tensor",
                   lambda e, l=biniB[:, dc * 128:(dc + 1) * 128], o=PB[bi][:, 0:TA]:
                   e.matmul(o, l, KAb[:, :], start=False, stop=True), "t")
                y_t = snap("t")
                ew("vector", "t", y_t)
                ei("vector", lambda e, c=dc, b2=bi: e.tensor_sub(ch(GRAD, c), PB[b2][:, 0:TA], ch(V32, c)))
                ei("vector", lambda e, c=dc: e.tensor_mul(ch(GRAD, c), ch(GRAD, c), ch(K32, c)))
                ei("vector", lambda e, c=dc: e.tensor_reduce(rcol[:, 0:1], ch(GRAD, c), AX, ALU.add))
                ei("vector", lambda e, c=dc: e.tensor_add(gtot[:, c:c + 1], gtot[:, c:c + 1], rcol[:, 0:1]), "v")
                bank_free[bi] = ("v", snap("v"))

            ew("vector", "t", lg_t)
            ew("tensor", "v", snap("v"))
            for hf in range(2):
                ei("tensor",
                   lambda e, h2=hf, r=lgsb[0:8, 0:128] if hf == 0 else lgsb[0:8, 128:256],
                   o=PB[6][0:128, 140 + hf * 8:148 + hf * 8]:
                   e.transpose(o, r, ident[0:8, 0:8]),
                   "t" if hf == 1 else None)
            tp_t = snap("t")
            ew("vector", "t", tp_t)
            vd(lambda e: e.tensor_copy(lgt[:, 0:16], PB[6][0:128, 140:156]))
            routing_tr_t = 0
            for hf in range(2):
                lg8 = lgt[:, hf * 8:(hf + 1) * 8]
                m1 = pk[:, 16:17]; m2 = pk[:, 17:18]; ssum = pk[:, 18:19]
                nm = rcol[:, 1:2]
                prb = rtmp[:, 0:8]; ohc = rtmp[:, 8:16]
                eq1 = rtmp[:, 16:24]; pr2 = rtmp[:, 24:32]
                oh1 = pk[:, 0:8]; oh2 = pk[:, 8:16]
                if hf == 1:
                    ew("vector", "t", routing_tr_t)
                vd(lambda e, a=lg8, m=m1: e.reduce_max(m, a, AX))
                vd(lambda e, m=m1, n=nm: e.tensor_sub(n, zcol[:, :], m))
                ew("scalar", "v", snap("v"))
                ei("scalar", lambda e, a=lg8, p=prb, n=nm: e.activation(p, a, AF.Exp, bias=n), "s")
                ew("vector", "s", snap("s"))
                vd(lambda e, p=prb, s_=ssum: e.tensor_reduce(s_, p, AX, ALU.add))
                vd(lambda e, p=prb, m=m1: e.reduce_max(m, p, AX))
                vd(lambda e, p=prb, m=m1, o=oh1: e.tensor_scalar(o, p, m, 0.0, ALU.subtract, ALU.is_ge))
                vd(lambda e, o=oh1, c2=ohc: e.tensor_tensor_scan(c2, ones512[:, 0:8], o, 0.0, ALU.mult, ALU.add))
                vd(lambda e, c2=ohc, q=eq1: e.tensor_scalar(q, c2, 1.0, 0.0, ALU.subtract, ALU.is_equal))
                vd(lambda e, o=oh1, q=eq1: e.tensor_mul(o, o, q))
                vd(lambda e, o=oh1, p=prb, p2=pr2: e.scalar_tensor_tensor(p2, o, -1e30, p, ALU.mult, ALU.add))
                vd(lambda e, p2=pr2, m=m2: e.reduce_max(m, p2, AX))
                vd(lambda e, p2=pr2, m=m2, o=oh2: e.tensor_scalar(o, p2, m, 0.0, ALU.subtract, ALU.is_ge))
                vd(lambda e, o=oh2, c2=ohc: e.tensor_tensor_scan(c2, ones512[:, 0:8], o, 0.0, ALU.mult, ALU.add))
                vd(lambda e, c2=ohc, q=eq1: e.tensor_scalar(q, c2, 1.0, 0.0, ALU.subtract, ALU.is_equal))
                vd(lambda e, o=oh2, q=eq1: e.tensor_mul(o, o, q))
                pk_v = snap("v")
                ew("tensor", "v", pk_v)
                ei("tensor", lambda e: e.transpose(PB[6][0:8, 0:128], pk[:, 0:8], ident[:, :]))
                ei("tensor", lambda e: e.transpose(PB[6][0:8, 128:256], pk[:, 8:16], ident[:, :]))
                ei("tensor", lambda e: e.transpose(PB[6][0:1, 256:384], pk[:, 16:17], ident[:, :]))
                ei("tensor", lambda e: e.transpose(PB[6][0:1, 384:512], pk[:, 17:18], ident[:, :]))
                ei("tensor", lambda e: e.transpose(PB[7][0:1, 0:128], pk[:, 18:19], ident[:, :]), "t")
                tr_t = snap("t")
                ew("vector", "t", tr_t)
                ei("vector",
                   lambda e, h2=hf: e.tensor_copy(ohst[:, h2 * 128:(h2 + 1) * 128],
                                                  PB[6][0:8, 0:128]))
                ei("vector",
                   lambda e, h2=hf: e.tensor_copy(ohst[:, TA + h2 * 128:TA + (h2 + 1) * 128],
                                                  PB[6][0:8, 128:256]))
                ei("vector",
                   lambda e, h2=hf: e.tensor_copy(w12t[0:1, h2 * 128:(h2 + 1) * 128],
                                                  PB[6][0:1, 256:384]))
                ei("vector",
                   lambda e, h2=hf: e.tensor_copy(w12t[0:1, TA + h2 * 128:TA + (h2 + 1) * 128],
                                                  PB[6][0:1, 384:512]))
                ei("vector",
                   lambda e, h2=hf: e.tensor_copy(w12t[0:1, 2 * TA + h2 * 128:2 * TA + (h2 + 1) * 128],
                                                  PB[7][0:1, 0:128]), "v")
                if hf == 0:
                    routing_tr_t = tr_t
            vd(lambda e: e.tensor_reduce(rcol[0:8, 0:1], ohst[0:8, 0:TA], AX, ALU.add))
            vd(lambda e: e.tensor_add(cnttot[:, 0:1], cnttot[:, 0:1], rcol[0:8, 0:1]))
            vd(lambda e: e.tensor_reduce(rcol[0:8, 0:1], ohst[0:8, TA:2 * TA], AX, ALU.add))
            vd(lambda e: e.tensor_add(cnttot[:, 1:2], cnttot[:, 1:2], rcol[0:8, 0:1]))
            oh_v = snap("v")
            ew("sync", "v", oh_v)
            dma("sync", ohs_d[:, 2 * i * TA:2 * (i + 1) * TA], ohst[:, :], sst)
            dma("sync", w12_d[0:1, 3 * i * TA:3 * (i + 1) * TA], w12t[:, :], sst)

            store_wm[i] = snapw(sst)
            vfree[("a1x", i)] = snap("v")
            vfree[("a1xbt", i)] = snap("t")

        a1_t, a1_v, a1_s = snap("t"), snap("v"), snap("s")
        ew("sync", "v", a1_v)
        dma("sync", grow_d[0:1, 0:1024].rearrange("o (cc p) -> (o p) cc", p=128), gtot[:, :], "st0", slow=True)
        dma("sync", grow_d[0:1, 1024:1040].rearrange("o (u k) -> (o k) u", u=2), cnttot[:, :], "st0", slow=True)
        ew("sync", snapw("st0"))
        dma("sync", myrow[:, :], grow_d[:, :], "x0")
        myrow_x = snapw("x0")
        ew("tensor", myrow_x)
        ew("tensor", "v", a1_v)
        ei("tensor", lambda e: e.matmul(PB[6][0:8, 0:512], onehotS[:, :], myrow[0:1, 0:512], start=True, stop=True))
        ei("tensor", lambda e: e.matmul(PB[7][0:8, 0:512], onehotS[:, :], myrow[0:1, 512:1024], start=True, stop=True))
        ei("tensor", lambda e: e.matmul(PB[5][0:8, 0:16], onehotS[:, :], myrow[0:1, 1024:1040], start=True, stop=True), "t")
        ew("vector", "t", snap("t"))
        ei("vector", lambda e: e.tensor_copy(contrib[:, 0:512], PB[6][0:8, 0:512]))
        ei("vector", lambda e: e.tensor_copy(contrib[:, 512:1024], PB[7][0:8, 0:512]))
        ei("vector", lambda e: e.tensor_copy(contrib[:, 1024:1040], PB[5][0:8, 0:16]), "v")
        ew("gpsimd", "v", snap("v"))
        dma("gpsimd", ccin_d[:, :], contrib[:, :], "g0")
        ew("gpsimd", snapw("g0"))
        ei("gpsimd", lambda e: e.collective_compute(
            "AllReduce", ALU.add, replica_groups=[list(range(NCORE))],
            ins=[ccin_d], outs=[ccout_d]), "cc")
        ew("gpsimd", "cc", snap("cc"))
        dma("gpsimd", gathered[:, :], ccout_d[:, :], "g1")
        gath_g = snapw("g1")
        ew("tensor", gath_g)
        ei("tensor", lambda e: e.matmul(PB[6][0:1, 0:512], maskgS[:, :], gathered[:, 0:512], start=True, stop=True))
        ei("tensor", lambda e: e.matmul(PB[7][0:1, 0:512], maskgS[:, :], gathered[:, 512:1024], start=True, stop=True))
        ei("tensor", lambda e: e.matmul(PB[5][0:1, 0:16], maskcS[:, :], gathered[:, 1024:1040], start=True, stop=True), "t")
        ew("vector", "t", snap("t"))
        ei("vector", lambda e: e.tensor_copy(prows[0:1, 0:512], PB[6][0:1, 0:512]))
        ei("vector", lambda e: e.tensor_copy(prows[0:1, 512:1024], PB[7][0:1, 0:512]))
        ei("vector", lambda e: e.tensor_copy(prows[0:1, 1024:1040], PB[5][0:1, 0:16]), "v")
        ew("sync", "v", snap("v"))
        dma("sync", prow_d[:, :], prows[:, :], "st1")
        ew("sync", snapw("st1"))
        dma("sync", gprefS[:, :], prow_d[0:1, 0:1024].rearrange("o (cc p) -> (o p) cc", p=128), "x1", slow=True)
        dma("sync", cprefS[:, :], prow_d[0:1, 1024:1040].rearrange("o (u k) -> (o k) u", u=2), "x1", slow=True)
        pref_x = snapw("x1")
        ew("vector", pref_x)
        ei("vector", lambda e: e.tensor_copy(carryg[:, :], gprefS[:, :]))
        ei("vector", lambda e: e.tensor_copy(ccarry[:, :], cprefS[:, :]), "v")
        exch_v = snap("v")

        ew("tensor", "v", exch_v)
        ei("tensor", lambda e: e.nop(), "t")
        qch_guard = {0: ("s", 0), 1: ("s", 0)}
        a2store_wm = {}
        a2bc_wm = {}
        for i in range(NTA):
            b = i % 2
            sx = f"x{i % 4}"
            sst = f"st{i % 4}"
            X = X0 if b == 0 else X1
            XB = XBF0 if b == 0 else XBF1
            if i < 2:
                ew("sync", "v", exch_v)
            else:
                ew("sync", "v", vfree.get(("a2x", i - 2), 0))
            dma("sync", X.rearrange("p (cc n) -> p cc n", n=TA), d["x"][:, i, :, :], sx)
            ew("sync", "t", a2bc_wm.get(i - 2, 0))
            dma("sync", RBs[i % 2][:, :], rows_d[0:1, i * 1024:(i + 1) * 1024], sx)
            ew("sync", "v", vfree.get(("a2x", i - 1), 0))
            dma("sync", ohst2[:, :], ohs_d[:, 2 * i * TA:2 * (i + 1) * TA], sx)
            dma("sync", w12t[:, :], w12_d[0:1, 3 * i * TA:3 * (i + 1) * TA], sx)
            x_x = snapw(sx)

            ew("vector", a2store_wm.get(i - 1))
            ew("tensor", x_x)
            ew("tensor", "v", vfree.get(("a2pb1", i - 1), 0))
            bcast_row(rbv(i, 0), PB[1][:, 0:TA])
            bcast_row(rbv(i, 2), PB[1][:, TA:2 * TA])
            ew("tensor", "v", vfree.get(("a2pb2", i - 1), 0))
            bcast_row(rbv(i, 1), PB[2][:, 0:TA])
            bcast_row(rbv(i, 3), PB[2][:, TA:2 * TA], "t")
            bc_t = snap("t")
            a2bc_wm[i] = bc_t

            ew("vector", "t", bc_t)
            ew("vector", x_x)
            ei("vector", lambda e, xx=X: e.tensor_mul(cm3(XN32), cm3(xx), bc_ap(PB[1][:, 0:TA])))
            ei("vector", lambda e: e.tensor_sub(cm3(XN32), cm3(XN32), bc_ap(PB[1][:, TA:2 * TA])), "v")
            xsub_v = snap("v")
            ew("scalar", "v", xsub_v)
            ei("scalar", lambda e, xb=XB: e.copy(xb[:, :], XN32[:, :]), "s")
            xnbf_s2 = snap("s")
            ew("sync", "s", xnbf_s2)
            dma("sync", xnbf_d[:, :, i * TA:(i + 1) * TA],
                XB.rearrange("p (cc n) -> p cc n", n=TA), sst)
            ei("vector", lambda e: e.tensor_mul(cm3(GRAD), cm3(XN32), bc_ap(PB[2][:, 0:TA])), "v")
            h32_v = snap("v")
            vfree[("a2pb1", i)] = h32_v
            ew("scalar", "v", h32_v)
            for cc in range(NCC):
                ei("scalar",
                   lambda e, c=cc: e.activation(HB[:, c * TA:(c + 1) * TA], ch(GRAD, c), AF.Identity,
                                                bias=lnbS[:, c:c + 1], scale=lngS[:, c:c + 1]),
                   "s" if cc == NCC - 1 else None)
            hb_s = snap("s")

            ew("tensor", "s", hb_s)
            bank_free = {3: ("t", 0), 4: ("t", 0)}
            for w, kind in ((WK, "K"), (WV, "V")):
                for dc in range(NCC):
                    bi = 3 + (dc % 2)
                    which, val = bank_free[bi]
                    ew("tensor", which, val)
                    for cc in range(NCC):
                        ei("tensor",
                           lambda e, l=lhs_w(w, cc, dc), r=HB[:, cc * TA:(cc + 1) * TA],
                           o=PB[bi][:, 0:TA], st=(cc == 0), sp=(cc == NCC - 1):
                           e.matmul(o, l, r, start=st, stop=sp),
                           "t" if cc == NCC - 1 else None)
                    mm_t = snap("t")
                    ew("vector", "t", mm_t)
                    if kind == "K":
                        ei("vector", lambda e, c=dc, b2=bi: e.tensor_copy(ch(K32, c), PB[b2][:, 0:TA]))
                        ei("vector", lambda e, c=dc, b2=bi: e.tensor_copy(KB[:, c * TA:(c + 1) * TA], PB[b2][:, 0:TA]), "v")
                    else:
                        ei("vector", lambda e, c=dc, b2=bi: e.tensor_copy(ch(V32, c), PB[b2][:, 0:TA]), "v")
                    bank_free[bi] = ("v", snap("v"))
            kv_v = snap("v")

            ew("tensor", "v", kv_v)
            for cc in range(NCC):
                ei("tensor",
                   lambda e, l=ainiB[:, cc * 8:(cc + 1) * 8], r=KB[:, cc * TA:(cc + 1) * TA],
                   o=PB[5][0:8, 0:TA], st=(cc == 0), sp=(cc == NCC - 1):
                   e.matmul(o, l, r, start=st, stop=sp),
                   "t" if cc == NCC - 1 else None)
            ka_t = snap("t")
            ew("vector", "t", ka_t)
            ei("vector", lambda e: e.tensor_copy(KAb[:, :], PB[5][0:8, 0:TA]), "v")
            kab_v = snap("v")

            ew("tensor", "v", kab_v)
            for dc in range(NCC):
                bi = 3 + (dc % 2)
                which, val = bank_free[bi]
                ew("tensor", which, val)
                for cc in range(NCC):
                    ei("tensor",
                       lambda e, l=lhs_w(W0B, cc, dc), r=KB[:, cc * TA:(cc + 1) * TA],
                       o=PB[bi][:, 0:TA], st=(cc == 0):
                       e.matmul(o, l, r, start=st, stop=False))
                ei("tensor",
                   lambda e, l=biniB[:, dc * 128:(dc + 1) * 128], o=PB[bi][:, 0:TA]:
                   e.matmul(o, l, KAb[:, :], start=False, stop=True), "t")
                y_t = snap("t")
                ew("vector", "t", y_t)
                ei("vector", lambda e, c=dc, b2=bi: e.tensor_copy(ch(Y032, c), PB[b2][:, 0:TA]))
                ei("vector", lambda e, c=dc, b2=bi: e.tensor_sub(ch(GRAD, c), PB[b2][:, 0:TA], ch(V32, c)))
                ei("vector", lambda e, c=dc: e.tensor_mul(ch(GRAD, c), ch(GRAD, c), ch(K32, c)), "v")
                bank_free[bi] = ("v", snap("v"))

            ew("tensor", "v", snap("v"))
            for cc in range(NCC):
                ei("vector", lambda e, c=cc: e.tensor_tensor_scan(ch(V32, c), ones512[:, 0:TA], ch(GRAD, c),
                                                                  carryg[:, c:c + 1], ALU.mult, ALU.add))
                vd(lambda e, c=cc: e.tensor_copy(carryg[:, c:c + 1], ch(V32, c)[:, TA - 1:TA]))
                ei("vector",
                   lambda e, c=cc: e.scalar_tensor_tensor(ch(GRAD, c), ch(V32, c), 1.0, ch(GRAD, c),
                                                          ALU.mult, ALU.subtract),
                   "v" if cc == NCC - 1 else None)
            scan_v = snap("v")
            for dc in range(NCC):
                bi = 3 + (dc % 2)
                which, val = bank_free[bi]
                ew("tensor", which, val)
                for cc in range(NCC):
                    ei("tensor",
                       lambda e, l=lhs_w(WG, cc, dc), r=XB[:, cc * TA:(cc + 1) * TA],
                       o=PB[bi][:, 0:TA], st=(cc == 0), sp=(cc == NCC - 1):
                       e.matmul(o, l, r, start=st, stop=sp),
                       "t" if cc == NCC - 1 else None)
                mm_t = snap("t")
                ew("scalar", "t", mm_t)
                ei("scalar", lambda e, c=dc, b2=bi: e.activation(ch(K32, c), PB[b2][:, 0:TA], AF.Sigmoid), "s")
                bank_free[bi] = ("s", snap("s"))
            gate_s = snap("s")

            ew("vector", "s", gate_s)
            for dc in range(NCC):
                bi = 3 + (dc % 2)
                which, val = bank_free[bi]
                ew("tensor", which, val)
                for cc in range(NCC):
                    ei("tensor",
                       lambda e, l=lhs_w(WQ, cc, dc), r=HB[:, cc * TA:(cc + 1) * TA],
                       o=PB[bi][:, 0:TA], st=(cc == 0), sp=(cc == NCC - 1):
                       e.matmul(o, l, r, start=st, stop=sp),
                       "t" if cc == NCC - 1 else None)
                mm_t = snap("t")
                q = QCH[:, (dc % 2) * TA:(dc % 2 + 1) * TA]
                which2, val2 = qch_guard[dc % 2]
                ew("scalar", which2, val2)
                ew("scalar", "t", mm_t)
                ei("scalar", lambda e, c=dc, b2=bi, q2=q: e.activation(q2, PB[b2][:, 0:TA], AF.Identity,
                                                                      scale=lrS[:, c:c + 1]), "s")
                q_s = snap("s")
                bank_free[bi] = ("s", q_s)
                ew("vector", "s", q_s)
                ei("vector", lambda e, q2=q: e.tensor_mul(q2, q2, PB[2][:, TA:2 * TA]))
                ei("vector", lambda e, c=dc, q2=q: e.tensor_mul(q2, q2, ch(GRAD, c)))
                ei("vector", lambda e, c=dc, q2=q: e.scalar_tensor_tensor(ch(Y032, c), q2, -1.0,
                                                                          ch(Y032, c), ALU.mult, ALU.add))
                ei("vector", lambda e, c=dc: e.tensor_mul(OG[:, c * TA:(c + 1) * TA], ch(Y032, c), ch(K32, c)), "v")
                qch_guard[dc % 2] = ("v", snap("v"))
            og_v = snap("v")
            vfree[("a2pb2", i)] = og_v

            ew("tensor", "v", og_v)
            for dc in range(NCC):
                bi = 3 + (dc % 2)
                which, val = bank_free[bi]
                ew("tensor", which, val)
                for cc in range(NCC):
                    ei("tensor",
                       lambda e, l=lhs_w(WO, cc, dc), r=OG[:, cc * TA:(cc + 1) * TA],
                       o=PB[bi][:, 0:TA], st=(cc == 0), sp=(cc == NCC - 1):
                       e.matmul(o, l, r, start=st, stop=sp),
                       "t" if cc == NCC - 1 else None)
                mm_t = snap("t")
                ew("scalar", "t", mm_t)
                ei("scalar", lambda e, c=dc, b2=bi: e.activation(ch(Y032, c), PB[b2][:, 0:TA], AF.Identity,
                                                                 scale=lsgS[:, c:c + 1]), "s")
                bank_free[bi] = ("s", snap("s"))
            wo_s = snap("s")
            ew("vector", "s", wo_s)
            ei("vector", lambda e, xx=X: e.tensor_add(Y032[:, :], Y032[:, :], xx[:, :]))
            ei("vector", lambda e: e.tensor_add(Y032[:, :], Y032[:, :], XN32[:, :]), "v")
            racc_v = snap("v")
            ew("sync", "v", racc_v)
            dma("sync", racc_d[:, :, i * TA:(i + 1) * TA],
                Y032.rearrange("p (cc n) -> p cc n", n=TA), sst)

            ew("vector", x_x)
            for k2 in range(2):
                sl = slice(k2 * TA, (k2 + 1) * TA)
                ei("vector", lambda e, s2_=sl, k3=k2: e.tensor_tensor_scan(keepS[:, s2_], ones512[0:8, 0:TA],
                                                                           ohst2[:, s2_], ccarry[:, k3:k3 + 1],
                                                                           ALU.mult, ALU.add))
                vd(lambda e, k3=k2: e.tensor_copy(
                    ccarry[:, k3:k3 + 1], keepS[:, (k3 + 1) * TA - 1:(k3 + 1) * TA]))
                ei("vector", lambda e, s2_=sl: e.tensor_scalar(keepS[:, s2_], keepS[:, s2_],
                                                               CAPP1, 0.0, ALU.subtract, ALU.is_lt))
                ei("vector", lambda e, s2_=sl: e.tensor_mul(keepS[:, s2_], keepS[:, s2_],
                                                            ohst2[:, s2_]), "v")
            keep_v = snap("v")
            ew("tensor", "v", keep_v)
            ei("tensor", lambda e: e.matmul(PB[5][0:1, 0:TA], ones512[0:8, 0:1],
                                            keepS[:, 0:TA], start=True, stop=True))
            ei("tensor", lambda e: e.matmul(PB[5][0:1, TA:2 * TA], ones512[0:8, 0:1],
                                            keepS[:, TA:2 * TA], start=True, stop=True), "t")
            sel_t = snap("t")
            ew("vector", "t", sel_t)
            ei("vector", lambda e: e.tensor_add(rdenT[:, :], w12t[0:1, 0:TA], w12t[0:1, TA:2 * TA]))
            ei("vector", lambda e: e.scalar_tensor_tensor(rdenT[:, :], w12t[0:1, 2 * TA:3 * TA], 1e-9,
                                                          rdenT[:, :], ALU.mult, ALU.add))
            ei("vector", lambda e: e.reciprocal(rdenT[:, :], rdenT[:, :]))
            ei("vector", lambda e: e.tensor_mul(kwS[0:1, 0:TA], PB[5][0:1, 0:TA], w12t[0:1, 0:TA]))
            ei("vector", lambda e: e.tensor_mul(kwS[0:1, TA:2 * TA], PB[5][0:1, TA:2 * TA], w12t[0:1, TA:2 * TA]))
            ei("vector", lambda e: e.tensor_mul(kwS[0:1, 0:TA], kwS[0:1, 0:TA], rdenT[:, :]))
            ei("vector", lambda e: e.tensor_mul(kwS[0:1, TA:2 * TA], kwS[0:1, TA:2 * TA], rdenT[:, :]), "v")
            kw_v = snap("v")
            ew("tensor", "v", kw_v)
            ei("tensor",
               lambda e: e.matmul(PB[7][0:8, 0:TA], onesrow[0:1, 0:8], kwS[0:1, 0:TA],
                                  start=True, stop=True))
            ei("tensor",
               lambda e: e.matmul(PB[7][0:8, TA:2 * TA], onesrow[0:1, 0:8], kwS[0:1, TA:2 * TA],
                                  start=True, stop=True), "t")
            kwb_t = snap("t")
            ew("vector", "t", kwb_t)
            ei("vector", lambda e: e.tensor_mul(combt[:, :], ohst2[:, 0:TA], PB[7][0:8, 0:TA]))
            ei("vector", lambda e: e.tensor_mul(keepS[:, 0:TA], ohst2[:, TA:2 * TA], PB[7][0:8, TA:2 * TA]))
            ei("vector", lambda e: e.tensor_add(combt[:, :], combt[:, :], keepS[:, 0:TA]), "v")
            comb_v = snap("v")
            ew("sync", "v", comb_v)
            dma("sync", comb_d[:, i * TA:(i + 1) * TA], combt[:, :], sst)

            a2store_wm[i] = snapw(sst)
            vfree[("a2x", i)] = snap("v")
            vfree[("a2xt", i)] = snap("t")

        a2_t, a2_v, a2_s = snap("t"), snap("v"), snap("s")
        a2_st = [snapw(f"st{j}") for j in range(4)]
        ew("tensor", "v", a2_v)
        ew("tensor", "s", a2_s)
        ei("tensor", lambda e: e.nop(), "t")
        moe_barrier_t = snap("t")
        ew("vector", "t", moe_barrier_t)
        ew("scalar", "t", moe_barrier_t)
        ew("gpsimd", "t", moe_barrier_t)
        ew("sync", "t", moe_barrier_t)
        for wm in a2_st:
            ew("sync", wm)

        slot_free = {j: ("t", moe_barrier_t) for j in range(5)}
        gu_bank_guard = {0: ("s", 0), 1: ("s", 0)}
        u_bank_guard = {0: ("v", 0), 1: ("v", 0)}
        y_bank_guard = {0: ("v", 0), 1: ("v", 0)}
        xnp_guard = 0
        combE_guard = [0]
        xnq_guard = 0
        gu_read_guard = {0: 0, 1: 0}
        out_store_st = None
        prev_pass_v = a2_v

        def emit_w_load(p, ve):
            e_i, half = ve // 2, ve % 2
            gsl, usl, dsl = (3 * ve) % 5, (3 * ve + 1) % 5, (3 * ve + 2) % 5
            for sl, src in ((gsl, d["gt"][e_i, :, :, half * HH:(half + 1) * HH]),
                            (usl, d["ut"][e_i, :, :, half * HH:(half + 1) * HH]),
                            (dsl, d["dtw"][e_i, :, half * 8:(half + 1) * 8, :])):
                which, val = slot_free[sl]
                ew("gpsimd", which, val)
                dma("gpsimd", WSL[sl].rearrange("p (cc n) -> p cc n", n=C), src, f"g{ve % 4}")
            return snapw(f"g{ve % 4}")

        ld_wm = {}
        fold_wm = {}

        def emit_fold(p, e_i, pass_x, xnq_x):
            ceb = combE[0:1, (e_i % 2) * PTOK:(e_i % 2 + 1) * PTOK]
            sce = f"x{2 + e_i % 2}"
            ew("sync", "t", combE_guard[e_i % 2])
            dma("sync", ceb, comb_d[e_i:e_i + 1, p * PTOK:(p + 1) * PTOK], sce)
            ce_x = snapw(sce)
            for tl in range(2):
                bank = PB[6] if tl == 0 else PB[7]
                ew("tensor", moefold_bank_guard[tl])
                ew("tensor", ce_x)
                ei("tensor",
                   lambda e, r=ceb[0:1, tl * TM:(tl + 1) * TM], bk=bank:
                   e.matmul(bk[:, 0:TM], onesrow[:, :], r, start=True, stop=True), "t")
                cb_t = snap("t")
                ew("vector", "t", cb_t)
                ew("vector", xnq_x)
                ew("vector", "t", xnp_guard)
                for cc in range(NCC):
                    ei("vector",
                       lambda e, c=cc, t2=tl, bk=bank: e.tensor_mul(
                           mch(XNP[t2], c), mch(XNQ[t2], c), bk[:, 0:TM]),
                       "v" if cc == NCC - 1 else None)
                moefold_bank_guard[tl] = ("v", snap("v"))
            combE_guard[e_i % 2] = snap("t")
            return snap("v")

        moefold_bank_guard = {0: ("t", 0), 1: ("t", 0)}
        combE_guard = {0: 0, 1: 0}
        for p in range(NPASS):
            ew("sync", "t", xnq_guard)
            svx = "x1"
            for tl in range(2):
                dma("sync", XNQ[tl].rearrange("p (cc n) -> p cc n", n=TM),
                    xnbf_d[:, :, p * PTOK + tl * TM:p * PTOK + (tl + 1) * TM], svx)
            xnq_x = snapw(svx)
            if p == 0:
                ld_wm[(0, 0)] = emit_w_load(0, 0)
            fold_wm[(p, 0)] = emit_fold(p, 0, None, xnq_x)
            ew("sync", out_store_st)
            dma("sync", OUT32.rearrange("p (cc n) -> p cc n", n=PTOK),
                racc_d[:, :, p * PTOK:(p + 1) * PTOK], "x0")
            pass_x = snapw("x0")

            for ve in range(16):
                e_i, half = ve // 2, ve % 2
                gsl, usl, dsl = (3 * ve) % 5, (3 * ve + 1) % 5, (3 * ve + 2) % 5
                GWs, UWs, DWs = WSL[gsl], WSL[usl], WSL[dsl]
                xnp_v = fold_wm[(p, e_i)]

                ew("tensor", ld_wm[(p, ve)])
                ew("tensor", xnq_x)
                if half == 0:
                    ew("tensor", "v", xnp_v)
                gu_done = {}
                for tl in range(2):
                    ew("tensor", "t", gu_read_guard[tl])
                    for hc in range(NCC):
                        pbi = hc % 2
                        which, val = gu_bank_guard[pbi]
                        ew("tensor", which, val)
                        for cc in range(NCC):
                            ei("tensor",
                               lambda e, l=GWs[:, cc * C + hc * 128:cc * C + hc * 128 + 128],
                               r=mch(XNQ[tl], cc), o=PB[pbi][:, 0:TM],
                               st=(cc == 0), sp=(cc == NCC - 1):
                               e.matmul(o, l, r, start=st, stop=sp))
                        which, val = u_bank_guard[pbi]
                        ew("tensor", which, val)
                        for cc in range(NCC):
                            ei("tensor",
                               lambda e, l=UWs[:, cc * C + hc * 128:cc * C + hc * 128 + 128],
                               r=mch(XNP[tl], cc), o=PB[2 + pbi][:, 0:TM],
                               st=(cc == 0), sp=(cc == NCC - 1):
                               e.matmul(o, l, r, start=st, stop=sp),
                               "t" if cc == NCC - 1 else None)
                        gu_t = snap("t")
                        ew("scalar", "t", gu_t)
                        ei("scalar",
                           lambda e, b2=pbi, t2=tl, h2=hc:
                           e.activation(GU[t2][:, h2 * TM:(h2 + 1) * TM],
                                        PB[b2][:, 0:TM], AF.Silu), "s")
                        sg_s = snap("s")
                        gu_bank_guard[pbi] = ("s", sg_s)
                        ew("vector", "s", sg_s)
                        ei("vector",
                           lambda e, b2=pbi, t2=tl, h2=hc:
                           e.tensor_mul(GU[t2][:, h2 * TM:(h2 + 1) * TM],
                                        GU[t2][:, h2 * TM:(h2 + 1) * TM], PB[2 + b2][:, 0:TM]), "v")
                        u_bank_guard[pbi] = ("v", snap("v"))
                    gu_done[tl] = snap("v")
                xnq_guard = snap("t")
                if half == 1:
                    xnp_guard = snap("t")

                slot_free[gsl] = ("t", snap("t"))
                slot_free[usl] = ("t", snap("t"))
                nxt = (p, ve + 1) if ve + 1 < 16 else ((p + 1, 0) if p + 1 < NPASS else None)
                if half == 1 and e_i + 1 < NE:
                    fold_wm[(p, e_i + 1)] = emit_fold(p, e_i + 1, pass_x, xnq_x)

                for tl in range(2):
                    ew("tensor", "v", gu_done[tl])
                    for dc in range(NCC):
                        pbi = dc % 2
                        which, val = y_bank_guard[pbi]
                        ew("tensor", which, val)
                        for hc in range(NCC):
                            ei("tensor",
                               lambda e, l=DWs[:, hc * C + dc * 128:hc * C + dc * 128 + 128],
                               r=GU[tl][:, hc * TM:(hc + 1) * TM], o=PB[4 + pbi][:, 0:TM],
                               st=(hc == 0), sp=(hc == NCC - 1):
                               e.matmul(o, l, r, start=st, stop=sp),
                               "t" if hc == NCC - 1 else None)
                        y_t = snap("t")
                        ew("vector", "t", y_t)
                        ew("vector", pass_x)
                        ei("vector",
                           lambda e, c=dc, t2=tl, b2=pbi:
                           e.tensor_add(OUT32[:, c * PTOK + t2 * TM:c * PTOK + t2 * TM + TM],
                                        OUT32[:, c * PTOK + t2 * TM:c * PTOK + t2 * TM + TM],
                                        PB[4 + b2][:, 0:TM]), "v")
                        y_bank_guard[pbi] = ("v", snap("v"))
                    gu_read_guard[tl] = snap("t")
                slot_free[dsl] = ("t", snap("t"))
                if nxt is not None:
                    ld_wm[nxt] = emit_w_load(*nxt)

            prev_pass_v = snap("v")
            ew("sync", "v", prev_pass_v)
            dma("sync", out_d[:, :, p * PTOK:(p + 1) * PTOK],
                OUT32.rearrange("p (cc n) -> p cc n", n=PTOK), f"st{p % 4}")
            out_store_st = snapw(f"st{p % 4}")

        for j in range(4):
            ew("sync", snapw(f"st{j}"))
            ew("sync", snapw(f"x{j}"))
            ew("sync", snapw(f"g{j}"))

        block = ES.enter_context(nc.Block())

        @block.tensor
        def _(e):
            for f in prog["tensor"]:
                f(e)

        @block.vector
        def _(e):
            for f in prog["vector"]:
                f(e)

        @block.scalar
        def _(e):
            for f in prog["scalar"]:
                f(e)

        @block.sync
        def _(e):
            for f in prog["sync"]:
                f(e)

        @block.gpsimd
        def _(e):
            for f in prog["gpsimd"]:
                f(e)

    return nc


def _prep_shared(inputs):
    f32 = np.float32
    sh = {}

    def cm_pack(w):
        return np.ascontiguousarray(w.reshape(NCC, 128, -1).transpose(1, 0, 2))

    sh["wq"] = cm_pack(np.asarray(inputs["L0_attn_Wq"], f32).T)
    sh["wk"] = cm_pack(np.asarray(inputs["L0_attn_Wk"], f32).T)
    sh["wv"] = cm_pack(np.asarray(inputs["L0_attn_Wv"], f32).T)
    sh["wg"] = cm_pack(np.asarray(inputs["L0_attn_Wg"], f32).T)
    sh["wo"] = cm_pack(np.asarray(inputs["L0_attn_Wo"], f32).T)
    sh["w0"] = cm_pack(np.asarray(inputs["L0_attn_W0"], f32))
    sh["aini"] = cm_pack(np.asarray(inputs["L0_attn_A_init"], f32))
    sh["bini"] = np.ascontiguousarray(np.asarray(inputs["L0_attn_B_init"], f32))
    sh["gw"] = cm_pack(np.asarray(inputs["L0_ffn_gate_w"], f32).T)
    for nm, key in (("lng", "L0_attn_ln_g"), ("lnb", "L0_attn_ln_b"),
                    ("lsg", "L0_attn_ls_g"), ("loglr", "L0_attn_log_lr")):
        v = np.asarray(inputs[key], f32)
        sh[nm] = np.ascontiguousarray(v.reshape(NCC, 128).T)
    gt = np.asarray(inputs["L0_ffn_gate_proj_w"], f32)
    ut = np.asarray(inputs["L0_ffn_up_proj_w"], f32)
    dw = np.asarray(inputs["L0_ffn_down_proj_w"], f32)
    sh["gt"] = np.ascontiguousarray(
        gt.transpose(0, 2, 1).reshape(NE, NCC, 128, 2048).transpose(0, 2, 1, 3))
    sh["ut"] = np.ascontiguousarray(
        ut.transpose(0, 2, 1).reshape(NE, NCC, 128, 2048).transpose(0, 2, 1, 3))
    sh["dtw"] = np.ascontiguousarray(
        dw.transpose(0, 2, 1).reshape(NE, 16, 128, 1024).transpose(0, 2, 1, 3))
    sh["ident"] = np.eye(128, dtype=f32)
    sh["ones512"] = np.ones((128, 256), f32)
    sh["onesrow"] = np.ones((1, 128), f32)
    sh["onescolb"] = np.ones((128, 1), ml_dtypes.bfloat16)

    return sh


def kernel(**inputs):
    x = np.asarray(inputs["L0_attn_x"], np.float32)
    B, T, _ = x.shape
    xf = x.reshape(B * T, C)

    sh = _prep_shared(inputs)
    nc = build_nc()

    in_maps = []
    for c in range(NCORE):
        m = dict(sh)
        shard = xf[c * TOK:(c + 1) * TOK]
        xcm = np.ascontiguousarray(shard.T)
        m["x"] = np.ascontiguousarray(
            xcm.reshape(NCC, 128, NTA, TA).transpose(1, 2, 0, 3))
        oh = np.zeros((1, 8), np.float32)
        oh[0, c] = 1.0
        m["onehot"] = oh
        mg = np.zeros((8, 1), np.float32)
        for cp in range(NCORE):
            if cp < c and cp // 4 == c // 4:
                mg[cp, 0] = 1.0
        m["maskg"] = mg
        mc = np.zeros((8, 1), np.float32)
        mc[0:c, 0] = 1.0
        m["maskc"] = mc
        pos0 = (c * TOK) % T
        m["pos"] = (np.arange(pos0 + 1, pos0 + TOK + 1, dtype=np.float64)
                    .astype(np.float32).reshape(1, TOK))
        in_maps.append(m)

    res = run_bass_kernel_spmd(nc, in_maps, core_ids=list(range(NCORE)))

    out = np.empty((B * T, C), np.float32)
    for c in range(NCORE):
        oc = np.asarray(res.results[c]["out"]).reshape(128, NCC, TOK)
        ocm = oc.transpose(1, 0, 2).reshape(C, TOK)
        out[c * TOK:(c + 1) * TOK] = ocm.T
    return out.reshape(B, T, C)
